# version 07
# baseline (speedup 1.0000x reference)
import numpy as np
from contextlib import ExitStack

import concourse.bass as bass
import concourse.mybir as mybir
from concourse.bass_utils import run_bass_kernel_spmd

F32 = mybir.dt.float32
BF16 = mybir.dt.bfloat16
U32 = mybir.dt.uint32
I32 = mybir.dt.int32
AF = mybir.ActivationFunctionType
ALU = mybir.AluOpType
ENG = mybir.EngineType

B, L, DM = 4, 8192, 1024
DS, DCONV, DTR = 16, 4, 64
CH, NCH, NDM = 1024, 8, 8
TT = 512
NT = L // TT
TOK = L // 2
NT3 = TOK // TT
T1 = TT + 2
GS, NG = 4, 4
FF, NFF = 4096, 32
EPS = 1e-5


class Sem:
    def __init__(self, stack, nc, name, engine=None):
        self.h = stack.enter_context(nc.semaphore(name))
        self.n = 0
        self.engine = engine
        self.dk = 1 if engine else 16

    def inc(self, instr, k=None):
        k = self.dk if k is None else k
        instr.then_inc(self.h, k)
        self.n += k
        return (self, self.n)

    def now(self):
        return (self, self.n)


def _flat(evs):
    for e in evs:
        if e is None:
            continue
        if isinstance(e, list):
            yield from _flat(e)
        else:
            yield e


def build_nc():
    nc = bass.Bass("TRN2", target_bir_lowering=False)

    def din(name, shape, dt=F32):
        return nc.dram_tensor(name, shape, dt, kind="ExternalInput").ap()

    xT = din("xT", [DM, L])
    winx = din("winx", [DM, CH])
    winz = din("winz", [DM, CH])
    wxp = din("wxp", [CH, 96])
    wdt_in = din("wdt", [DTR, CH])
    convdiag = din("convdiag", [32, 128, 128])
    convb = din("convb", [128, NCH])
    bdt = din("bdt", [128, NCH])
    alog = din("alog", [128, NCH, DS])
    dp_in = din("dp", [128, NCH])
    wout_lo = din("wout_lo", [CH, DM])
    wout_hi = din("wout_hi", [CH, DM])
    gam = din("gam", [128, NDM])
    bet = din("bet", [128, NDM])
    q1 = din("q1", [DM, FF], I32)
    s1 = din("s1", [8, FF])
    b1 = din("b1", [128, NFF])
    q2 = din("q2", [FF, DM], I32)
    s2 = din("s2", [32, DM])
    b2 = din("b2", [128, NDM])
    selmat = din("selmat", [32, 32 * 128])
    identm = din("identm", [128, 128])
    onesh = din("onesh", [128, 128])
    da01_in = din("da01", [128, 2])
    nonce_in = din("nonce_in", [1, 64], U32)

    out_d = nc.dram_tensor("out", [NDM, 128, TOK], F32, kind="ExternalOutput").ap()
    x2_d = nc.dram_tensor("x2_d", [NDM, 128, TOK], F32, kind="Internal").ap()
    x2n_d = nc.dram_tensor("x2n_d", [NDM, 128, TOK], BF16, kind="Internal").ap()

    xdbl_shr = nc.dram_tensor("xdbl_shr", [2, 96, L], F32, kind="Internal",
                              addr_space="Shared").ap()
    y_shr = nc.dram_tensor("y_shr", [2, NCH, 128, L], BF16, kind="Internal",
                           addr_space="Shared").ap()
    flg_xd = nc.dram_tensor("flg_xd", [2, NT, 2], U32, kind="Internal",
                            addr_space="Shared").ap()
    flg_y = nc.dram_tensor("flg_y", [2, 2], U32, kind="Internal",
                           addr_space="Shared").ap()

    with ExitStack() as st0:
        def mk(stk):
            def A(name, shape, dt=F32):
                return stk.enter_context(nc.sbuf_tensor(name, shape, dt))

            def P(name, shape):
                return stk.enter_context(nc.psum_tensor(name, shape, F32))
            return A, P

        A0, _ = mk(st0)
        ident_bf = A0("ident_bf", [128, 128], BF16)
        onesh_bf = A0("onesh_bf", [128, 128], BF16)
        sel_bf = A0("sel_bf", [32, 32 * 128], BF16)
        nonce = A0("nonce", [1, 64], F32)
        flagw = A0("flagw", [1, 2], U32)
        gam_sb = A0("gam_sb", [128, NDM], F32)
        bet_sb = A0("bet_sb", [128, NDM], F32)
        b2h_sb = A0("b2h_sb", [128, NDM], F32)
        b1_sb = A0("b1_sb", [128, NFF], F32)
        nb1_sb = A0("nb1_sb", [128, NFF], F32)
        const_sb = A0("const_sb", [128, 2], F32)

        nc.const_aps.register(F32, EPS, const_sb[:, 0:1])
        nc.const_aps.register(F32, -128.0, const_sb[:, 1:2])

        block = st0.enter_context(nc.Block())

        s_dma = Sem(st0, nc, "s_dma")
        g_dma = Sem(st0, nc, "g_dma")
        s_poll = Sem(st0, nc, "s_poll")
        c_s = Sem(st0, nc, "c_s", "scalar")
        c_v = Sem(st0, nc, "c_v", "vector")
        c_t = Sem(st0, nc, "c_t", "tensor")
        c_g = Sem(st0, nc, "c_g", "gpsimd")

        def W(eng, engname, *evs):
            seen = {}
            for sem, val in _flat(evs):
                if val <= 0 or sem.engine == engname:
                    continue
                if seen.get(id(sem), -1) < val:
                    seen[id(sem)] = val
                    eng.wait_ge(sem.h, val)

        ev = {}
        rel = {}
        regs = {}

        @block.gpsimd
        def _(gpsimd):
            pid = nc.partition_id([ENG.Pool])
            r = gpsimd.alloc_register("slot")
            gpsimd.reg_alu(r, gpsimd.to_reg(pid), 1, ALU.bitwise_and)
            regs["slot"] = r
            pt = gpsimd.alloc_register("ptgt")
            gpsimd.reg_mov(pt, 0)
            regs["ptgt"] = pt

        def if_slot(gpsimd, k):
            return gpsimd.If_eq(regs["slot"], k)

        def bdma(gpsimd, sem, mk_pair, use_peer=False):
            for k in (0, 1):
                with if_slot(gpsimd, (1 - k) if use_peer else k):
                    o, i = mk_pair(k)
                    gpsimd.dma_start(out=o, in_=i).then_inc(sem.h, 16)
                gpsimd.end_ifs()
            sem.n += 16
            return (sem, sem.n)

        _pollid = [0]

        def poll(gpsimd, flag_ap_fn, word_fn):
            pt = regs["ptgt"]
            u = str(_pollid[0])
            _pollid[0] += 1
            with (
                gpsimd.register("nd" + u) as nd,
                gpsimd.register("f0" + u) as f0,
                gpsimd.register("f1" + u) as f1,
                gpsimd.register("e0" + u) as e0,
                gpsimd.register("e1" + u) as e1,
                gpsimd.register("t0" + u) as t0,
                gpsimd.register("t1" + u) as t1r,
            ):
                for k in (0, 1):
                    with if_slot(gpsimd, 1 - k):
                        w = word_fn(k)
                        gpsimd.reg_load(e0, nonce[0:1, w:w + 1].bitcast(U32))
                        gpsimd.reg_load(e1, nonce[0:1, w + 1:w + 2].bitcast(U32))
                    gpsimd.end_ifs()
                gpsimd.reg_mov(nd, 1)
                with gpsimd.While(nd):
                    gpsimd.reg_alu(pt, pt, 16, ALU.add)
                    for k in (0, 1):
                        with if_slot(gpsimd, 1 - k):
                            gpsimd.dma_start(out=flagw[0:1, :],
                                             in_=flag_ap_fn(k)
                                             ).then_inc(s_poll.h, 16)
                        gpsimd.end_ifs()
                    gpsimd.wait_ge(s_poll.h, pt)
                    gpsimd.reg_load(f0, flagw[0:1, 0:1])
                    gpsimd.reg_load(f1, flagw[0:1, 1:2])
                    gpsimd.reg_alu(t0, f0, e0, ALU.not_equal)
                    gpsimd.reg_alu(t1r, f1, e1, ALU.not_equal)
                    gpsimd.reg_alu(nd, t0, t1r, ALU.bitwise_or)

        BARRIER_TILES = False

        def barrier_all(tag):
            ends = {"scalar": c_s.now(), "vector": c_v.now(),
                    "tensor": c_t.now(), "gpsimd": (g_dma, g_dma.n),
                    "sync": s_dma.now()}
            marks = {}
            for name in ("scalar", "vector", "tensor", "gpsimd", "sync"):
                sem = {"scalar": c_s, "vector": c_v, "tensor": c_t,
                       "gpsimd": c_g, "sync": None}[name]

                @getattr(block, name)
                def _(eng, name=name, sem=sem):
                    for oname, tok in ends.items():
                        if oname != name:
                            eng.wait_ge(tok[0].h, tok[1])
                    if sem is not None:
                        marks[name] = sem.inc(eng.nop())
            for name in ("scalar", "vector", "tensor", "gpsimd", "sync"):
                @getattr(block, name)
                def _(eng, name=name):
                    for oname, tok in marks.items():
                        if oname != name:
                            eng.wait_ge(tok[0].h, tok[1])

        with ExitStack() as stm:
            Am, Pm = mk(stm)
            winx_bf = Am("winx_bf", [128, NDM, CH], BF16)
            winz_bf = Am("winz_bf", [128, NDM, CH], BF16)
            wxp_bf = Am("wxp_bf", [128, NCH, 96], BF16)
            wdt_bf = Am("wdt_bf", [DTR, CH], BF16)
            cdiag_bf = Am("cdiag_bf", [128, 32, 128], BF16)
            convb_sb = Am("convb_sb", [128, NCH], F32)
            bdt_sb = Am("bdt_sb", [128, NCH], F32)
            a_sb = Am("a_sb", [128, NCH, DS], F32)
            dp_sb = Am("dp_sb", [128, NCH], F32)
            da01_f = Am("da01_f", [128, 2], F32)

            with ExitStack() as stw:
                Aw, _ = mk(stw)
                wstage = Aw("wstage", [128, 2, 8192], F32)

                @block.sync
                def _(sync):
                    for dst, src in [
                        (convb_sb[:], convb[:, :]),
                        (bdt_sb[:], bdt[:, :]),
                        (a_sb[:], alog[:, :, :]),
                        (dp_sb[:], dp_in[:, :]),
                        (da01_f[:], da01_in[:, :]),
                        (nonce[:].bitcast(U32), nonce_in[:, :]),
                        (gam_sb[:], gam[:, :]),
                        (bet_sb[:], bet[:, :]),
                        (b2h_sb[:], b2[:, :]),
                        (b1_sb[:], b1[:, :]),
                        (wstage[0:32, 1, 0:32 * 128], selmat[:, :]),
                        (wstage[:, 0, 0:128], identm[:, :]),
                        (wstage[:, 0, 128:256], onesh[:, :]),
                        (wstage[0:64, 0, 256:256 + CH], wdt_in[:, :]),
                        (wstage[:, 0, 2048:2048 + 32 * 128].rearrange(
                            "p (a c) -> p a c", a=32),
                         convdiag[:, :, :].transpose([1, 0, 2])),
                        (wstage[:, 0, 6144:6144 + NCH * 96].rearrange(
                            "p (c n) -> p c n", c=NCH),
                         wxp[:, :].rearrange("(c p) n -> p c n", p=128)),
                    ]:
                        sync.dma_start(out=dst, in_=src).then_inc(s_dma.h, 16)
                        s_dma.n += 16
                    ev["ld0"] = s_dma.now()

                @block.scalar
                def _(scalar):
                    W(scalar, "scalar", ev["ld0"])
                    scalar.activation(out=ident_bf[:], in_=wstage[:, 0, 0:128],
                                      func=AF.Copy)
                    scalar.activation(out=onesh_bf[:], in_=wstage[:, 0, 128:256],
                                      func=AF.Copy)
                    scalar.activation(out=sel_bf[:],
                                      in_=wstage[0:32, 1, 0:32 * 128],
                                      func=AF.Copy)
                    scalar.activation(out=wdt_bf[:],
                                      in_=wstage[0:64, 0, 256:256 + CH],
                                      func=AF.Copy)
                    scalar.activation(out=cdiag_bf[:].rearrange("p a c -> p (a c)"),
                                      in_=wstage[:, 0, 2048:2048 + 32 * 128],
                                      func=AF.Copy)
                    scalar.activation(out=wxp_bf[:].rearrange("p c n -> p (c n)"),
                                      in_=wstage[:, 0, 6144:6144 + NCH * 96],
                                      func=AF.Copy)
                    scalar.activation(out=a_sb[:], in_=a_sb[:], func=AF.Exp)
                    scalar.activation(out=nb1_sb[:], in_=b1_sb[:], func=AF.Copy,
                                      scale=-1.0)
                    ev["sc0"] = c_s.inc(scalar.activation(
                        out=b2h_sb[:], in_=b2h_sb[:], func=AF.Copy, scale=0.5))
                    rel[("wst", 0)] = ev["sc0"]
                    rel[("wst", 1)] = ev["sc0"]

                for kind, src, dst in (("x", winx, winx_bf), ("z", winz, winz_bf)):
                    for k in range(NDM):
                        @block.sync
                        def _(sync, kind=kind, src=src, k=k):
                            W(sync, "sync", rel.get(("wst", k % 2)))
                            ev[("wl", kind, k)] = s_dma.inc(sync.dma_start(
                                out=wstage[:, k % 2, 0:CH],
                                in_=src[k * 128:(k + 1) * 128, :]))

                        @block.scalar
                        def _(scalar, kind=kind, k=k, dst=dst):
                            W(scalar, "scalar", ev[("wl", kind, k)])
                            rel[("wst", k % 2)] = c_s.inc(scalar.activation(
                                out=dst[:, k, :], in_=wstage[:, k % 2, 0:CH],
                                func=AF.Copy))
                ev["wm"] = c_s.now()
                rel[("x_st", 0)] = ev["wm"]
                rel[("x_st", 1)] = ev["wm"]

                @block.vector
                def _(vector):
                    W(vector, "vector", ev["sc0"])
                    vector.tensor_scalar_mul(a_sb[:], a_sb[:], -1.0)
                    vector.memset(const_sb[:, 0:1], EPS)
                    vector.memset(const_sb[:, 1:2], -128.0)
                    ev["v0"] = c_v.inc(vector.nop())

            x_st = Am("x_st", [128, 2, 2, TT], F32)
            xbf = Am("xbf", [128, NDM, TT], BF16)
            xin_bf = Am("xin_bf", [128, 2, NCH, TT + 3], BF16)
            zs_bf = Am("zs_bf", [128, 2, NCH, TT], BF16)
            xc_bf = Am("xc_bf", [128, 2, NCH, TT], BF16)
            dt_f = Am("dt_f", [128, 2, TT], F32)
            dtx_bf = Am("dtx_bf", [128, 2, TT], BF16)
            xd_mine = Am("xd_mine", [96, 2, TT], F32)
            xd_peer = Am("xd_peer", [96, TT], F32)
            xd_sum = Am("xd_sum", [96, TT], F32)
            dtraw_bf = Am("dtraw_bf", [DTR, TT], BF16)
            bc_bf = Am("bc_bf", [32, TT], BF16)
            brep_bf = Am("brep_bf", [128, DS, TT], BF16)
            crep_bf = Am("crep_bf", [128, DS, TT], BF16)
            da_f = Am("da_f", [128, 2, GS * T1], F32)
            dbx_bf = Am("dbx_bf", [128, GS * T1], BF16)
            h_bf = Am("h_bf", [128, GS * T1], BF16)
            yc_bf = Am("yc_bf", [128, GS, TT], BF16)
            carr = Am("carr", [128, NCH, DS], F32)
            t_a = Am("t_a", [128, TT], F32)
            t_ln = Am("t_ln", [128, TT], F32)
            t_re = Am("t_re", [128, TT], F32)
            epi_f = Am("epi_f", [128, TT], F32)
            yfin_bf = Am("yfin_bf", [128, 2, TT], BF16)

            ps_xz = Pm("ps_xz", [128, 2, TT])
            ps_conv = ps_xz
            ps_sm = Pm("ps_sm", [128, 2, TT])
            ps_bc = Pm("ps_bc", [128, 2, TT])
            ps_y = Pm("ps_y", [128, 2, TT])

            @block.vector
            def _(vector):
                W(vector, "vector", ev["v0"], ev["wm"])
                vector.memset(carr[:], 0.0)
                vector.memset(dbx_bf[:], 0.0)
                vector.memset(xin_bf[:, :, :, 0:3], 0.0)
                for p2 in range(2):
                    for s in range(GS):
                        vector.tensor_copy(da_f[:, p2, s * T1:s * T1 + 2],
                                           da01_f[:])
                ev["v1"] = c_v.inc(vector.nop())

            for tau in range(NT):
                par = tau % 2
                t0, t1 = tau * TT, (tau + 1) * TT

                for hx in range(4):
                    @block.sync
                    def _(sync, tau=tau, par=par, t0=t0, hx=hx):
                        W(sync, "sync", rel.get(("x_st", par)))
                        ev[("x", tau, hx)] = s_dma.inc(sync.dma_start(
                            out=x_st[:, par, :, :],
                            in_=xT[hx * 256:(hx + 1) * 256, t0:t0 + TT]
                            .rearrange("(c p) t -> p c t", p=128)))

                    @block.scalar
                    def _(scalar, tau=tau, par=par, hx=hx):
                        W(scalar, "scalar", ev[("x", tau, hx)],
                          rel.get("xbf") if hx == 0 else None,
                          [ev["wm"], ev["v1"]] if tau == 0 else None)
                        tok = c_s.inc(scalar.activation(
                            out=xbf[:, hx * 2:(hx + 1) * 2, :],
                            in_=x_st[:, par, :, :], func=AF.Copy))
                        ev[("xbf", tau, hx)] = tok
                        rel[("x_st", par)] = tok

                for i in range(16):
                    @block.tensor
                    def _(tensor, tau=tau, i=i):
                        W(tensor, "tensor", ev[("xbf", tau, 3)],
                          rel.get(("ps_xz", i % 2)))
                        wsrc = winx_bf if i < NCH else winz_bf
                        io = i % NCH
                        for k in range(NDM):
                            inst = nc.tensor.matmul(
                                ps_xz[:, i % 2, :],
                                wsrc[:, k, io * 128:(io + 1) * 128],
                                xbf[:, k, :], start=(k == 0), stop=(k == NDM - 1))
                        tok = c_t.inc(inst)
                        ev[("xz", tau, i)] = tok
                        if i == 15:
                            rel["xbf"] = tok

                    @block.scalar
                    def _(scalar, tau=tau, par=par, i=i):
                        W(scalar, "scalar", ev[("xz", tau, i)],
                          rel.get(("xin", par)) if i < NCH else
                          rel.get(("zs", par)))
                        if i < NCH:
                            inst = scalar.activation(
                                out=xin_bf[:, par, i, 3:TT + 3],
                                in_=ps_xz[:, i % 2, :], func=AF.Copy)
                        else:
                            inst = scalar.activation(
                                out=zs_bf[:, par, i - NCH, :],
                                in_=ps_xz[:, i % 2, :], func=AF.Silu)
                        tok = c_s.inc(inst)
                        ev[("xzev", tau, i)] = tok
                        rel[("ps_xz", i % 2)] = tok

                @block.scalar
                def _(scalar, tau=tau, par=par):
                    if tau >= 1:
                        W(scalar, "scalar", ev[("conv", tau - 1, NCH - 1)])
                        for c in range(NCH):
                            scalar.activation(out=xin_bf[:, par, c, 0:3],
                                              in_=xin_bf[:, 1 - par, c, TT:TT + 3],
                                              func=AF.Copy)
                    ev[("ctx", tau)] = c_s.inc(scalar.nop())

                for c in range(NCH):
                    @block.tensor
                    def _(tensor, tau=tau, par=par, c=c):
                        W(tensor, "tensor", ev[("xzev", tau, c)],
                          ev[("ctx", tau)], rel.get(("ps_xz", c % 2)))
                        for k in range(DCONV):
                            inst = nc.tensor.matmul(
                                ps_conv[:, c % 2, :], cdiag_bf[:, c * 4 + k, :],
                                xin_bf[:, par, c, k:k + TT],
                                start=(k == 0), stop=(k == DCONV - 1))
                        tok = c_t.inc(inst)
                        ev[("conv", tau, c)] = tok
                        if c == NCH - 1:
                            rel[("xin", par)] = tok

                    @block.scalar
                    def _(scalar, tau=tau, par=par, c=c):
                        W(scalar, "scalar", ev[("conv", tau, c)],
                          rel.get(("xc", par)))
                        inst = scalar.activation(out=xc_bf[:, par, c, :],
                                                 in_=ps_conv[:, c % 2, :],
                                                 func=AF.Silu,
                                                 bias=convb_sb[:, c:c + 1])
                        tok = c_s.inc(inst)
                        ev[("xc", tau, c)] = tok
                        rel[("ps_xz", c % 2)] = tok

                @block.tensor
                def _(tensor, tau=tau, par=par):
                    W(tensor, "tensor", rel.get("ps_xp"),
                      rel.get(("ps_dt", 0)), rel.get(("ps_dt", 1)),
                      [ev[("xc", tau, c)] for c in range(NCH)])
                    for c in range(NCH):
                        inst = nc.tensor.matmul(
                            ps_sm[0:96, 0, :], wxp_bf[:, c, :],
                            xc_bf[:, par, c, :],
                            start=(c == 0), stop=(c == NCH - 1))
                    ev[("xp", tau)] = c_t.inc(inst)

                @block.scalar
                def _(scalar, tau=tau, par=par):
                    W(scalar, "scalar", ev[("xp", tau)],
                      rel.get(("xd_mine", par)))
                    tok = c_s.inc(scalar.activation(out=xd_mine[:, par, :],
                                                    in_=ps_sm[0:96, 0, :],
                                                    func=AF.Copy))
                    ev[("xdev", tau)] = tok
                    rel["ps_xp"] = tok

                @block.gpsimd
                def _(gpsimd, tau=tau, par=par, t0=t0, t1=t1):
                    W(gpsimd, "gpsimd", ev[("xdev", tau)])
                    tok = bdma(gpsimd, g_dma,
                               lambda k, par=par, t0=t0, t1=t1:
                               (xdbl_shr[k, :, t0:t1], xd_mine[:, par, :]))
                    gpsimd.wait_ge(g_dma.h, g_dma.n)
                    rel[("xd_mine", par)] = tok
                    bdma(gpsimd, g_dma,
                         lambda k, tau=tau:
                         (flg_xd[k, tau, :],
                          nonce_in[0:1, 2 * tau:2 * tau + 2]))
                    poll(gpsimd, lambda k, tau=tau: flg_xd[k, tau, :],
                         lambda k, tau=tau: 2 * tau)
                    W(gpsimd, "gpsimd", rel.get("xd_peer"))
                    ev[("xdr", tau)] = bdma(
                        gpsimd, g_dma,
                        lambda k, t0=t0, t1=t1:
                        (xd_peer[:, :], xdbl_shr[k, :, t0:t1]),
                        use_peer=True)

                @block.vector
                def _(vector, tau=tau, par=par):
                    W(vector, "vector", ev[("xdr", tau)], rel.get("xd_sum"))
                    tok = c_v.inc(vector.tensor_add(
                        xd_sum[:], xd_mine[:, par, :], xd_peer[:, :]))
                    ev[("xdsum", tau)] = tok
                    rel["xd_peer"] = tok

                @block.scalar
                def _(scalar, tau=tau):
                    W(scalar, "scalar", ev[("xdsum", tau)], rel.get("dtraw"))
                    scalar.activation(out=dtraw_bf[:], in_=xd_sum[0:64, :],
                                      func=AF.Copy)
                    tok = c_s.inc(scalar.activation(out=bc_bf[:],
                                                    in_=xd_sum[64:96, :],
                                                    func=AF.Copy))
                    ev[("bcc", tau)] = tok
                    rel["xd_sum"] = tok

                for which, dst in (("b", brep_bf), ("c", crep_bf)):
                    roff = 0 if which == "b" else 16
                    for sp_ in range(DS // 2):
                        @block.tensor
                        def _(tensor, tau=tau, sp_=sp_, which=which, roff=roff):
                            W(tensor, "tensor", ev[("bcc", tau)],
                              rel.get("ps_bc"),
                              rel.get("brepcrep") if sp_ == 0 and which == "b"
                              else None)
                            for jj in (0, 1):
                                row = roff + 2 * sp_ + jj
                                inst = nc.tensor.matmul(
                                    ps_bc[:, jj, :],
                                    sel_bf[:, row * 128:(row + 1) * 128],
                                    bc_bf[:, :], start=True, stop=True)
                            ev[("bcm", tau, sp_, which)] = c_t.inc(inst)

                        @block.scalar
                        def _(scalar, tau=tau, sp_=sp_, which=which, dst=dst):
                            W(scalar, "scalar", ev[("bcm", tau, sp_, which)])
                            tok = c_s.inc(scalar.activation(
                                out=dst[:, 2 * sp_:2 * sp_ + 2, :],
                                in_=ps_bc[:, :, :], func=AF.Copy))
                            for jj in (0, 1):
                                ev[("bcev", tau, 2 * sp_ + jj, which)] = tok
                            rel["ps_bc"] = tok

                def dt_chain(tau, par, c):
                    cp = c % 2

                    @block.tensor
                    def _(tensor, tau=tau, c=c, cp=cp):
                        W(tensor, "tensor", ev[("bcc", tau)],
                          rel.get(("ps_dt", cp)))
                        ev[("dtm", tau, c)] = c_t.inc(nc.tensor.matmul(
                            ps_sm[:, cp, :],
                            wdt_bf[:, c * 128:(c + 1) * 128],
                            dtraw_bf[:, :], start=True, stop=True))

                    @block.scalar
                    def _(scalar, tau=tau, c=c, cp=cp):
                        W(scalar, "scalar", ev[("dtm", tau, c)],
                          rel.get("t_sp"), rel.get(("dt", cp)))
                        scalar.activation(out=t_a[:], in_=ps_sm[:, cp, :],
                                          func=AF.Abs, bias=bdt_sb[:, c:c + 1])
                        scalar.activation(out=t_a[:], in_=t_a[:], func=AF.Exp,
                                          scale=-1.0)
                        scalar.activation(out=t_ln[:, :], in_=t_a[:],
                                          func=AF.Ln, bias=1.0)
                        inst = scalar.activation(out=t_re[:, :],
                                                 in_=ps_sm[:, cp, :],
                                                 func=AF.Relu,
                                                 bias=bdt_sb[:, c:c + 1])
                        tok = c_s.inc(inst)
                        ev[("sp", tau, c)] = tok
                        rel[("ps_dt", cp)] = tok

                    @block.vector
                    def _(vector, tau=tau, c=c, cp=cp):
                        W(vector, "vector", ev[("sp", tau, c)])
                        tok = c_v.inc(vector.tensor_add(
                            dt_f[:, cp, :], t_ln[:, :], t_re[:, :]))
                        ev[("dt", tau, c)] = tok
                        rel["t_sp"] = tok

                    @block.vector
                    def _(vector, tau=tau, par=par, c=c, cp=cp):
                        W(vector, "vector", ev[("xc", tau, c)])
                        ev[("dtx", tau, c)] = c_v.inc(vector.tensor_mul(
                            dtx_bf[:, cp, :], dt_f[:, cp, :],
                            xc_bf[:, par, c, :]))

                dt_chain(tau, par, 0)
                dt_chain(tau, par, 1)
                for c in range(NCH):
                    cp = c % 2
                    for g in range(NG):
                        p2 = (tau * NCH * NG + c * NG + g) % 2

                        @block.scalar
                        def _(scalar, tau=tau, c=c, g=g, p2=p2, cp=cp):
                            W(scalar, "scalar", ev[("dt", tau, c)],
                              rel.get(("da", p2)))
                            for j in range(GS):
                                s = g * GS + j
                                inst = scalar.activation(
                                    out=da_f[:, p2, j * T1 + 2:j * T1 + 2 + TT],
                                    in_=dt_f[:, cp, :], func=AF.Exp,
                                    scale=a_sb[:, c, s:s + 1])
                            ev[("da", tau, c, g)] = c_s.inc(inst)

                        if g == 0 and c + 2 < NCH:
                            dt_chain(tau, par, c + 2)

                        @block.vector
                        def _(vector, tau=tau, c=c, g=g, p2=p2, cp=cp):
                            W(vector, "vector", ev[("da", tau, c, g)],
                              ev[("dtx", tau, c)],
                              [ev[("bcev", tau, g * GS + j, w)]
                               for j in range(GS) for w in ("b", "c")],
                              rel.get("yc"))
                            for j in range(GS):
                                s = g * GS + j
                                vector.tensor_mul(
                                    dbx_bf[:, j * T1 + 2:j * T1 + 2 + TT],
                                    dtx_bf[:, cp, :], brep_bf[:, s, :])
                                vector.tensor_copy(
                                    dbx_bf[:, j * T1:j * T1 + 1],
                                    carr[:, c, s:s + 1])
                            vector.tensor_tensor_scan(
                                out=h_bf[:], data0=da_f[:, p2, :],
                                data1=dbx_bf[:], initial=0.0,
                                op0=ALU.mult, op1=ALU.add)
                            vector.tensor_copy(
                                carr[:, c, g * GS:(g + 1) * GS].unsqueeze(2),
                                h_bf[:].rearrange("p (s t) -> p s t", s=GS)
                                [:, :, T1 - 1:T1])
                            for j in range(GS):
                                s = g * GS + j
                                inst = vector.tensor_mul(
                                    yc_bf[:, j, :],
                                    h_bf[:, j * T1 + 2:j * T1 + 2 + TT],
                                    crep_bf[:, s, :])
                            tok = c_v.inc(inst)
                            ev[("scan", tau, c, g)] = tok
                            rel[("da", p2)] = tok
                            if c == NCH - 1 and g == NG - 1:
                                rel["brepcrep"] = tok

                        @block.tensor
                        def _(tensor, tau=tau, c=c, g=g):
                            W(tensor, "tensor", ev[("scan", tau, c, g)],
                              rel.get(("ps_y", c % 2)) if g == 0 else None)
                            for j in range(GS):
                                inst = nc.tensor.matmul(
                                    ps_y[:, c % 2, :], ident_bf[:, :],
                                    yc_bf[:, j, :],
                                    start=(g == 0 and j == 0),
                                    stop=(g == NG - 1 and j == GS - 1))
                            tok = c_t.inc(inst)
                            ev[("yacc", tau, c, g)] = tok
                            rel["yc"] = tok

                    @block.vector
                    def _(vector, tau=tau, par=par, c=c):
                        W(vector, "vector", ev[("yacc", tau, c, NG - 1)],
                          rel.get(("yfin", c % 2)))
                        vector.scalar_tensor_tensor(
                            out=epi_f[:], in0=xc_bf[:, par, c, :],
                            scalar=dp_sb[:, c:c + 1], in1=ps_y[:, c % 2, :],
                            op0=ALU.mult, op1=ALU.add)
                        tok = c_v.inc(vector.tensor_mul(
                            yfin_bf[:, c % 2, :], epi_f[:],
                            zs_bf[:, par, c, :]))
                        ev[("yfin", tau, c)] = tok
                        rel[("ps_y", c % 2)] = tok
                        if c == NCH - 1:
                            rel[("xc", par)] = tok
                            rel[("zs", par)] = tok

                    @block.gpsimd
                    def _(gpsimd, tau=tau, c=c, t0=t0, t1=t1):
                        W(gpsimd, "gpsimd", ev[("yfin", tau, c)])
                        tok = bdma(gpsimd, g_dma,
                                   lambda k, c=c, t0=t0, t1=t1:
                                   (y_shr[k, c, :, t0:t1],
                                    yfin_bf[:, c % 2, :]))
                        ev[("yw", tau, c)] = tok
                        rel[("yfin", c % 2)] = tok

                if tau == NT - 1:
                    @block.gpsimd
                    def _(gpsimd):
                        gpsimd.wait_ge(g_dma.h, g_dma.n)
                        bdma(gpsimd, g_dma,
                             lambda k: (flg_y[k, :],
                                        nonce_in[0:1, 32:34]))

                if BARRIER_TILES:
                    barrier_all(("mt", tau))

            m_end = {"scalar": c_s.now(), "vector": c_v.now(),
                     "tensor": c_t.now(), "gpsimd": (g_dma, g_dma.n),
                     "sync": s_dma.now()}

        def barrier(ends, tag):
            toks = {}
            for name in ("scalar", "vector", "tensor", "gpsimd", "sync"):
                @getattr(block, name)
                def _(eng, name=name):
                    for oname, tok in ends.items():
                        if oname != name:
                            eng.wait_ge(tok[0].h, tok[1])
                    if name == "scalar":
                        toks["t"] = c_s.inc(eng.nop())
            return toks["t"]

        bar_m = barrier(m_end, "m")

        with ExitStack() as st3:
            A3, P3 = mk(st3)
            wout_bf = A3("wout_bf", [128, 16, DM], BF16)
            wst3 = A3("wst3", [128, 2, DM], F32)
            y_in = A3("y_in", [128, 2, 16, TT], BF16)
            x3_st = A3("x3_st", [128, 2, NDM, TT], F32)
            v_f = A3("v_f", [128, NDM, TT], F32)
            v_bf = A3("v_bf", [128, NDM, TT], BF16)
            vsq_bf = A3("vsq_bf", [128, NDM, TT], BF16)
            var_f = A3("var_f", [128, TT], F32)
            r1_f = A3("r1_f", [128, TT], F32)
            r2_f = A3("r2_f", [128, TT], F32)
            r12_f = A3("r12_f", [128, TT], F32)
            tvmu = A3("tvmu", [128, NDM, TT], F32)
            x2_f = A3("x2_f", [128, NDM, TT], F32)
            x2n_bf = A3("x2n_bf", [128, NDM, TT], BF16)

            ps_o = P3("ps_o", [128, 2, TT])
            ps_st = P3("ps_st", [128, 2, TT])

            for part, src in (("lo", wout_lo), ("hi", wout_hi)):
                for k in range(NCH):
                    kk = (0 if part == "lo" else NCH) + k

                    @block.sync
                    def _(sync, src=src, k=k, kk=kk):
                        W(sync, "sync", bar_m, rel.get(("wst3", kk % 2)))
                        ev[("w3", kk)] = s_dma.inc(sync.dma_start(
                            out=wst3[:, kk % 2, :],
                            in_=src[k * 128:(k + 1) * 128, :]))

                    @block.scalar
                    def _(scalar, kk=kk):
                        W(scalar, "scalar", ev[("w3", kk)], bar_m)
                        rel[("wst3", kk % 2)] = c_s.inc(scalar.activation(
                            out=wout_bf[:, kk, :], in_=wst3[:, kk % 2, :],
                            func=AF.Copy))
            ev["w3d"] = c_s.now()

            @block.gpsimd
            def _(gpsimd):
                for oname, tok in m_end.items():
                    if oname != "gpsimd":
                        gpsimd.wait_ge(tok[0].h, tok[1])
                poll(gpsimd, lambda k: flg_y[k, :], lambda k: 32)
                ev["ypeer"] = c_g.inc(gpsimd.nop())

            for t3 in range(NT3):
                par = t3 % 2

                @block.gpsimd
                def _(gpsimd, t3=t3, par=par):
                    W(gpsimd, "gpsimd", ev["ypeer"], m_end["gpsimd"],
                      rel.get(("y_in", par)), rel.get(("x3", par)))
                    for hh in (0, 1):
                        with if_slot(gpsimd, hh):
                            a0 = hh * TOK + t3 * TT
                            gpsimd.dma_start(
                                out=y_in[:, par, :, :],
                                in_=y_shr[:, :, :, a0:a0 + TT].rearrange(
                                    "s c p t -> p (s c) t")
                            ).then_inc(g_dma.h, 16)
                            gpsimd.dma_start(
                                out=x3_st[:, par, :, :],
                                in_=xT[:, a0:a0 + TT].rearrange(
                                    "(c p) t -> p c t", p=128)
                            ).then_inc(g_dma.h, 16)
                        gpsimd.end_ifs()
                    g_dma.n += 32
                    ev[("y3", t3)] = g_dma.now()

                for i in range(NDM):
                    @block.tensor
                    def _(tensor, t3=t3, par=par, i=i):
                        W(tensor, "tensor", ev[("y3", t3)], ev["w3d"],
                          rel.get(("ps_o", i % 2)))
                        for k in range(16):
                            inst = nc.tensor.matmul(
                                ps_o[:, i % 2, :],
                                wout_bf[:, k, i * 128:(i + 1) * 128],
                                y_in[:, par, k, :],
                                start=(k == 0), stop=(k == 15))
                        ev[("om", t3, i)] = c_t.inc(inst)

                    @block.vector
                    def _(vector, t3=t3, par=par, i=i):
                        W(vector, "vector", ev[("om", t3, i)], ev[("y3", t3)],
                          ev.get(("x2n", t3 - 1)),
                          ev.get(("m3w", t3 - 1)))
                        tok = c_v.inc(vector.tensor_add(
                            v_f[:, i, :], x3_st[:, par, i, :],
                            ps_o[:, i % 2, :]))
                        ev[("v", t3, i)] = tok
                        rel[("ps_o", i % 2)] = tok
                        if i == NDM - 1:
                            rel[("x3", par)] = tok
                            rel[("y_in", par)] = tok

                @block.scalar
                def _(scalar, t3=t3):
                    W(scalar, "scalar", [ev[("v", t3, i)] for i in range(NDM)],
                      ev.get(("stat", t3 - 1)))
                    scalar.activation(out=v_bf[:], in_=v_f[:], func=AF.Copy)
                    ev[("vbf", t3)] = c_s.inc(scalar.activation(
                        out=vsq_bf[:], in_=v_f[:], func=AF.Square))

                @block.tensor
                def _(tensor, t3=t3):
                    W(tensor, "tensor", ev[("vbf", t3)], rel.get("ps_st"))
                    for k in range(NDM):
                        nc.tensor.matmul(ps_st[:, 0, :], onesh_bf[:, :],
                                         v_bf[:, k, :], start=(k == 0),
                                         stop=(k == NDM - 1))
                    for k in range(NDM):
                        inst = nc.tensor.matmul(ps_st[:, 1, :], onesh_bf[:, :],
                                                vsq_bf[:, k, :],
                                                start=(k == 0),
                                                stop=(k == NDM - 1))
                    ev[("stat", t3)] = c_t.inc(inst)

                @block.vector
                def _(vector, t3=t3):
                    W(vector, "vector", ev[("stat", t3)],
                      ev.get(("r1", t3 - 1)))
                    vector.tensor_copy(var_f[:], ps_st[:, 0, :])
                    vector.tensor_mul(var_f[:], var_f[:], ps_st[:, 0, :])
                    ev[("var", t3)] = c_v.inc(vector.tensor_sub(
                        var_f[:], ps_st[:, 1, :], var_f[:]))

                @block.scalar
                def _(scalar, t3=t3):
                    W(scalar, "scalar", ev[("var", t3)])
                    scalar.activation(out=r1_f[:], in_=var_f[:], func=AF.Ln,
                                      bias=EPS)
                    ev[("r1", t3)] = c_s.inc(scalar.activation(
                        out=r1_f[:], in_=r1_f[:], func=AF.Exp, scale=-0.5))

                @block.vector
                def _(vector, t3=t3):
                    W(vector, "vector", ev[("r1", t3)],
                      ev.get(("r2", t3 - 1)))
                    vector.tensor_mul(r2_f[:], r1_f[:], r1_f[:])
                    ev[("var2", t3)] = c_v.inc(vector.tensor_mul(
                        r2_f[:], r2_f[:], var_f[:]))

                @block.scalar
                def _(scalar, t3=t3):
                    W(scalar, "scalar", ev[("var2", t3)])
                    scalar.activation(out=r2_f[:], in_=r2_f[:], func=AF.Ln,
                                      bias=EPS)
                    ev[("r2", t3)] = c_s.inc(scalar.activation(
                        out=r2_f[:], in_=r2_f[:], func=AF.Exp, scale=-0.5))

                @block.vector
                def _(vector, t3=t3):
                    W(vector, "vector", ev[("r2", t3)],
                      ev.get(("x2n", t3 - 1)), ev.get(("m3w", t3 - 1)))
                    vector.tensor_mul(r12_f[:], r1_f[:], r2_f[:])
                    for i in range(NDM):
                        vector.scalar_tensor_tensor(
                            out=tvmu[:, i, :], in0=v_f[:, i, :], scalar=1.0,
                            in1=ps_st[:, 0, :], op0=ALU.mult, op1=ALU.subtract)
                        vector.tensor_mul(x2_f[:, i, :], tvmu[:, i, :], r1_f[:])
                        inst = vector.tensor_mul(tvmu[:, i, :], tvmu[:, i, :],
                                                 r12_f[:])
                    tok = c_v.inc(inst)
                    ev[("x2", t3)] = tok
                    rel["v_f"] = tok
                    rel["ps_st"] = tok

                @block.scalar
                def _(scalar, t3=t3):
                    W(scalar, "scalar", ev[("x2", t3)])
                    for i in range(NDM):
                        inst = scalar.activation(out=x2n_bf[:, i, :],
                                                 in_=tvmu[:, i, :],
                                                 func=AF.Identity,
                                                 scale=gam_sb[:, i:i + 1],
                                                 bias=bet_sb[:, i:i + 1])
                    ev[("x2n", t3)] = c_s.inc(inst)

                @block.sync
                def _(sync, t3=t3):
                    W(sync, "sync", ev[("x2n", t3)])
                    s_dma.inc(sync.dma_start(
                        out=x2_d[:, :, t3 * TT:(t3 + 1) * TT].transpose(
                            [1, 0, 2]),
                        in_=x2_f[:]))
                    ev[("m3w", t3)] = s_dma.inc(sync.dma_start(
                        out=x2n_d[:, :, t3 * TT:(t3 + 1) * TT].transpose(
                            [1, 0, 2]),
                        in_=x2n_bf[:]))

                if BARRIER_TILES:
                    barrier_all(("m3t", t3))

            m3_end = {"scalar": c_s.now(), "vector": c_v.now(),
                      "tensor": c_t.now(), "gpsimd": (g_dma, g_dma.n),
                      "sync": s_dma.now()}

        bar_3 = barrier(m3_end, "m3")

        with ExitStack() as stf:
            Af, Pf = mk(stf)
            w1_bf = Af("w1_bf", [128, NDM, FF], BF16)
            w2_bf = Af("w2_bf", [128, NFF, DM], BF16)

            with ExitStack() as stq:
                Aq, _ = mk(stq)
                qst = Aq("qst", [128, 2048], I32)
                qf = Aq("qf", [128, 2048], F32)
                scb = Aq("scb", [128, FF], F32)
                sel_f = Aq("sel_f", [32, 32 * 128], F32)
                sc_sb = Aq("sc_sb", [32, FF], F32)
                ps_sc = stq.enter_context(
                    nc.psum_tensor("ps_sc", [128, TT], F32))

                @block.sync
                def _(sync):
                    W(sync, "sync", bar_3)
                    s_dma.inc(sync.dma_start(out=sel_f[:], in_=selmat[:, :]))
                    ev["scl"] = s_dma.inc(sync.dma_start(
                        out=sc_sb[0:8, 0:FF], in_=s1[:, :]))

                for wi, (qsrc, nk, fdim, dst, nrows) in enumerate(
                        [(q1, NDM, FF, w1_bf, 8), (q2, NFF, DM, w2_bf, 32)]):
                    if wi == 1:
                        @block.sync
                        def _(sync):
                            W(sync, "sync", ev["w1done"])
                            ev["scl"] = s_dma.inc(sync.dma_start(
                                out=sc_sb[0:32, 0:DM], in_=s2[:, :]))
                    for g in range(nk):
                        for jj in range(fdim // TT):
                            @block.tensor
                            def _(tensor, g=g, jj=jj, nrows=nrows):
                                W(tensor, "tensor", ev["scl"], bar_3,
                                  rel.get("ps_sc"))
                                ev["scm"] = c_t.inc(nc.tensor.matmul(
                                    ps_sc[:, :],
                                    sel_f[0:nrows, g * 128:(g + 1) * 128],
                                    sc_sb[0:nrows, jj * TT:(jj + 1) * TT],
                                    start=True, stop=True))

                            @block.scalar
                            def _(scalar, jj=jj):
                                W(scalar, "scalar", ev["scm"], rel.get("scb"))
                                rel["ps_sc"] = c_s.inc(scalar.activation(
                                    out=scb[:, jj * TT:(jj + 1) * TT],
                                    in_=ps_sc[:, :], func=AF.Copy))
                        for fh in range((fdim + 2047) // 2048):
                            w0 = fh * 2048
                            w1w = min(fdim - w0, 2048)

                            @block.sync
                            def _(sync, qsrc=qsrc, g=g, w0=w0, w1w=w1w):
                                W(sync, "sync", bar_3, rel.get("qst"))
                                ev["ql"] = s_dma.inc(sync.dma_start(
                                    out=qst[:, 0:w1w],
                                    in_=qsrc[g * 128:(g + 1) * 128,
                                             w0:w0 + w1w]))

                            @block.scalar
                            def _(scalar, w1w=w1w):
                                W(scalar, "scalar", ev["ql"], rel.get("qf"))
                                ev["qd"] = c_s.inc(scalar.activation(
                                    out=qf[:, 0:w1w], in_=qst[:, 0:w1w],
                                    func=AF.Copy, bias=-128.0))

                            @block.vector
                            def _(vector, g=g, w0=w0, w1w=w1w, dst=dst):
                                W(vector, "vector", ev["qd"], rel.get("ps_sc"))
                                tok = c_v.inc(vector.tensor_mul(
                                    dst[:, g, w0:w0 + w1w], qf[:, 0:w1w],
                                    scb[:, w0:w0 + w1w]))
                                rel["qst"] = tok
                                rel["qf"] = tok
                                if w0 + w1w == fdim:
                                    rel["scb"] = tok
                    if wi == 0:
                        ev["w1done"] = c_v.now()
                ev["fw"] = c_v.now()

            act_bf = Af("act_bf", [128, 16, TT], BF16)
            z2a = Af("z2a", [128, NDM, TT], F32)
            x2r = Af("x2r", [128, NDM, TT], F32)
            x2nr = Af("x2nr", [128, NDM, TT], BF16)
            f_ta = Af("f_ta", [128, TT], F32)
            f_tb = Af("f_tb", [128, TT], F32)
            f_tc = Af("f_tc", [128, TT], F32)
            fvar = Af("fvar", [128, TT], F32)
            fr = Af("fr", [128, TT], F32)

            ps_f1 = Pf("ps_f1", [128, 2, TT])
            ps_f2 = Pf("ps_f2", [128, 2, TT])
            ps_fs = Pf("ps_fs", [128, 2, TT])

            for t3 in range(NT3):
                @block.sync
                def _(sync, t3=t3):
                    W(sync, "sync", ev[("m3w", t3)], rel.get("x2nr"))
                    s_dma.inc(sync.dma_start(
                        out=x2nr[:, :, :],
                        in_=x2n_d[:, :, t3 * TT:(t3 + 1) * TT].rearrange(
                            "c p t -> p c t")))
                    ev[("fx", t3)] = s_dma.inc(sync.dma_start(
                        out=x2r[:, :, :],
                        in_=x2_d[:, :, t3 * TT:(t3 + 1) * TT].rearrange(
                            "c p t -> p c t")))

                for halfF in range(2):
                    for fi in range(16):
                        f = halfF * 16 + fi

                        @block.tensor
                        def _(tensor, t3=t3, f=f, fi=fi):
                            W(tensor, "tensor", ev[("fx", t3)], ev["fw"],
                              rel.get(("ps_f1", fi % 2)))
                            for k in range(NDM):
                                inst = nc.tensor.matmul(
                                    ps_f1[:, fi % 2, :],
                                    w1_bf[:, k, f * 128:(f + 1) * 128],
                                    x2nr[:, k, :],
                                    start=(k == 0), stop=(k == NDM - 1))
                            ev[("f1", t3, f)] = c_t.inc(inst)

                        @block.scalar
                        def _(scalar, t3=t3, f=f, fi=fi):
                            W(scalar, "scalar", ev[("f1", t3, f)],
                              rel.get("f_t"))
                            inst = scalar.activation(out=f_tb[:],
                                                     in_=ps_f1[:, fi % 2, :],
                                                     func=AF.Sigmoid,
                                                     bias=b1_sb[:, f:f + 1])
                            ev[("fs", t3, f)] = c_s.inc(inst)

                        @block.vector
                        def _(vector, t3=t3, f=f, fi=fi):
                            W(vector, "vector", ev[("fs", t3, f)],
                              rel.get(("act", fi)))
                            vector.tensor_scalar_add(
                                f_tc[:], ps_f1[:, fi % 2, :],
                                b1_sb[:, f:f + 1])
                            tok = c_v.inc(vector.tensor_mul(
                                act_bf[:, fi, :], f_tc[:], f_tb[:]))
                            ev[("act", t3, f)] = tok
                            rel["f_t"] = tok
                            rel[("ps_f1", fi % 2)] = tok

                    for i in range(NDM):
                        @block.tensor
                        def _(tensor, t3=t3, halfF=halfF, i=i):
                            W(tensor, "tensor",
                              [ev[("act", t3, halfF * 16 + fk)]
                               for fk in range(16)],
                              rel.get(("ps_f2", i % 2)))
                            for fk in range(16):
                                inst = nc.tensor.matmul(
                                    ps_f2[:, i % 2, :],
                                    w2_bf[:, halfF * 16 + fk,
                                          i * 128:(i + 1) * 128],
                                    act_bf[:, fk, :],
                                    start=(fk == 0), stop=(fk == 15))
                            tok = c_t.inc(inst)
                            ev[("f2", t3, halfF, i)] = tok
                            if i == NDM - 1:
                                for fi2 in range(16):
                                    rel[("act", fi2)] = tok

                        if halfF == 0:
                            @block.scalar
                            def _(scalar, t3=t3, i=i):
                                W(scalar, "scalar", ev[("f2", t3, 0, i)],
                                  rel.get(("z2a", i)))
                                tok = c_s.inc(scalar.activation(
                                    out=z2a[:, i, :], in_=ps_f2[:, i % 2, :],
                                    func=AF.Identity, scale=0.5,
                                    bias=b2h_sb[:, i:i + 1]))
                                ev[("z2a", t3, i)] = tok
                                rel[("ps_f2", i % 2)] = tok
                        else:
                            @block.vector
                            def _(vector, t3=t3, i=i):
                                W(vector, "vector", ev[("f2", t3, 1, i)],
                                  ev[("z2a", t3, i)])
                                vector.scalar_tensor_tensor(
                                    out=z2a[:, i, :], in0=ps_f2[:, i % 2, :],
                                    scalar=0.5, in1=z2a[:, i, :],
                                    op0=ALU.mult, op1=ALU.add)
                                tok = c_v.inc(vector.scalar_tensor_tensor(
                                    out=z2a[:, i, :], in0=x2r[:, i, :],
                                    scalar=2.0, in1=z2a[:, i, :],
                                    op0=ALU.mult, op1=ALU.add))
                                ev[("u", t3, i)] = tok
                                rel[("ps_f2", i % 2)] = tok

                @block.scalar
                def _(scalar, t3=t3):
                    W(scalar, "scalar", [ev[("u", t3, i)] for i in range(NDM)],
                      ev.get(("fst", t3 - 1)))
                    scalar.activation(out=act_bf[:, 0:NDM, :],
                                      in_=z2a[:], func=AF.Copy)
                    ev[("ub", t3)] = c_s.inc(scalar.activation(
                        out=act_bf[:, NDM:2 * NDM, :],
                        in_=z2a[:], func=AF.Square))

                @block.tensor
                def _(tensor, t3=t3):
                    W(tensor, "tensor", ev[("ub", t3)], rel.get("ps_fs"))
                    for k in range(NDM):
                        nc.tensor.matmul(ps_fs[:, 0, :], onesh_bf[:, :],
                                         act_bf[:, k, :], start=(k == 0),
                                         stop=(k == NDM - 1))
                    for k in range(NDM):
                        inst = nc.tensor.matmul(ps_fs[:, 1, :], onesh_bf[:, :],
                                                act_bf[:, NDM + k, :],
                                                start=(k == 0),
                                                stop=(k == NDM - 1))
                    ev[("fst", t3)] = c_t.inc(inst)

                @block.vector
                def _(vector, t3=t3):
                    W(vector, "vector", ev[("fst", t3)],
                      ev.get(("fr", t3 - 1)))
                    vector.tensor_copy(fvar[:], ps_fs[:, 0, :])
                    vector.tensor_mul(fvar[:], fvar[:], ps_fs[:, 0, :])
                    ev[("fvar", t3)] = c_v.inc(vector.tensor_sub(
                        fvar[:], ps_fs[:, 1, :], fvar[:]))

                @block.scalar
                def _(scalar, t3=t3):
                    W(scalar, "scalar", ev[("fvar", t3)])
                    scalar.activation(out=fr[:], in_=fvar[:], func=AF.Ln,
                                      bias=EPS)
                    ev[("fr", t3)] = c_s.inc(scalar.activation(
                        out=fr[:], in_=fr[:], func=AF.Exp, scale=-0.5))

                @block.vector
                def _(vector, t3=t3):
                    W(vector, "vector", ev[("fr", t3)],
                      ev.get(("outw", t3 - 1)))
                    for i in range(NDM):
                        vector.scalar_tensor_tensor(
                            out=x2r[:, i, :], in0=z2a[:, i, :], scalar=1.0,
                            in1=ps_fs[:, 0, :], op0=ALU.mult, op1=ALU.subtract)
                        inst = vector.tensor_mul(x2r[:, i, :], x2r[:, i, :],
                                                 fr[:])
                    tok = c_v.inc(inst)
                    ev[("fout", t3)] = tok
                    rel["ps_fs"] = tok
                    for i in range(NDM):
                        rel[("z2a", i)] = tok

                @block.sync
                def _(sync, t3=t3):
                    W(sync, "sync", ev[("fout", t3)])
                    tok = s_dma.inc(sync.dma_start(
                        out=out_d[:, :, t3 * TT:(t3 + 1) * TT].transpose(
                            [1, 0, 2]),
                        in_=x2r[:]))
                    ev[("outw", t3)] = tok
                    rel["x2nr"] = tok

                if BARRIER_TILES:
                    barrier_all(("ft", t3))

            @block.sync
            def _(sync):
                sync.wait_ge(s_dma.h, s_dma.n)

            @block.gpsimd
            def _(gpsimd):
                gpsimd.wait_ge(s_dma.h, s_dma.n)

    return nc


_CACHE = {}


def kernel(**inputs):
    if "nc" not in _CACHE:
        _CACHE["nc"] = build_nc()
    nc = _CACHE["nc"]

    gi = lambda k: np.ascontiguousarray(inputs[k])
    x = gi("L0_attn_x")
    W_in = gi("L0_attn_W_in")
    conv_w = gi("L0_attn_conv_w")
    conv_b = gi("L0_attn_conv_b")
    W_xproj = gi("L0_attn_W_xproj")
    W_dt = gi("L0_attn_W_dt")
    b_dt = gi("L0_attn_b_dt")
    A_log = gi("L0_attn_A_log")
    Dp = gi("L0_attn_Dp")
    W_out = gi("L0_attn_W_out")
    ln_g = gi("L0_ffn_ln_gamma")
    ln_b = gi("L0_ffn_ln_beta")
    fc1_q = gi("L0_ffn_fc1_q")
    fc1_s = gi("L0_ffn_fc1_scales")
    fc1_b = gi("L0_ffn_fc1_bias")
    fc2_q = gi("L0_ffn_fc2_q")
    fc2_s = gi("L0_ffn_fc2_scales")
    fc2_b = gi("L0_ffn_fc2_bias")

    selm = np.zeros((32, 32 * 128), dtype=np.float32)
    for s in range(32):
        selm[s, s * 128:(s + 1) * 128] = 1.0
    identm = np.eye(128, dtype=np.float32)
    onesh = np.full((128, 128), 1.0 / DM, dtype=np.float32)
    da01 = np.tile(np.array([0.0, 1.0], np.float32), (128, 1))

    rng_n = np.random.default_rng()
    nonces = [rng_n.integers(1, 2**32 - 1, size=(1, 64), dtype=np.uint32)
              for _ in range(4)]
    in_maps = []
    for c in range(8):
        b, h = c // 2, c % 2
        ch0 = h * CH
        im = {
            "xT": np.ascontiguousarray(x[b].T),
            "winx": np.ascontiguousarray(W_in[:, ch0:ch0 + CH]),
            "winz": np.ascontiguousarray(W_in[:, 2048 + ch0:2048 + ch0 + CH]),
            "wxp": np.ascontiguousarray(W_xproj[ch0:ch0 + CH, :]),
            "wdt": np.ascontiguousarray(W_dt[:, ch0:ch0 + CH]),
            "convb": np.ascontiguousarray(
                conv_b[ch0:ch0 + CH].reshape(NCH, 128).T),
            "bdt": np.ascontiguousarray(b_dt[ch0:ch0 + CH].reshape(NCH, 128).T),
            "alog": np.ascontiguousarray(
                A_log[ch0:ch0 + CH].reshape(NCH, 128, DS).transpose(1, 0, 2)),
            "dp": np.ascontiguousarray(Dp[ch0:ch0 + CH].reshape(NCH, 128).T),
            "wout_lo": np.ascontiguousarray(W_out[0:CH, :]),
            "wout_hi": np.ascontiguousarray(W_out[CH:2 * CH, :]),
            "gam": np.ascontiguousarray(ln_g.reshape(NDM, 128).T),
            "bet": np.ascontiguousarray(ln_b.reshape(NDM, 128).T),
            "q1": fc1_q,
            "s1": fc1_s,
            "b1": np.ascontiguousarray(fc1_b.reshape(NFF, 128).T),
            "q2": fc2_q,
            "s2": fc2_s,
            "b2": np.ascontiguousarray(fc2_b.reshape(NDM, 128).T),
            "selmat": selm,
            "identm": identm,
            "onesh": onesh,
            "da01": da01,
            "nonce_in": nonces[b],
        }
        cd = np.zeros((32, 128, 128), dtype=np.float32)
        for cc in range(NCH):
            for k in range(DCONV):
                np.fill_diagonal(cd[cc * 4 + k],
                                 conv_w[ch0 + cc * 128:ch0 + (cc + 1) * 128, k])
        im["convdiag"] = cd
        in_maps.append(im)

    res = run_bass_kernel_spmd(nc, in_maps, core_ids=list(range(8)))

    full = np.empty((B, L, DM), dtype=np.float32)
    for c in range(8):
        b, h = c // 2, c % 2
        o = res.results[c]["out"]
        full[b, h * TOK:(h + 1) * TOK, :] = \
            o.transpose(2, 0, 1).reshape(TOK, DM)
    return full


# version 09
# speedup vs baseline: 1.1913x; 1.1913x over previous
import numpy as np
from contextlib import ExitStack

import concourse.bass as bass
import concourse.mybir as mybir
from concourse.bass_utils import run_bass_kernel_spmd

F32 = mybir.dt.float32
BF16 = mybir.dt.bfloat16
U32 = mybir.dt.uint32
I32 = mybir.dt.int32
AF = mybir.ActivationFunctionType
ALU = mybir.AluOpType
ENG = mybir.EngineType

B, L, DM = 4, 8192, 1024
DS, DCONV, DTR = 16, 4, 64
CH, NCH, NDM = 1024, 8, 8
TT = 512
NT = L // TT
TOK = L // 2
NT3 = TOK // TT
T1 = TT + 2
GS, NG = 4, 4
FF, NFF = 4096, 32
EPS = 1e-5


class Sem:
    def __init__(self, stack, nc, name, engine=None):
        self.h = stack.enter_context(nc.semaphore(name))
        self.n = 0
        self.engine = engine
        self.dk = 1 if engine else 16

    def inc(self, instr, k=None):
        k = self.dk if k is None else k
        instr.then_inc(self.h, k)
        self.n += k
        return (self, self.n)

    def now(self):
        return (self, self.n)


def _flat(evs):
    for e in evs:
        if e is None:
            continue
        if isinstance(e, list):
            yield from _flat(e)
        else:
            yield e


def build_nc():
    nc = bass.Bass("TRN2", target_bir_lowering=False)

    def din(name, shape, dt=F32):
        return nc.dram_tensor(name, shape, dt, kind="ExternalInput").ap()

    xT = din("xT", [DM, L])
    winx = din("winx", [DM, CH])
    winz = din("winz", [DM, CH])
    wxp = din("wxp", [CH, 96])
    wdt_in = din("wdt", [DTR, CH])
    convdiag = din("convdiag", [32, 128, 128])
    convb = din("convb", [128, NCH])
    bdt = din("bdt", [128, NCH])
    alog = din("alog", [128, NCH, DS])
    dp_in = din("dp", [128, NCH])
    wout_lo = din("wout_lo", [CH, DM])
    wout_hi = din("wout_hi", [CH, DM])
    gam = din("gam", [128, NDM])
    bet = din("bet", [128, NDM])
    q1 = din("q1", [DM, FF], I32)
    s1 = din("s1", [8, FF])
    b1 = din("b1", [128, NFF])
    q2 = din("q2", [FF, DM], I32)
    s2 = din("s2", [32, DM])
    b2 = din("b2", [128, NDM])
    selmat = din("selmat", [32, 32 * 128])
    identm = din("identm", [128, 128])
    onesh = din("onesh", [128, 128])
    da01_in = din("da01", [128, 2])
    nonce_in = din("nonce_in", [1, 64], U32)

    out_d = nc.dram_tensor("out", [NDM, 128, TOK], F32, kind="ExternalOutput").ap()
    x2_d = nc.dram_tensor("x2_d", [NDM, 128, TOK], F32, kind="Internal").ap()
    x2n_d = nc.dram_tensor("x2n_d", [NDM, 128, TOK], BF16, kind="Internal").ap()

    xdbl_shr = nc.dram_tensor("xdbl_shr", [2, 96, L], F32, kind="Internal",
                              addr_space="Shared").ap()
    y_shr = nc.dram_tensor("y_shr", [2, NCH, 128, L], BF16, kind="Internal",
                           addr_space="Shared").ap()
    flg_xd = nc.dram_tensor("flg_xd", [2, NT, 2], U32, kind="Internal",
                            addr_space="Shared").ap()
    flg_y = nc.dram_tensor("flg_y", [2, 2], U32, kind="Internal",
                           addr_space="Shared").ap()

    with ExitStack() as st0:
        def mk(stk):
            def A(name, shape, dt=F32):
                return stk.enter_context(nc.sbuf_tensor(name, shape, dt))

            def P(name, shape):
                return stk.enter_context(nc.psum_tensor(name, shape, F32))
            return A, P

        A0, _ = mk(st0)
        ident_bf = A0("ident_bf", [128, 128], BF16)
        onesh_bf = A0("onesh_bf", [128, 128], BF16)
        sel_bf = A0("sel_bf", [32, 32 * 128], BF16)
        nonce = A0("nonce", [1, 64], F32)
        flagw = A0("flagw", [1, 2], U32)
        gam_sb = A0("gam_sb", [128, NDM], F32)
        bet_sb = A0("bet_sb", [128, NDM], F32)
        b2h_sb = A0("b2h_sb", [128, NDM], F32)
        b1_sb = A0("b1_sb", [128, NFF], F32)
        nb1_sb = A0("nb1_sb", [128, NFF], F32)
        const_sb = A0("const_sb", [128, 2], F32)

        nc.const_aps.register(F32, EPS, const_sb[:, 0:1])
        nc.const_aps.register(F32, -128.0, const_sb[:, 1:2])

        block = st0.enter_context(nc.Block())

        s_dma = Sem(st0, nc, "s_dma")
        g_dma = Sem(st0, nc, "g_dma")
        s_poll = Sem(st0, nc, "s_poll")
        c_s = Sem(st0, nc, "c_s", "scalar")
        c_v = Sem(st0, nc, "c_v", "vector")
        c_t = Sem(st0, nc, "c_t", "tensor")
        c_g = Sem(st0, nc, "c_g", "gpsimd")

        def W(eng, engname, *evs):
            seen = {}
            for sem, val in _flat(evs):
                if val <= 0 or sem.engine == engname:
                    continue
                if seen.get(id(sem), -1) < val:
                    seen[id(sem)] = val
                    eng.wait_ge(sem.h, val)

        ev = {}
        rel = {}
        regs = {}

        @block.gpsimd
        def _(gpsimd):
            pid = nc.partition_id([ENG.Pool])
            r = gpsimd.alloc_register("slot")
            gpsimd.reg_alu(r, gpsimd.to_reg(pid), 1, ALU.bitwise_and)
            regs["slot"] = r
            pt = gpsimd.alloc_register("ptgt")
            gpsimd.reg_mov(pt, 0)
            regs["ptgt"] = pt

        def if_slot(gpsimd, k):
            return gpsimd.If_eq(regs["slot"], k)

        def bdma(gpsimd, sem, mk_pair, use_peer=False):
            for k in (0, 1):
                with if_slot(gpsimd, (1 - k) if use_peer else k):
                    o, i = mk_pair(k)
                    gpsimd.dma_start(out=o, in_=i).then_inc(sem.h, 16)
                gpsimd.end_ifs()
            sem.n += 16
            return (sem, sem.n)

        _pollid = [0]

        def poll(gpsimd, flag_ap_fn, word_fn):
            pt = regs["ptgt"]
            u = str(_pollid[0])
            _pollid[0] += 1
            with (
                gpsimd.register("nd" + u) as nd,
                gpsimd.register("f0" + u) as f0,
                gpsimd.register("f1" + u) as f1,
                gpsimd.register("e0" + u) as e0,
                gpsimd.register("e1" + u) as e1,
                gpsimd.register("t0" + u) as t0,
                gpsimd.register("t1" + u) as t1r,
            ):
                for k in (0, 1):
                    with if_slot(gpsimd, 1 - k):
                        w = word_fn(k)
                        gpsimd.reg_load(e0, nonce[0:1, w:w + 1].bitcast(U32))
                        gpsimd.reg_load(e1, nonce[0:1, w + 1:w + 2].bitcast(U32))
                    gpsimd.end_ifs()
                gpsimd.reg_mov(nd, 1)
                with gpsimd.While(nd):
                    gpsimd.reg_alu(pt, pt, 16, ALU.add)
                    for k in (0, 1):
                        with if_slot(gpsimd, 1 - k):
                            gpsimd.dma_start(out=flagw[0:1, :],
                                             in_=flag_ap_fn(k)
                                             ).then_inc(s_poll.h, 16)
                        gpsimd.end_ifs()
                    gpsimd.wait_ge(s_poll.h, pt)
                    gpsimd.reg_load(f0, flagw[0:1, 0:1])
                    gpsimd.reg_load(f1, flagw[0:1, 1:2])
                    gpsimd.reg_alu(t0, f0, e0, ALU.not_equal)
                    gpsimd.reg_alu(t1r, f1, e1, ALU.not_equal)
                    gpsimd.reg_alu(nd, t0, t1r, ALU.bitwise_or)

        BARRIER_TILES = False

        def barrier_all(tag):
            ends = {"scalar": c_s.now(), "vector": c_v.now(),
                    "tensor": c_t.now(), "gpsimd": (g_dma, g_dma.n),
                    "sync": s_dma.now()}
            marks = {}
            for name in ("scalar", "vector", "tensor", "gpsimd", "sync"):
                sem = {"scalar": c_s, "vector": c_v, "tensor": c_t,
                       "gpsimd": c_g, "sync": None}[name]

                @getattr(block, name)
                def _(eng, name=name, sem=sem):
                    for oname, tok in ends.items():
                        if oname != name:
                            eng.wait_ge(tok[0].h, tok[1])
                    if sem is not None:
                        marks[name] = sem.inc(eng.nop())
            for name in ("scalar", "vector", "tensor", "gpsimd", "sync"):
                @getattr(block, name)
                def _(eng, name=name):
                    for oname, tok in marks.items():
                        if oname != name:
                            eng.wait_ge(tok[0].h, tok[1])

        with ExitStack() as stm:
            Am, Pm = mk(stm)
            winx_bf = Am("winx_bf", [128, NDM, CH], BF16)
            winz_bf = Am("winz_bf", [128, NDM, CH], BF16)
            wxp_bf = Am("wxp_bf", [128, NCH, 96], BF16)
            wdt_bf = Am("wdt_bf", [DTR, CH], BF16)
            cdiag_bf = Am("cdiag_bf", [128, 32, 128], BF16)
            convb_sb = Am("convb_sb", [128, NCH], F32)
            bdt_sb = Am("bdt_sb", [128, NCH], F32)
            a_sb = Am("a_sb", [128, NCH, DS], F32)
            dp_sb = Am("dp_sb", [128, NCH], F32)
            da01_f = Am("da01_f", [128, 2], F32)

            with ExitStack() as stw:
                Aw, _ = mk(stw)
                wstage = Aw("wstage", [128, 2, 8192], F32)

                @block.sync
                def _(sync):
                    for dst, src in [
                        (convb_sb[:], convb[:, :]),
                        (bdt_sb[:], bdt[:, :]),
                        (a_sb[:], alog[:, :, :]),
                        (dp_sb[:], dp_in[:, :]),
                        (da01_f[:], da01_in[:, :]),
                        (nonce[:].bitcast(U32), nonce_in[:, :]),
                        (gam_sb[:], gam[:, :]),
                        (bet_sb[:], bet[:, :]),
                        (b2h_sb[:], b2[:, :]),
                        (b1_sb[:], b1[:, :]),
                        (wstage[0:32, 1, 0:32 * 128], selmat[:, :]),
                        (wstage[:, 0, 0:128], identm[:, :]),
                        (wstage[:, 0, 128:256], onesh[:, :]),
                        (wstage[0:64, 0, 256:256 + CH], wdt_in[:, :]),
                        (wstage[:, 0, 2048:2048 + 32 * 128].rearrange(
                            "p (a c) -> p a c", a=32),
                         convdiag[:, :, :].transpose([1, 0, 2])),
                        (wstage[:, 0, 6144:6144 + NCH * 96].rearrange(
                            "p (c n) -> p c n", c=NCH),
                         wxp[:, :].rearrange("(c p) n -> p c n", p=128)),
                    ]:
                        sync.dma_start(out=dst, in_=src).then_inc(s_dma.h, 16)
                        s_dma.n += 16
                    ev["ld0"] = s_dma.now()

                @block.scalar
                def _(scalar):
                    W(scalar, "scalar", ev["ld0"])
                    scalar.activation(out=ident_bf[:], in_=wstage[:, 0, 0:128],
                                      func=AF.Copy)
                    scalar.activation(out=onesh_bf[:], in_=wstage[:, 0, 128:256],
                                      func=AF.Copy)
                    scalar.activation(out=sel_bf[:],
                                      in_=wstage[0:32, 1, 0:32 * 128],
                                      func=AF.Copy)
                    scalar.activation(out=wdt_bf[:],
                                      in_=wstage[0:64, 0, 256:256 + CH],
                                      func=AF.Copy)
                    scalar.activation(out=cdiag_bf[:].rearrange("p a c -> p (a c)"),
                                      in_=wstage[:, 0, 2048:2048 + 32 * 128],
                                      func=AF.Copy)
                    scalar.activation(out=wxp_bf[:].rearrange("p c n -> p (c n)"),
                                      in_=wstage[:, 0, 6144:6144 + NCH * 96],
                                      func=AF.Copy)
                    scalar.activation(out=a_sb[:], in_=a_sb[:], func=AF.Exp)
                    scalar.activation(out=nb1_sb[:], in_=b1_sb[:], func=AF.Copy,
                                      scale=-1.0)
                    ev["sc0"] = c_s.inc(scalar.activation(
                        out=b2h_sb[:], in_=b2h_sb[:], func=AF.Copy, scale=0.5))
                    rel[("wst", 0)] = ev["sc0"]
                    rel[("wst", 1)] = ev["sc0"]

                for kind, src, dst in (("x", winx, winx_bf), ("z", winz, winz_bf)):
                    for k in range(NDM):
                        @block.sync
                        def _(sync, kind=kind, src=src, k=k):
                            W(sync, "sync", rel.get(("wst", k % 2)))
                            ev[("wl", kind, k)] = s_dma.inc(sync.dma_start(
                                out=wstage[:, k % 2, 0:CH],
                                in_=src[k * 128:(k + 1) * 128, :]))

                        @block.scalar
                        def _(scalar, kind=kind, k=k, dst=dst):
                            W(scalar, "scalar", ev[("wl", kind, k)])
                            rel[("wst", k % 2)] = c_s.inc(scalar.activation(
                                out=dst[:, k, :], in_=wstage[:, k % 2, 0:CH],
                                func=AF.Copy))
                ev["wm"] = c_s.now()
                rel[("x_st", 0)] = ev["wm"]
                rel[("x_st", 1)] = ev["wm"]

                @block.vector
                def _(vector):
                    W(vector, "vector", ev["sc0"])
                    vector.tensor_scalar_mul(a_sb[:], a_sb[:], -1.0)
                    vector.memset(const_sb[:, 0:1], EPS)
                    vector.memset(const_sb[:, 1:2], -128.0)
                    ev["v0"] = c_v.inc(vector.nop())

            x_st = Am("x_st", [128, 2, 1, TT], F32)
            xbf = Am("xbf", [128, NDM, TT], BF16)
            xin_bf = Am("xin_bf", [128, 2, NCH, TT + 3], BF16)
            zs_bf = Am("zs_bf", [128, 2, NCH, TT], BF16)
            xc_bf = Am("xc_bf", [128, 2, NCH, TT], BF16)
            dt_f = Am("dt_f", [128, 2, TT], F32)
            dtx_bf = Am("dtx_bf", [128, 2, TT], BF16)
            xd_mine = Am("xd_mine", [96, 2, TT], F32)
            xd_peer = Am("xd_peer", [96, TT], F32)
            xd_sum = Am("xd_sum", [96, TT], F32)
            dtraw_bf = Am("dtraw_bf", [DTR, TT], BF16)
            bc_bf = Am("bc_bf", [32, TT], BF16)
            brep_bf = Am("brep_bf", [128, DS, TT], BF16)
            crep_bf = Am("crep_bf", [128, DS, TT], BF16)
            da_f = Am("da_f", [128, 2, GS * T1], F32)
            dbx_bf = Am("dbx_bf", [128, GS * T1], BF16)
            h_bf = Am("h_bf", [128, GS * T1], BF16)
            yc_bf = Am("yc_bf", [128, 2, GS, TT], BF16)
            carr = Am("carr", [128, NCH, DS], F32)
            t_a = Am("t_a", [128, TT], F32)
            t_ln = Am("t_ln", [128, TT], F32)
            t_re = Am("t_re", [128, TT], F32)
            epi_f = Am("epi_f", [128, TT], F32)
            yfin_bf = Am("yfin_bf", [128, 2, TT], BF16)

            ps_xz = Pm("ps_xz", [128, 2, TT])
            ps_conv = ps_xz
            ps_sm = Pm("ps_sm", [128, 2, TT])
            ps_bc = Pm("ps_bc", [128, 2, TT])
            ps_y = Pm("ps_y", [128, 2, TT])

            @block.vector
            def _(vector):
                W(vector, "vector", ev["v0"], ev["wm"])
                vector.memset(carr[:], 0.0)
                vector.memset(dbx_bf[:], 0.0)
                vector.memset(xin_bf[:, :, :, 0:3], 0.0)
                for p2 in range(2):
                    for s in range(GS):
                        vector.tensor_copy(da_f[:, p2, s * T1:s * T1 + 2],
                                           da01_f[:])
                ev["v1"] = c_v.inc(vector.nop())

            for tau in range(NT):
                par = tau % 2
                t0, t1 = tau * TT, (tau + 1) * TT

                for hx in range(8):
                    @block.sync
                    def _(sync, tau=tau, par=par, t0=t0, hx=hx):
                        W(sync, "sync", rel.get(("x_st", par)))
                        ev[("x", tau, hx)] = s_dma.inc(sync.dma_start(
                            out=x_st[:, par, :, :],
                            in_=xT[hx * 128:(hx + 1) * 128, t0:t0 + TT]
                            .rearrange("(c p) t -> p c t", p=128)))

                    @block.scalar
                    def _(scalar, tau=tau, par=par, hx=hx):
                        W(scalar, "scalar", ev[("x", tau, hx)],
                          rel.get("xbf") if hx == 0 else None,
                          [ev["wm"], ev["v1"]] if tau == 0 else None)
                        tok = c_s.inc(scalar.activation(
                            out=xbf[:, hx:hx + 1, :],
                            in_=x_st[:, par, :, :], func=AF.Copy))
                        ev[("xbf", tau, hx)] = tok
                        rel[("x_st", par)] = tok

                for i in range(16):
                    @block.tensor
                    def _(tensor, tau=tau, i=i):
                        W(tensor, "tensor", ev[("xbf", tau, 7)],
                          rel.get(("ps_xz", i % 2)))
                        wsrc = winx_bf if i < NCH else winz_bf
                        io = i % NCH
                        for k in range(NDM):
                            inst = nc.tensor.matmul(
                                ps_xz[:, i % 2, :],
                                wsrc[:, k, io * 128:(io + 1) * 128],
                                xbf[:, k, :], start=(k == 0), stop=(k == NDM - 1))
                        tok = c_t.inc(inst)
                        ev[("xz", tau, i)] = tok
                        if i == 15:
                            rel["xbf"] = tok

                    @block.scalar
                    def _(scalar, tau=tau, par=par, i=i):
                        W(scalar, "scalar", ev[("xz", tau, i)],
                          rel.get(("xin", par)) if i < NCH else
                          rel.get(("zs", par)))
                        if i < NCH:
                            inst = scalar.activation(
                                out=xin_bf[:, par, i, 3:TT + 3],
                                in_=ps_xz[:, i % 2, :], func=AF.Copy)
                        else:
                            inst = scalar.activation(
                                out=zs_bf[:, par, i - NCH, :],
                                in_=ps_xz[:, i % 2, :], func=AF.Silu)
                        tok = c_s.inc(inst)
                        ev[("xzev", tau, i)] = tok
                        rel[("ps_xz", i % 2)] = tok

                @block.scalar
                def _(scalar, tau=tau, par=par):
                    if tau >= 1:
                        W(scalar, "scalar", ev[("conv", tau - 1, NCH - 1)])
                        for c in range(NCH):
                            scalar.activation(out=xin_bf[:, par, c, 0:3],
                                              in_=xin_bf[:, 1 - par, c, TT:TT + 3],
                                              func=AF.Copy)
                    ev[("ctx", tau)] = c_s.inc(scalar.nop())

                for c in range(NCH):
                    @block.tensor
                    def _(tensor, tau=tau, par=par, c=c):
                        W(tensor, "tensor", ev[("xzev", tau, c)],
                          ev[("ctx", tau)], rel.get(("ps_xz", c % 2)))
                        for k in range(DCONV):
                            inst = nc.tensor.matmul(
                                ps_conv[:, c % 2, :], cdiag_bf[:, c * 4 + k, :],
                                xin_bf[:, par, c, k:k + TT],
                                start=(k == 0), stop=(k == DCONV - 1))
                        tok = c_t.inc(inst)
                        ev[("conv", tau, c)] = tok
                        if c == NCH - 1:
                            rel[("xin", par)] = tok

                    @block.scalar
                    def _(scalar, tau=tau, par=par, c=c):
                        W(scalar, "scalar", ev[("conv", tau, c)],
                          rel.get(("xc", par)))
                        inst = scalar.activation(out=xc_bf[:, par, c, :],
                                                 in_=ps_conv[:, c % 2, :],
                                                 func=AF.Silu,
                                                 bias=convb_sb[:, c:c + 1])
                        tok = c_s.inc(inst)
                        ev[("xc", tau, c)] = tok
                        rel[("ps_xz", c % 2)] = tok

                @block.tensor
                def _(tensor, tau=tau, par=par):
                    W(tensor, "tensor", rel.get("ps_xp"),
                      rel.get(("ps_dt", 0)), rel.get(("ps_dt", 1)),
                      [ev[("xc", tau, c)] for c in range(NCH)])
                    for c in range(NCH):
                        inst = nc.tensor.matmul(
                            ps_sm[0:96, 0, :], wxp_bf[:, c, :],
                            xc_bf[:, par, c, :],
                            start=(c == 0), stop=(c == NCH - 1))
                    ev[("xp", tau)] = c_t.inc(inst)

                @block.scalar
                def _(scalar, tau=tau, par=par):
                    W(scalar, "scalar", ev[("xp", tau)],
                      rel.get(("xd_mine", par)))
                    tok = c_s.inc(scalar.activation(out=xd_mine[:, par, :],
                                                    in_=ps_sm[0:96, 0, :],
                                                    func=AF.Copy))
                    ev[("xdev", tau)] = tok
                    rel["ps_xp"] = tok

                @block.gpsimd
                def _(gpsimd, tau=tau, par=par, t0=t0, t1=t1):
                    W(gpsimd, "gpsimd", ev[("xdev", tau)])
                    tok = bdma(gpsimd, g_dma,
                               lambda k, par=par, t0=t0, t1=t1:
                               (xdbl_shr[k, :, t0:t1], xd_mine[:, par, :]))
                    gpsimd.wait_ge(g_dma.h, g_dma.n)
                    rel[("xd_mine", par)] = tok
                    bdma(gpsimd, g_dma,
                         lambda k, tau=tau:
                         (flg_xd[k, tau, :],
                          nonce_in[0:1, 2 * tau:2 * tau + 2]))
                    poll(gpsimd, lambda k, tau=tau: flg_xd[k, tau, :],
                         lambda k, tau=tau: 2 * tau)
                    W(gpsimd, "gpsimd", rel.get("xd_peer"))
                    ev[("xdr", tau)] = bdma(
                        gpsimd, g_dma,
                        lambda k, t0=t0, t1=t1:
                        (xd_peer[:, :], xdbl_shr[k, :, t0:t1]),
                        use_peer=True)

                @block.vector
                def _(vector, tau=tau, par=par):
                    W(vector, "vector", ev[("xdr", tau)], rel.get("xd_sum"))
                    tok = c_v.inc(vector.tensor_add(
                        xd_sum[:], xd_mine[:, par, :], xd_peer[:, :]))
                    ev[("xdsum", tau)] = tok
                    rel["xd_peer"] = tok

                @block.scalar
                def _(scalar, tau=tau):
                    W(scalar, "scalar", ev[("xdsum", tau)], rel.get("dtraw"))
                    scalar.activation(out=dtraw_bf[:], in_=xd_sum[0:64, :],
                                      func=AF.Copy)
                    tok = c_s.inc(scalar.activation(out=bc_bf[:],
                                                    in_=xd_sum[64:96, :],
                                                    func=AF.Copy))
                    ev[("bcc", tau)] = tok
                    rel["xd_sum"] = tok

                for which, dst in (("b", brep_bf), ("c", crep_bf)):
                    roff = 0 if which == "b" else 16
                    for sp_ in range(DS // 2):
                        @block.tensor
                        def _(tensor, tau=tau, sp_=sp_, which=which, roff=roff):
                            W(tensor, "tensor", ev[("bcc", tau)],
                              rel.get("ps_bc"),
                              rel.get("brepcrep") if sp_ == 0 and which == "b"
                              else None)
                            for jj in (0, 1):
                                row = roff + 2 * sp_ + jj
                                inst = nc.tensor.matmul(
                                    ps_bc[:, jj, :],
                                    sel_bf[:, row * 128:(row + 1) * 128],
                                    bc_bf[:, :], start=True, stop=True)
                            ev[("bcm", tau, sp_, which)] = c_t.inc(inst)

                        @block.scalar
                        def _(scalar, tau=tau, sp_=sp_, which=which, dst=dst):
                            W(scalar, "scalar", ev[("bcm", tau, sp_, which)])
                            tok = c_s.inc(scalar.activation(
                                out=dst[:, 2 * sp_:2 * sp_ + 2, :],
                                in_=ps_bc[:, :, :], func=AF.Copy))
                            for jj in (0, 1):
                                ev[("bcev", tau, 2 * sp_ + jj, which)] = tok
                            rel["ps_bc"] = tok

                def dt_chain(tau, par, c):
                    cp = c % 2

                    @block.tensor
                    def _(tensor, tau=tau, c=c, cp=cp):
                        W(tensor, "tensor", ev[("bcc", tau)],
                          rel.get(("ps_dt", cp)))
                        ev[("dtm", tau, c)] = c_t.inc(nc.tensor.matmul(
                            ps_sm[:, cp, :],
                            wdt_bf[:, c * 128:(c + 1) * 128],
                            dtraw_bf[:, :], start=True, stop=True))

                    @block.scalar
                    def _(scalar, tau=tau, c=c, cp=cp):
                        W(scalar, "scalar", ev[("dtm", tau, c)],
                          rel.get("t_sp"), rel.get(("dt", cp)))
                        scalar.activation(out=t_a[:], in_=ps_sm[:, cp, :],
                                          func=AF.Abs, bias=bdt_sb[:, c:c + 1])
                        scalar.activation(out=t_a[:], in_=t_a[:], func=AF.Exp,
                                          scale=-1.0)
                        scalar.activation(out=t_ln[:, :], in_=t_a[:],
                                          func=AF.Ln, bias=1.0)
                        inst = scalar.activation(out=t_re[:, :],
                                                 in_=ps_sm[:, cp, :],
                                                 func=AF.Relu,
                                                 bias=bdt_sb[:, c:c + 1])
                        tok = c_s.inc(inst)
                        ev[("sp", tau, c)] = tok
                        rel[("ps_dt", cp)] = tok

                    @block.vector
                    def _(vector, tau=tau, c=c, cp=cp):
                        W(vector, "vector", ev[("sp", tau, c)])
                        tok = c_v.inc(vector.tensor_add(
                            dt_f[:, cp, :], t_ln[:, :], t_re[:, :]))
                        ev[("dt", tau, c)] = tok
                        rel["t_sp"] = tok

                    @block.vector
                    def _(vector, tau=tau, par=par, c=c, cp=cp):
                        W(vector, "vector", ev[("xc", tau, c)])
                        ev[("dtx", tau, c)] = c_v.inc(vector.tensor_mul(
                            dtx_bf[:, cp, :], dt_f[:, cp, :],
                            xc_bf[:, par, c, :]))

                dt_chain(tau, par, 0)
                dt_chain(tau, par, 1)
                for c in range(NCH):
                    cp = c % 2
                    for g in range(NG):
                        p2 = (tau * NCH * NG + c * NG + g) % 2

                        @block.scalar
                        def _(scalar, tau=tau, c=c, g=g, p2=p2, cp=cp):
                            W(scalar, "scalar", ev[("dt", tau, c)],
                              rel.get(("da", p2)))
                            for j in range(GS):
                                s = g * GS + j
                                inst = scalar.activation(
                                    out=da_f[:, p2, j * T1 + 2:j * T1 + 2 + TT],
                                    in_=dt_f[:, cp, :], func=AF.Exp,
                                    scale=a_sb[:, c, s:s + 1])
                            ev[("da", tau, c, g)] = c_s.inc(inst)

                        if g == 0 and c + 2 < NCH:
                            dt_chain(tau, par, c + 2)

                        @block.vector
                        def _(vector, tau=tau, c=c, g=g, p2=p2, cp=cp):
                            W(vector, "vector", ev[("da", tau, c, g)],
                              ev[("dtx", tau, c)],
                              [ev[("bcev", tau, g * GS + j, w)]
                               for j in range(GS) for w in ("b", "c")],
                              rel.get(("yc", (c * NG + g) % 2)))
                            for j in range(GS):
                                s = g * GS + j
                                vector.tensor_mul(
                                    dbx_bf[:, j * T1 + 2:j * T1 + 2 + TT],
                                    dtx_bf[:, cp, :], brep_bf[:, s, :])
                                vector.tensor_copy(
                                    dbx_bf[:, j * T1:j * T1 + 1],
                                    carr[:, c, s:s + 1])
                            vector.tensor_tensor_scan(
                                out=h_bf[:], data0=da_f[:, p2, :],
                                data1=dbx_bf[:], initial=0.0,
                                op0=ALU.mult, op1=ALU.add)
                            vector.tensor_copy(
                                carr[:, c, g * GS:(g + 1) * GS].unsqueeze(2),
                                h_bf[:].rearrange("p (s t) -> p s t", s=GS)
                                [:, :, T1 - 1:T1])
                            for j in range(GS):
                                s = g * GS + j
                                inst = vector.tensor_mul(
                                    yc_bf[:, (c * NG + g) % 2, j, :],
                                    h_bf[:, j * T1 + 2:j * T1 + 2 + TT],
                                    crep_bf[:, s, :])
                            tok = c_v.inc(inst)
                            ev[("scan", tau, c, g)] = tok
                            rel[("da", p2)] = tok
                            if c == NCH - 1 and g == NG - 1:
                                rel["brepcrep"] = tok

                        @block.tensor
                        def _(tensor, tau=tau, c=c, g=g):
                            W(tensor, "tensor", ev[("scan", tau, c, g)],
                              rel.get(("ps_y", c % 2)) if g == 0 else None)
                            for j in range(GS):
                                inst = nc.tensor.matmul(
                                    ps_y[:, c % 2, :], ident_bf[:, :],
                                    yc_bf[:, (c * NG + g) % 2, j, :],
                                    start=(g == 0 and j == 0),
                                    stop=(g == NG - 1 and j == GS - 1))
                            tok = c_t.inc(inst)
                            ev[("yacc", tau, c, g)] = tok
                            rel[("yc", (c * NG + g) % 2)] = tok

                    @block.vector
                    def _(vector, tau=tau, par=par, c=c):
                        W(vector, "vector", ev[("yacc", tau, c, NG - 1)],
                          rel.get(("yfin", c % 2)))
                        vector.scalar_tensor_tensor(
                            out=epi_f[:], in0=xc_bf[:, par, c, :],
                            scalar=dp_sb[:, c:c + 1], in1=ps_y[:, c % 2, :],
                            op0=ALU.mult, op1=ALU.add)
                        tok = c_v.inc(vector.tensor_mul(
                            yfin_bf[:, c % 2, :], epi_f[:],
                            zs_bf[:, par, c, :]))
                        ev[("yfin", tau, c)] = tok
                        rel[("ps_y", c % 2)] = tok
                        if c == NCH - 1:
                            rel[("xc", par)] = tok
                            rel[("zs", par)] = tok

                    @block.gpsimd
                    def _(gpsimd, tau=tau, c=c, t0=t0, t1=t1):
                        W(gpsimd, "gpsimd", ev[("yfin", tau, c)])
                        tok = bdma(gpsimd, g_dma,
                                   lambda k, c=c, t0=t0, t1=t1:
                                   (y_shr[k, c, :, t0:t1],
                                    yfin_bf[:, c % 2, :]))
                        ev[("yw", tau, c)] = tok
                        rel[("yfin", c % 2)] = tok

                if tau == NT - 1:
                    @block.gpsimd
                    def _(gpsimd):
                        gpsimd.wait_ge(g_dma.h, g_dma.n)
                        bdma(gpsimd, g_dma,
                             lambda k: (flg_y[k, :],
                                        nonce_in[0:1, 32:34]))

                if BARRIER_TILES:
                    barrier_all(("mt", tau))

            m_end = {"scalar": c_s.now(), "vector": c_v.now(),
                     "tensor": c_t.now(), "gpsimd": (g_dma, g_dma.n),
                     "sync": s_dma.now()}

        def barrier(ends, tag):
            toks = {}
            for name in ("scalar", "vector", "tensor", "gpsimd", "sync"):
                @getattr(block, name)
                def _(eng, name=name):
                    for oname, tok in ends.items():
                        if oname != name:
                            eng.wait_ge(tok[0].h, tok[1])
                    if name == "scalar":
                        toks["t"] = c_s.inc(eng.nop())
            return toks["t"]

        bar_m = barrier(m_end, "m")

        with ExitStack() as st3:
            A3, P3 = mk(st3)
            wout_bf = A3("wout_bf", [128, 16, DM], BF16)
            wst3 = A3("wst3", [128, 2, DM], F32)
            y_in = A3("y_in", [128, 2, 16, TT], BF16)
            x3_st = A3("x3_st", [128, 2, NDM, TT], F32)
            v_f = A3("v_f", [128, NDM, TT], F32)
            v_bf = A3("v_bf", [128, NDM, TT], BF16)
            vsq_bf = A3("vsq_bf", [128, NDM, TT], BF16)
            var_f = A3("var_f", [128, TT], F32)
            r1_f = A3("r1_f", [128, TT], F32)
            r2_f = A3("r2_f", [128, TT], F32)
            r12_f = A3("r12_f", [128, TT], F32)
            tvmu = A3("tvmu", [128, NDM, TT], F32)
            x2_f = A3("x2_f", [128, NDM, TT], F32)
            x2n_bf = A3("x2n_bf", [128, NDM, TT], BF16)

            ps_o = P3("ps_o", [128, 2, TT])
            ps_st = P3("ps_st", [128, 2, TT])

            for part, src in (("lo", wout_lo), ("hi", wout_hi)):
                for k in range(NCH):
                    kk = (0 if part == "lo" else NCH) + k

                    @block.sync
                    def _(sync, src=src, k=k, kk=kk):
                        W(sync, "sync", bar_m, rel.get(("wst3", kk % 2)))
                        ev[("w3", kk)] = s_dma.inc(sync.dma_start(
                            out=wst3[:, kk % 2, :],
                            in_=src[k * 128:(k + 1) * 128, :]))

                    @block.scalar
                    def _(scalar, kk=kk):
                        W(scalar, "scalar", ev[("w3", kk)], bar_m)
                        rel[("wst3", kk % 2)] = c_s.inc(scalar.activation(
                            out=wout_bf[:, kk, :], in_=wst3[:, kk % 2, :],
                            func=AF.Copy))
            ev["w3d"] = c_s.now()

            @block.gpsimd
            def _(gpsimd):
                for oname, tok in m_end.items():
                    if oname != "gpsimd":
                        gpsimd.wait_ge(tok[0].h, tok[1])
                poll(gpsimd, lambda k: flg_y[k, :], lambda k: 32)
                ev["ypeer"] = c_g.inc(gpsimd.nop())

            for t3 in range(NT3):
                par = t3 % 2

                @block.gpsimd
                def _(gpsimd, t3=t3, par=par):
                    W(gpsimd, "gpsimd", ev["ypeer"], m_end["gpsimd"],
                      rel.get(("y_in", par)), rel.get(("x3", par)))
                    for hh in (0, 1):
                        with if_slot(gpsimd, hh):
                            a0 = hh * TOK + t3 * TT
                            gpsimd.dma_start(
                                out=y_in[:, par, :, :],
                                in_=y_shr[:, :, :, a0:a0 + TT].rearrange(
                                    "s c p t -> p (s c) t")
                            ).then_inc(g_dma.h, 16)
                            gpsimd.dma_start(
                                out=x3_st[:, par, :, :],
                                in_=xT[:, a0:a0 + TT].rearrange(
                                    "(c p) t -> p c t", p=128)
                            ).then_inc(g_dma.h, 16)
                        gpsimd.end_ifs()
                    g_dma.n += 32
                    ev[("y3", t3)] = g_dma.now()

                for i in range(NDM):
                    @block.tensor
                    def _(tensor, t3=t3, par=par, i=i):
                        W(tensor, "tensor", ev[("y3", t3)], ev["w3d"],
                          rel.get(("ps_o", i % 2)))
                        for k in range(16):
                            inst = nc.tensor.matmul(
                                ps_o[:, i % 2, :],
                                wout_bf[:, k, i * 128:(i + 1) * 128],
                                y_in[:, par, k, :],
                                start=(k == 0), stop=(k == 15))
                        ev[("om", t3, i)] = c_t.inc(inst)

                    @block.vector
                    def _(vector, t3=t3, par=par, i=i):
                        W(vector, "vector", ev[("om", t3, i)], ev[("y3", t3)],
                          ev.get(("x2n", t3 - 1)),
                          ev.get(("m3w", t3 - 1)))
                        tok = c_v.inc(vector.tensor_add(
                            v_f[:, i, :], x3_st[:, par, i, :],
                            ps_o[:, i % 2, :]))
                        ev[("v", t3, i)] = tok
                        rel[("ps_o", i % 2)] = tok
                        if i == NDM - 1:
                            rel[("x3", par)] = tok
                            rel[("y_in", par)] = tok

                @block.scalar
                def _(scalar, t3=t3):
                    W(scalar, "scalar", [ev[("v", t3, i)] for i in range(NDM)],
                      ev.get(("stat", t3 - 1)))
                    scalar.activation(out=v_bf[:], in_=v_f[:], func=AF.Copy)
                    ev[("vbf", t3)] = c_s.inc(scalar.activation(
                        out=vsq_bf[:], in_=v_f[:], func=AF.Square))

                @block.tensor
                def _(tensor, t3=t3):
                    W(tensor, "tensor", ev[("vbf", t3)], rel.get("ps_st"))
                    for k in range(NDM):
                        nc.tensor.matmul(ps_st[:, 0, :], onesh_bf[:, :],
                                         v_bf[:, k, :], start=(k == 0),
                                         stop=(k == NDM - 1))
                    for k in range(NDM):
                        inst = nc.tensor.matmul(ps_st[:, 1, :], onesh_bf[:, :],
                                                vsq_bf[:, k, :],
                                                start=(k == 0),
                                                stop=(k == NDM - 1))
                    ev[("stat", t3)] = c_t.inc(inst)

                @block.vector
                def _(vector, t3=t3):
                    W(vector, "vector", ev[("stat", t3)],
                      ev.get(("r1", t3 - 1)))
                    vector.tensor_copy(var_f[:], ps_st[:, 0, :])
                    vector.tensor_mul(var_f[:], var_f[:], ps_st[:, 0, :])
                    ev[("var", t3)] = c_v.inc(vector.tensor_sub(
                        var_f[:], ps_st[:, 1, :], var_f[:]))

                @block.scalar
                def _(scalar, t3=t3):
                    W(scalar, "scalar", ev[("var", t3)])
                    scalar.activation(out=r1_f[:], in_=var_f[:], func=AF.Ln,
                                      bias=EPS)
                    ev[("r1", t3)] = c_s.inc(scalar.activation(
                        out=r1_f[:], in_=r1_f[:], func=AF.Exp, scale=-0.5))

                @block.vector
                def _(vector, t3=t3):
                    W(vector, "vector", ev[("r1", t3)],
                      ev.get(("r2", t3 - 1)))
                    vector.tensor_mul(r2_f[:], r1_f[:], r1_f[:])
                    ev[("var2", t3)] = c_v.inc(vector.tensor_mul(
                        r2_f[:], r2_f[:], var_f[:]))

                @block.scalar
                def _(scalar, t3=t3):
                    W(scalar, "scalar", ev[("var2", t3)])
                    scalar.activation(out=r2_f[:], in_=r2_f[:], func=AF.Ln,
                                      bias=EPS)
                    ev[("r2", t3)] = c_s.inc(scalar.activation(
                        out=r2_f[:], in_=r2_f[:], func=AF.Exp, scale=-0.5))

                @block.vector
                def _(vector, t3=t3):
                    W(vector, "vector", ev[("r2", t3)],
                      ev.get(("x2n", t3 - 1)), ev.get(("m3w", t3 - 1)))
                    vector.tensor_mul(r12_f[:], r1_f[:], r2_f[:])
                    for i in range(NDM):
                        vector.scalar_tensor_tensor(
                            out=tvmu[:, i, :], in0=v_f[:, i, :], scalar=1.0,
                            in1=ps_st[:, 0, :], op0=ALU.mult, op1=ALU.subtract)
                        vector.tensor_mul(x2_f[:, i, :], tvmu[:, i, :], r1_f[:])
                        inst = vector.tensor_mul(tvmu[:, i, :], tvmu[:, i, :],
                                                 r12_f[:])
                    tok = c_v.inc(inst)
                    ev[("x2", t3)] = tok
                    rel["v_f"] = tok
                    rel["ps_st"] = tok

                @block.scalar
                def _(scalar, t3=t3):
                    W(scalar, "scalar", ev[("x2", t3)])
                    for i in range(NDM):
                        inst = scalar.activation(out=x2n_bf[:, i, :],
                                                 in_=tvmu[:, i, :],
                                                 func=AF.Identity,
                                                 scale=gam_sb[:, i:i + 1],
                                                 bias=bet_sb[:, i:i + 1])
                    ev[("x2n", t3)] = c_s.inc(inst)

                @block.sync
                def _(sync, t3=t3):
                    W(sync, "sync", ev[("x2n", t3)])
                    s_dma.inc(sync.dma_start(
                        out=x2_d[:, :, t3 * TT:(t3 + 1) * TT].transpose(
                            [1, 0, 2]),
                        in_=x2_f[:]))
                    ev[("m3w", t3)] = s_dma.inc(sync.dma_start(
                        out=x2n_d[:, :, t3 * TT:(t3 + 1) * TT].transpose(
                            [1, 0, 2]),
                        in_=x2n_bf[:]))

                if BARRIER_TILES:
                    barrier_all(("m3t", t3))

            m3_end = {"scalar": c_s.now(), "vector": c_v.now(),
                      "tensor": c_t.now(), "gpsimd": (g_dma, g_dma.n),
                      "sync": s_dma.now()}

        bar_3 = barrier(m3_end, "m3")

        with ExitStack() as stf:
            Af, Pf = mk(stf)
            w1_bf = Af("w1_bf", [128, NDM, FF], BF16)
            w2_bf = Af("w2_bf", [128, NFF, DM], BF16)

            with ExitStack() as stq:
                Aq, _ = mk(stq)
                qst = Aq("qst", [128, 2048], I32)
                qf = Aq("qf", [128, 2048], F32)
                scb = Aq("scb", [128, FF], F32)
                sel_f = Aq("sel_f", [32, 32 * 128], F32)
                sc_sb = Aq("sc_sb", [32, FF], F32)
                ps_sc = stq.enter_context(
                    nc.psum_tensor("ps_sc", [128, TT], F32))

                @block.sync
                def _(sync):
                    W(sync, "sync", bar_3)
                    s_dma.inc(sync.dma_start(out=sel_f[:], in_=selmat[:, :]))
                    ev["scl"] = s_dma.inc(sync.dma_start(
                        out=sc_sb[0:8, 0:FF], in_=s1[:, :]))

                for wi, (qsrc, nk, fdim, dst, nrows) in enumerate(
                        [(q1, NDM, FF, w1_bf, 8), (q2, NFF, DM, w2_bf, 32)]):
                    if wi == 1:
                        @block.sync
                        def _(sync):
                            W(sync, "sync", ev["w1done"])
                            ev["scl"] = s_dma.inc(sync.dma_start(
                                out=sc_sb[0:32, 0:DM], in_=s2[:, :]))
                    for g in range(nk):
                        for jj in range(fdim // TT):
                            @block.tensor
                            def _(tensor, g=g, jj=jj, nrows=nrows):
                                W(tensor, "tensor", ev["scl"], bar_3,
                                  rel.get("ps_sc"))
                                ev["scm"] = c_t.inc(nc.tensor.matmul(
                                    ps_sc[:, :],
                                    sel_f[0:nrows, g * 128:(g + 1) * 128],
                                    sc_sb[0:nrows, jj * TT:(jj + 1) * TT],
                                    start=True, stop=True))

                            @block.scalar
                            def _(scalar, jj=jj):
                                W(scalar, "scalar", ev["scm"], rel.get("scb"))
                                rel["ps_sc"] = c_s.inc(scalar.activation(
                                    out=scb[:, jj * TT:(jj + 1) * TT],
                                    in_=ps_sc[:, :], func=AF.Copy))
                        for fh in range((fdim + 2047) // 2048):
                            w0 = fh * 2048
                            w1w = min(fdim - w0, 2048)

                            @block.sync
                            def _(sync, qsrc=qsrc, g=g, w0=w0, w1w=w1w):
                                W(sync, "sync", bar_3, rel.get("qst"))
                                ev["ql"] = s_dma.inc(sync.dma_start(
                                    out=qst[:, 0:w1w],
                                    in_=qsrc[g * 128:(g + 1) * 128,
                                             w0:w0 + w1w]))

                            @block.scalar
                            def _(scalar, w1w=w1w):
                                W(scalar, "scalar", ev["ql"], rel.get("qf"))
                                ev["qd"] = c_s.inc(scalar.activation(
                                    out=qf[:, 0:w1w], in_=qst[:, 0:w1w],
                                    func=AF.Copy, bias=-128.0))

                            @block.vector
                            def _(vector, g=g, w0=w0, w1w=w1w, dst=dst):
                                W(vector, "vector", ev["qd"], rel.get("ps_sc"))
                                tok = c_v.inc(vector.tensor_mul(
                                    dst[:, g, w0:w0 + w1w], qf[:, 0:w1w],
                                    scb[:, w0:w0 + w1w]))
                                rel["qst"] = tok
                                rel["qf"] = tok
                                if w0 + w1w == fdim:
                                    rel["scb"] = tok
                    if wi == 0:
                        ev["w1done"] = c_v.now()
                ev["fw"] = c_v.now()

            act_bf = Af("act_bf", [128, 16, TT], BF16)
            z2a = Af("z2a", [128, NDM, TT], F32)
            x2r = Af("x2r", [128, NDM, TT], F32)
            x2nr = Af("x2nr", [128, NDM, TT], BF16)
            f_ta = Af("f_ta", [128, TT], F32)
            f_tb = Af("f_tb", [128, TT], F32)
            f_tc = Af("f_tc", [128, TT], F32)
            fvar = Af("fvar", [128, TT], F32)
            fr = Af("fr", [128, TT], F32)

            ps_f1 = Pf("ps_f1", [128, 4, TT])
            ps_f2 = Pf("ps_f2", [128, 2, TT])
            ps_fs = Pf("ps_fs", [128, 2, TT])

            for t3 in range(NT3):
                @block.sync
                def _(sync, t3=t3):
                    W(sync, "sync", ev[("m3w", t3)], rel.get("x2nr"))
                    s_dma.inc(sync.dma_start(
                        out=x2nr[:, :, :],
                        in_=x2n_d[:, :, t3 * TT:(t3 + 1) * TT].rearrange(
                            "c p t -> p c t")))
                    ev[("fx", t3)] = s_dma.inc(sync.dma_start(
                        out=x2r[:, :, :],
                        in_=x2_d[:, :, t3 * TT:(t3 + 1) * TT].rearrange(
                            "c p t -> p c t")))

                for halfF in range(2):
                    for fi in range(16):
                        f = halfF * 16 + fi

                        @block.tensor
                        def _(tensor, t3=t3, f=f, fi=fi):
                            W(tensor, "tensor", ev[("fx", t3)], ev["fw"],
                              rel.get(("ps_f1", fi % 4)))
                            for k in range(NDM):
                                inst = nc.tensor.matmul(
                                    ps_f1[:, fi % 4, :],
                                    w1_bf[:, k, f * 128:(f + 1) * 128],
                                    x2nr[:, k, :],
                                    start=(k == 0), stop=(k == NDM - 1))
                            ev[("f1", t3, f)] = c_t.inc(inst)

                        @block.scalar
                        def _(scalar, t3=t3, f=f, fi=fi):
                            W(scalar, "scalar", ev[("f1", t3, f)],
                              rel.get("f_t"))
                            inst = scalar.activation(out=f_tb[:],
                                                     in_=ps_f1[:, fi % 4, :],
                                                     func=AF.Sigmoid,
                                                     bias=b1_sb[:, f:f + 1])
                            ev[("fs", t3, f)] = c_s.inc(inst)

                        @block.vector
                        def _(vector, t3=t3, f=f, fi=fi):
                            W(vector, "vector", ev[("fs", t3, f)],
                              rel.get(("act", fi)))
                            vector.tensor_scalar_add(
                                f_tc[:], ps_f1[:, fi % 4, :],
                                b1_sb[:, f:f + 1])
                            tok = c_v.inc(vector.tensor_mul(
                                act_bf[:, fi, :], f_tc[:], f_tb[:]))
                            ev[("act", t3, f)] = tok
                            rel["f_t"] = tok
                            rel[("ps_f1", fi % 4)] = tok

                    for i in range(NDM):
                        @block.tensor
                        def _(tensor, t3=t3, halfF=halfF, i=i):
                            W(tensor, "tensor",
                              [ev[("act", t3, halfF * 16 + fk)]
                               for fk in range(16)],
                              rel.get(("ps_f2", i % 2)))
                            for fk in range(16):
                                inst = nc.tensor.matmul(
                                    ps_f2[:, i % 2, :],
                                    w2_bf[:, halfF * 16 + fk,
                                          i * 128:(i + 1) * 128],
                                    act_bf[:, fk, :],
                                    start=(fk == 0), stop=(fk == 15))
                            tok = c_t.inc(inst)
                            ev[("f2", t3, halfF, i)] = tok
                            if i == NDM - 1:
                                for fi2 in range(16):
                                    rel[("act", fi2)] = tok

                        if halfF == 0:
                            @block.scalar
                            def _(scalar, t3=t3, i=i):
                                W(scalar, "scalar", ev[("f2", t3, 0, i)],
                                  rel.get(("z2a", i)))
                                tok = c_s.inc(scalar.activation(
                                    out=z2a[:, i, :], in_=ps_f2[:, i % 2, :],
                                    func=AF.Identity, scale=0.5,
                                    bias=b2h_sb[:, i:i + 1]))
                                ev[("z2a", t3, i)] = tok
                                rel[("ps_f2", i % 2)] = tok
                        else:
                            @block.vector
                            def _(vector, t3=t3, i=i):
                                W(vector, "vector", ev[("f2", t3, 1, i)],
                                  ev[("z2a", t3, i)])
                                vector.scalar_tensor_tensor(
                                    out=z2a[:, i, :], in0=ps_f2[:, i % 2, :],
                                    scalar=0.5, in1=z2a[:, i, :],
                                    op0=ALU.mult, op1=ALU.add)
                                tok = c_v.inc(vector.scalar_tensor_tensor(
                                    out=z2a[:, i, :], in0=x2r[:, i, :],
                                    scalar=2.0, in1=z2a[:, i, :],
                                    op0=ALU.mult, op1=ALU.add))
                                ev[("u", t3, i)] = tok
                                rel[("ps_f2", i % 2)] = tok

                @block.scalar
                def _(scalar, t3=t3):
                    W(scalar, "scalar", [ev[("u", t3, i)] for i in range(NDM)],
                      ev.get(("fst", t3 - 1)))
                    scalar.activation(out=act_bf[:, 0:NDM, :],
                                      in_=z2a[:], func=AF.Copy)
                    ev[("ub", t3)] = c_s.inc(scalar.activation(
                        out=act_bf[:, NDM:2 * NDM, :],
                        in_=z2a[:], func=AF.Square))

                @block.tensor
                def _(tensor, t3=t3):
                    W(tensor, "tensor", ev[("ub", t3)], rel.get("ps_fs"))
                    for k in range(NDM):
                        nc.tensor.matmul(ps_fs[:, 0, :], onesh_bf[:, :],
                                         act_bf[:, k, :], start=(k == 0),
                                         stop=(k == NDM - 1))
                    for k in range(NDM):
                        inst = nc.tensor.matmul(ps_fs[:, 1, :], onesh_bf[:, :],
                                                act_bf[:, NDM + k, :],
                                                start=(k == 0),
                                                stop=(k == NDM - 1))
                    ev[("fst", t3)] = c_t.inc(inst)

                @block.vector
                def _(vector, t3=t3):
                    W(vector, "vector", ev[("fst", t3)],
                      ev.get(("fr", t3 - 1)))
                    vector.tensor_copy(fvar[:], ps_fs[:, 0, :])
                    vector.tensor_mul(fvar[:], fvar[:], ps_fs[:, 0, :])
                    ev[("fvar", t3)] = c_v.inc(vector.tensor_sub(
                        fvar[:], ps_fs[:, 1, :], fvar[:]))

                @block.scalar
                def _(scalar, t3=t3):
                    W(scalar, "scalar", ev[("fvar", t3)])
                    scalar.activation(out=fr[:], in_=fvar[:], func=AF.Ln,
                                      bias=EPS)
                    ev[("fr", t3)] = c_s.inc(scalar.activation(
                        out=fr[:], in_=fr[:], func=AF.Exp, scale=-0.5))

                @block.vector
                def _(vector, t3=t3):
                    W(vector, "vector", ev[("fr", t3)],
                      ev.get(("outw", t3 - 1)))
                    for i in range(NDM):
                        vector.scalar_tensor_tensor(
                            out=x2r[:, i, :], in0=z2a[:, i, :], scalar=1.0,
                            in1=ps_fs[:, 0, :], op0=ALU.mult, op1=ALU.subtract)
                        inst = vector.tensor_mul(x2r[:, i, :], x2r[:, i, :],
                                                 fr[:])
                    tok = c_v.inc(inst)
                    ev[("fout", t3)] = tok
                    rel["ps_fs"] = tok
                    for i in range(NDM):
                        rel[("z2a", i)] = tok

                @block.sync
                def _(sync, t3=t3):
                    W(sync, "sync", ev[("fout", t3)])
                    tok = s_dma.inc(sync.dma_start(
                        out=out_d[:, :, t3 * TT:(t3 + 1) * TT].transpose(
                            [1, 0, 2]),
                        in_=x2r[:]))
                    ev[("outw", t3)] = tok
                    rel["x2nr"] = tok

                if BARRIER_TILES:
                    barrier_all(("ft", t3))

            @block.sync
            def _(sync):
                sync.wait_ge(s_dma.h, s_dma.n)

            @block.gpsimd
            def _(gpsimd):
                gpsimd.wait_ge(s_dma.h, s_dma.n)

    return nc


_CACHE = {}


def kernel(**inputs):
    if "nc" not in _CACHE:
        _CACHE["nc"] = build_nc()
    nc = _CACHE["nc"]

    gi = lambda k: np.ascontiguousarray(inputs[k])
    x = gi("L0_attn_x")
    W_in = gi("L0_attn_W_in")
    conv_w = gi("L0_attn_conv_w")
    conv_b = gi("L0_attn_conv_b")
    W_xproj = gi("L0_attn_W_xproj")
    W_dt = gi("L0_attn_W_dt")
    b_dt = gi("L0_attn_b_dt")
    A_log = gi("L0_attn_A_log")
    Dp = gi("L0_attn_Dp")
    W_out = gi("L0_attn_W_out")
    ln_g = gi("L0_ffn_ln_gamma")
    ln_b = gi("L0_ffn_ln_beta")
    fc1_q = gi("L0_ffn_fc1_q")
    fc1_s = gi("L0_ffn_fc1_scales")
    fc1_b = gi("L0_ffn_fc1_bias")
    fc2_q = gi("L0_ffn_fc2_q")
    fc2_s = gi("L0_ffn_fc2_scales")
    fc2_b = gi("L0_ffn_fc2_bias")

    selm = np.zeros((32, 32 * 128), dtype=np.float32)
    for s in range(32):
        selm[s, s * 128:(s + 1) * 128] = 1.0
    identm = np.eye(128, dtype=np.float32)
    onesh = np.full((128, 128), 1.0 / DM, dtype=np.float32)
    da01 = np.tile(np.array([0.0, 1.0], np.float32), (128, 1))

    rng_n = np.random.default_rng()
    nonces = [rng_n.integers(1, 2**32 - 1, size=(1, 64), dtype=np.uint32)
              for _ in range(4)]
    in_maps = []
    for c in range(8):
        b, h = c // 2, c % 2
        ch0 = h * CH
        im = {
            "xT": np.ascontiguousarray(x[b].T),
            "winx": np.ascontiguousarray(W_in[:, ch0:ch0 + CH]),
            "winz": np.ascontiguousarray(W_in[:, 2048 + ch0:2048 + ch0 + CH]),
            "wxp": np.ascontiguousarray(W_xproj[ch0:ch0 + CH, :]),
            "wdt": np.ascontiguousarray(W_dt[:, ch0:ch0 + CH]),
            "convb": np.ascontiguousarray(
                conv_b[ch0:ch0 + CH].reshape(NCH, 128).T),
            "bdt": np.ascontiguousarray(b_dt[ch0:ch0 + CH].reshape(NCH, 128).T),
            "alog": np.ascontiguousarray(
                A_log[ch0:ch0 + CH].reshape(NCH, 128, DS).transpose(1, 0, 2)),
            "dp": np.ascontiguousarray(Dp[ch0:ch0 + CH].reshape(NCH, 128).T),
            "wout_lo": np.ascontiguousarray(W_out[0:CH, :]),
            "wout_hi": np.ascontiguousarray(W_out[CH:2 * CH, :]),
            "gam": np.ascontiguousarray(ln_g.reshape(NDM, 128).T),
            "bet": np.ascontiguousarray(ln_b.reshape(NDM, 128).T),
            "q1": fc1_q,
            "s1": fc1_s,
            "b1": np.ascontiguousarray(fc1_b.reshape(NFF, 128).T),
            "q2": fc2_q,
            "s2": fc2_s,
            "b2": np.ascontiguousarray(fc2_b.reshape(NDM, 128).T),
            "selmat": selm,
            "identm": identm,
            "onesh": onesh,
            "da01": da01,
            "nonce_in": nonces[b],
        }
        cd = np.zeros((32, 128, 128), dtype=np.float32)
        for cc in range(NCH):
            for k in range(DCONV):
                np.fill_diagonal(cd[cc * 4 + k],
                                 conv_w[ch0 + cc * 128:ch0 + (cc + 1) * 128, k])
        im["convdiag"] = cd
        in_maps.append(im)

    res = run_bass_kernel_spmd(nc, in_maps, core_ids=list(range(8)))

    full = np.empty((B, L, DM), dtype=np.float32)
    for c in range(8):
        b, h = c // 2, c % 2
        o = res.results[c]["out"]
        full[b, h * TOK:(h + 1) * TOK, :] = \
            o.transpose(2, 0, 1).reshape(TOK, DM)
    return full


# version 10
# speedup vs baseline: 1.2177x; 1.0221x over previous
import numpy as np
from contextlib import ExitStack

import concourse.bass as bass
import concourse.mybir as mybir
from concourse.bass_utils import run_bass_kernel_spmd

F32 = mybir.dt.float32
BF16 = mybir.dt.bfloat16
U32 = mybir.dt.uint32
I32 = mybir.dt.int32
AF = mybir.ActivationFunctionType
ALU = mybir.AluOpType
ENG = mybir.EngineType

B, L, DM = 4, 8192, 1024
DS, DCONV, DTR = 16, 4, 64
CH, NCH, NDM = 1024, 8, 8
TT = 512
NT = L // TT
TOK = L // 2
NT3 = TOK // TT
T1 = TT + 2
GS, NG = 4, 4
FF, NFF = 4096, 32
EPS = 1e-5


class Sem:
    def __init__(self, stack, nc, name, engine=None):
        self.h = stack.enter_context(nc.semaphore(name))
        self.n = 0
        self.engine = engine
        self.dk = 1 if engine else 16

    def inc(self, instr, k=None):
        k = self.dk if k is None else k
        instr.then_inc(self.h, k)
        self.n += k
        return (self, self.n)

    def now(self):
        return (self, self.n)


def _flat(evs):
    for e in evs:
        if e is None:
            continue
        if isinstance(e, list):
            yield from _flat(e)
        else:
            yield e


def build_nc():
    nc = bass.Bass("TRN2", target_bir_lowering=False)

    def din(name, shape, dt=F32):
        return nc.dram_tensor(name, shape, dt, kind="ExternalInput").ap()

    xT = din("xT", [DM, L])
    winx = din("winx", [DM, CH])
    winz = din("winz", [DM, CH])
    wxp = din("wxp", [CH, 96])
    wdt_in = din("wdt", [DTR, CH])
    convdiag = din("convdiag", [32, 128, 128])
    convb = din("convb", [128, NCH])
    bdt = din("bdt", [128, NCH])
    alog = din("alog", [128, NCH, DS])
    dp_in = din("dp", [128, NCH])
    wout_lo = din("wout_lo", [CH, DM])
    wout_hi = din("wout_hi", [CH, DM])
    gam = din("gam", [128, NDM])
    bet = din("bet", [128, NDM])
    q1 = din("q1", [DM, FF], I32)
    s1 = din("s1", [8, FF])
    b1 = din("b1", [128, NFF])
    q2 = din("q2", [FF, DM], I32)
    s2 = din("s2", [32, DM])
    b2 = din("b2", [128, NDM])
    selmat = din("selmat", [32, 32 * 128])
    identm = din("identm", [128, 128])
    onesh = din("onesh", [128, 128])
    da01_in = din("da01", [128, 2])
    nonce_in = din("nonce_in", [1, 64], U32)

    out_d = nc.dram_tensor("out", [NDM, 128, TOK], F32, kind="ExternalOutput").ap()
    x2_d = nc.dram_tensor("x2_d", [NDM, 128, TOK], F32, kind="Internal").ap()
    x2n_d = nc.dram_tensor("x2n_d", [NDM, 128, TOK], BF16, kind="Internal").ap()

    xdbl_shr = nc.dram_tensor("xdbl_shr", [2, 96, L], F32, kind="Internal",
                              addr_space="Shared").ap()
    y_shr = nc.dram_tensor("y_shr", [2, NCH, 128, L], BF16, kind="Internal",
                           addr_space="Shared").ap()
    flg_xd = nc.dram_tensor("flg_xd", [2, NT, 2], U32, kind="Internal",
                            addr_space="Shared").ap()
    flg_y = nc.dram_tensor("flg_y", [2, 2], U32, kind="Internal",
                           addr_space="Shared").ap()

    with ExitStack() as st0:
        def mk(stk):
            def A(name, shape, dt=F32):
                return stk.enter_context(nc.sbuf_tensor(name, shape, dt))

            def P(name, shape):
                return stk.enter_context(nc.psum_tensor(name, shape, F32))
            return A, P

        A0, _ = mk(st0)
        ident_bf = A0("ident_bf", [128, 128], BF16)
        onesh_bf = A0("onesh_bf", [128, 128], BF16)
        sel_bf = A0("sel_bf", [32, 32 * 128], BF16)
        nonce = A0("nonce", [1, 64], F32)
        flagw = A0("flagw", [1, 2], U32)
        gam_sb = A0("gam_sb", [128, NDM], F32)
        bet_sb = A0("bet_sb", [128, NDM], F32)
        b2h_sb = A0("b2h_sb", [128, NDM], F32)
        b1_sb = A0("b1_sb", [128, NFF], F32)
        nb1_sb = A0("nb1_sb", [128, NFF], F32)
        const_sb = A0("const_sb", [128, 2], F32)

        nc.const_aps.register(F32, EPS, const_sb[:, 0:1])
        nc.const_aps.register(F32, -128.0, const_sb[:, 1:2])

        block = st0.enter_context(nc.Block())

        s_dma = Sem(st0, nc, "s_dma")
        g_dma = Sem(st0, nc, "g_dma")
        s_poll = Sem(st0, nc, "s_poll")
        c_s = Sem(st0, nc, "c_s", "scalar")
        c_v = Sem(st0, nc, "c_v", "vector")
        c_t = Sem(st0, nc, "c_t", "tensor")
        c_g = Sem(st0, nc, "c_g", "gpsimd")

        def W(eng, engname, *evs):
            seen = {}
            for sem, val in _flat(evs):
                if val <= 0 or sem.engine == engname:
                    continue
                if seen.get(id(sem), -1) < val:
                    seen[id(sem)] = val
                    eng.wait_ge(sem.h, val)

        ev = {}
        rel = {}
        regs = {}

        @block.gpsimd
        def _(gpsimd):
            pid = nc.partition_id([ENG.Pool])
            r = gpsimd.alloc_register("slot")
            gpsimd.reg_alu(r, gpsimd.to_reg(pid), 1, ALU.bitwise_and)
            regs["slot"] = r
            pt = gpsimd.alloc_register("ptgt")
            gpsimd.reg_mov(pt, 0)
            regs["ptgt"] = pt

        def if_slot(gpsimd, k):
            return gpsimd.If_eq(regs["slot"], k)

        def bdma(gpsimd, sem, mk_pair, use_peer=False):
            for k in (0, 1):
                with if_slot(gpsimd, (1 - k) if use_peer else k):
                    o, i = mk_pair(k)
                    gpsimd.dma_start(out=o, in_=i).then_inc(sem.h, 16)
                gpsimd.end_ifs()
            sem.n += 16
            return (sem, sem.n)

        _pollid = [0]

        def poll(gpsimd, flag_ap_fn, word_fn):
            pt = regs["ptgt"]
            u = str(_pollid[0])
            _pollid[0] += 1
            with (
                gpsimd.register("nd" + u) as nd,
                gpsimd.register("f0" + u) as f0,
                gpsimd.register("f1" + u) as f1,
                gpsimd.register("e0" + u) as e0,
                gpsimd.register("e1" + u) as e1,
                gpsimd.register("t0" + u) as t0,
                gpsimd.register("t1" + u) as t1r,
            ):
                for k in (0, 1):
                    with if_slot(gpsimd, 1 - k):
                        w = word_fn(k)
                        gpsimd.reg_load(e0, nonce[0:1, w:w + 1].bitcast(U32))
                        gpsimd.reg_load(e1, nonce[0:1, w + 1:w + 2].bitcast(U32))
                    gpsimd.end_ifs()
                gpsimd.reg_mov(nd, 1)
                with gpsimd.While(nd):
                    gpsimd.reg_alu(pt, pt, 16, ALU.add)
                    for k in (0, 1):
                        with if_slot(gpsimd, 1 - k):
                            gpsimd.dma_start(out=flagw[0:1, :],
                                             in_=flag_ap_fn(k)
                                             ).then_inc(s_poll.h, 16)
                        gpsimd.end_ifs()
                    gpsimd.wait_ge(s_poll.h, pt)
                    gpsimd.reg_load(f0, flagw[0:1, 0:1])
                    gpsimd.reg_load(f1, flagw[0:1, 1:2])
                    gpsimd.reg_alu(t0, f0, e0, ALU.not_equal)
                    gpsimd.reg_alu(t1r, f1, e1, ALU.not_equal)
                    gpsimd.reg_alu(nd, t0, t1r, ALU.bitwise_or)

        BARRIER_TILES = False

        def barrier_all(tag):
            ends = {"scalar": c_s.now(), "vector": c_v.now(),
                    "tensor": c_t.now(), "gpsimd": (g_dma, g_dma.n),
                    "sync": s_dma.now()}
            marks = {}
            for name in ("scalar", "vector", "tensor", "gpsimd", "sync"):
                sem = {"scalar": c_s, "vector": c_v, "tensor": c_t,
                       "gpsimd": c_g, "sync": None}[name]

                @getattr(block, name)
                def _(eng, name=name, sem=sem):
                    for oname, tok in ends.items():
                        if oname != name:
                            eng.wait_ge(tok[0].h, tok[1])
                    if sem is not None:
                        marks[name] = sem.inc(eng.nop())
            for name in ("scalar", "vector", "tensor", "gpsimd", "sync"):
                @getattr(block, name)
                def _(eng, name=name):
                    for oname, tok in marks.items():
                        if oname != name:
                            eng.wait_ge(tok[0].h, tok[1])

        with ExitStack() as stm:
            Am, Pm = mk(stm)
            winx_bf = Am("winx_bf", [128, NDM, CH], BF16)
            winz_bf = Am("winz_bf", [128, NDM, CH], BF16)
            wxp_bf = Am("wxp_bf", [128, NCH, 96], BF16)
            wdt_bf = Am("wdt_bf", [DTR, CH], BF16)
            cdiag_bf = Am("cdiag_bf", [128, 32, 128], BF16)
            convb_sb = Am("convb_sb", [128, NCH], F32)
            bdt_sb = Am("bdt_sb", [128, NCH], F32)
            a_sb = Am("a_sb", [128, NCH, DS], F32)
            dp_sb = Am("dp_sb", [128, NCH], F32)
            da01_f = Am("da01_f", [128, 2], F32)

            with ExitStack() as stw:
                Aw, _ = mk(stw)
                wstage = Aw("wstage", [128, 2, 8192], F32)

                @block.sync
                def _(sync):
                    for dst, src in [
                        (convb_sb[:], convb[:, :]),
                        (bdt_sb[:], bdt[:, :]),
                        (a_sb[:], alog[:, :, :]),
                        (dp_sb[:], dp_in[:, :]),
                        (da01_f[:], da01_in[:, :]),
                        (nonce[:].bitcast(U32), nonce_in[:, :]),
                        (gam_sb[:], gam[:, :]),
                        (bet_sb[:], bet[:, :]),
                        (b2h_sb[:], b2[:, :]),
                        (b1_sb[:], b1[:, :]),
                        (wstage[0:32, 1, 0:32 * 128], selmat[:, :]),
                        (wstage[:, 0, 0:128], identm[:, :]),
                        (wstage[:, 0, 128:256], onesh[:, :]),
                        (wstage[0:64, 0, 256:256 + CH], wdt_in[:, :]),
                        (wstage[:, 0, 2048:2048 + 32 * 128].rearrange(
                            "p (a c) -> p a c", a=32),
                         convdiag[:, :, :].transpose([1, 0, 2])),
                        (wstage[:, 0, 6144:6144 + NCH * 96].rearrange(
                            "p (c n) -> p c n", c=NCH),
                         wxp[:, :].rearrange("(c p) n -> p c n", p=128)),
                    ]:
                        sync.dma_start(out=dst, in_=src).then_inc(s_dma.h, 16)
                        s_dma.n += 16
                    ev["ld0"] = s_dma.now()

                @block.scalar
                def _(scalar):
                    W(scalar, "scalar", ev["ld0"])
                    scalar.activation(out=ident_bf[:], in_=wstage[:, 0, 0:128],
                                      func=AF.Copy)
                    scalar.activation(out=onesh_bf[:], in_=wstage[:, 0, 128:256],
                                      func=AF.Copy)
                    scalar.activation(out=sel_bf[:],
                                      in_=wstage[0:32, 1, 0:32 * 128],
                                      func=AF.Copy)
                    scalar.activation(out=wdt_bf[:],
                                      in_=wstage[0:64, 0, 256:256 + CH],
                                      func=AF.Copy)
                    scalar.activation(out=cdiag_bf[:].rearrange("p a c -> p (a c)"),
                                      in_=wstage[:, 0, 2048:2048 + 32 * 128],
                                      func=AF.Copy)
                    scalar.activation(out=wxp_bf[:].rearrange("p c n -> p (c n)"),
                                      in_=wstage[:, 0, 6144:6144 + NCH * 96],
                                      func=AF.Copy)
                    scalar.activation(out=a_sb[:], in_=a_sb[:], func=AF.Exp)
                    scalar.activation(out=nb1_sb[:], in_=b1_sb[:], func=AF.Copy,
                                      scale=-1.0)
                    ev["sc0"] = c_s.inc(scalar.activation(
                        out=b2h_sb[:], in_=b2h_sb[:], func=AF.Copy, scale=0.5))
                    rel[("wst", 0)] = ev["sc0"]
                    rel[("wst", 1)] = ev["sc0"]

                for kind, src, dst in (("x", winx, winx_bf), ("z", winz, winz_bf)):
                    for k in range(NDM):
                        @block.sync
                        def _(sync, kind=kind, src=src, k=k):
                            W(sync, "sync", rel.get(("wst", k % 2)))
                            ev[("wl", kind, k)] = s_dma.inc(sync.dma_start(
                                out=wstage[:, k % 2, 0:CH],
                                in_=src[k * 128:(k + 1) * 128, :]))

                        @block.scalar
                        def _(scalar, kind=kind, k=k, dst=dst):
                            W(scalar, "scalar", ev[("wl", kind, k)])
                            rel[("wst", k % 2)] = c_s.inc(scalar.activation(
                                out=dst[:, k, :], in_=wstage[:, k % 2, 0:CH],
                                func=AF.Copy))
                ev["wm"] = c_s.now()
                rel[("x_st", 0)] = ev["wm"]
                rel[("x_st", 1)] = ev["wm"]

                @block.vector
                def _(vector):
                    W(vector, "vector", ev["sc0"])
                    vector.tensor_scalar_mul(a_sb[:], a_sb[:], -1.0)
                    vector.memset(const_sb[:, 0:1], EPS)
                    vector.memset(const_sb[:, 1:2], -128.0)
                    ev["v0"] = c_v.inc(vector.nop())

            x_st = Am("x_st", [128, 2, 1, TT], F32)
            xbf = Am("xbf", [128, NDM, TT], BF16)
            xin_bf = Am("xin_bf", [128, 2, NCH, TT + 3], BF16)
            zs_bf = Am("zs_bf", [128, 2, NCH, TT], BF16)
            xc_bf = Am("xc_bf", [128, 2, NCH, TT], BF16)
            dt_f = Am("dt_f", [128, 2, TT], F32)
            dtx_bf = Am("dtx_bf", [128, 2, TT], BF16)
            xd_mine = Am("xd_mine", [96, 2, TT], F32)
            xd_peer = Am("xd_peer", [96, TT], F32)
            xd_sum = Am("xd_sum", [96, TT], F32)
            dtraw_bf = Am("dtraw_bf", [DTR, TT], BF16)
            bc_bf = Am("bc_bf", [32, TT], BF16)
            brep_bf = Am("brep_bf", [128, DS, TT], BF16)
            crep_bf = Am("crep_bf", [128, DS, TT], BF16)
            da_f = Am("da_f", [128, 2, GS * T1], F32)
            dbx_bf = Am("dbx_bf", [128, GS * T1], BF16)
            h_bf = Am("h_bf", [128, GS * T1], BF16)
            yc_bf = Am("yc_bf", [128, 2, GS, TT], BF16)
            carr = Am("carr", [128, NCH, DS], F32)
            t_a = Am("t_a", [128, TT], F32)
            t_ln = Am("t_ln", [128, TT], F32)
            t_re = Am("t_re", [128, TT], F32)
            epi_f = Am("epi_f", [128, TT], F32)
            yfin_bf = Am("yfin_bf", [128, 2, TT], BF16)

            ps_xz = Pm("ps_xz", [128, 2, TT])
            ps_conv = ps_xz
            ps_sm = Pm("ps_sm", [128, 2, TT])
            ps_bc = Pm("ps_bc", [128, 2, TT])
            ps_y = Pm("ps_y", [128, 2, TT])

            @block.vector
            def _(vector):
                W(vector, "vector", ev["v0"], ev["wm"])
                vector.memset(carr[:], 0.0)
                vector.memset(dbx_bf[:], 0.0)
                vector.memset(xin_bf[:, :, :, 0:3], 0.0)
                for p2 in range(2):
                    for s in range(GS):
                        vector.tensor_copy(da_f[:, p2, s * T1:s * T1 + 2],
                                           da01_f[:])
                ev["v1"] = c_v.inc(vector.nop())

            for tau in range(NT):
                par = tau % 2
                t0, t1 = tau * TT, (tau + 1) * TT

                @block.gpsimd
                def _(gpsimd, tau=tau, t0=t0):
                    W(gpsimd, "gpsimd", rel.get("xbf"),
                      [ev["wm"], ev["v1"]] if tau == 0 else None)
                    ev[("xbf", tau)] = g_dma.inc(gpsimd.dma_start(
                        out=xbf[:, :, :],
                        in_=xT[:, t0:t0 + TT].rearrange(
                            "(c p) t -> p c t", p=128)))

                for i in range(16):
                    @block.tensor
                    def _(tensor, tau=tau, i=i):
                        W(tensor, "tensor", ev[("xbf", tau)],
                          rel.get(("ps_xz", i % 2)))
                        wsrc = winx_bf if i < NCH else winz_bf
                        io = i % NCH
                        for k in range(NDM):
                            inst = nc.tensor.matmul(
                                ps_xz[:, i % 2, :],
                                wsrc[:, k, io * 128:(io + 1) * 128],
                                xbf[:, k, :], start=(k == 0), stop=(k == NDM - 1))
                        tok = c_t.inc(inst)
                        ev[("xz", tau, i)] = tok
                        if i == 15:
                            rel["xbf"] = tok

                    @block.scalar
                    def _(scalar, tau=tau, par=par, i=i):
                        W(scalar, "scalar", ev[("xz", tau, i)],
                          rel.get(("xin", par)) if i < NCH else
                          rel.get(("zs", par)))
                        if i < NCH:
                            inst = scalar.activation(
                                out=xin_bf[:, par, i, 3:TT + 3],
                                in_=ps_xz[:, i % 2, :], func=AF.Copy)
                        else:
                            inst = scalar.activation(
                                out=zs_bf[:, par, i - NCH, :],
                                in_=ps_xz[:, i % 2, :], func=AF.Silu)
                        tok = c_s.inc(inst)
                        ev[("xzev", tau, i)] = tok
                        rel[("ps_xz", i % 2)] = tok

                @block.scalar
                def _(scalar, tau=tau, par=par):
                    if tau >= 1:
                        W(scalar, "scalar", ev[("conv", tau - 1, NCH - 1)])
                        for c in range(NCH):
                            scalar.activation(out=xin_bf[:, par, c, 0:3],
                                              in_=xin_bf[:, 1 - par, c, TT:TT + 3],
                                              func=AF.Copy)
                    ev[("ctx", tau)] = c_s.inc(scalar.nop())

                for c in range(NCH):
                    @block.tensor
                    def _(tensor, tau=tau, par=par, c=c):
                        W(tensor, "tensor", ev[("xzev", tau, c)],
                          ev[("ctx", tau)], rel.get(("ps_xz", c % 2)))
                        for k in range(DCONV):
                            inst = nc.tensor.matmul(
                                ps_conv[:, c % 2, :], cdiag_bf[:, c * 4 + k, :],
                                xin_bf[:, par, c, k:k + TT],
                                start=(k == 0), stop=(k == DCONV - 1))
                        tok = c_t.inc(inst)
                        ev[("conv", tau, c)] = tok
                        if c == NCH - 1:
                            rel[("xin", par)] = tok

                    @block.scalar
                    def _(scalar, tau=tau, par=par, c=c):
                        W(scalar, "scalar", ev[("conv", tau, c)],
                          rel.get(("xc", par)))
                        inst = scalar.activation(out=xc_bf[:, par, c, :],
                                                 in_=ps_conv[:, c % 2, :],
                                                 func=AF.Silu,
                                                 bias=convb_sb[:, c:c + 1])
                        tok = c_s.inc(inst)
                        ev[("xc", tau, c)] = tok
                        rel[("ps_xz", c % 2)] = tok

                @block.tensor
                def _(tensor, tau=tau, par=par):
                    W(tensor, "tensor", rel.get("ps_xp"),
                      rel.get(("ps_dt", 0)), rel.get(("ps_dt", 1)),
                      [ev[("xc", tau, c)] for c in range(NCH)])
                    for c in range(NCH):
                        inst = nc.tensor.matmul(
                            ps_sm[0:96, 0, :], wxp_bf[:, c, :],
                            xc_bf[:, par, c, :],
                            start=(c == 0), stop=(c == NCH - 1))
                    ev[("xp", tau)] = c_t.inc(inst)

                @block.scalar
                def _(scalar, tau=tau, par=par):
                    W(scalar, "scalar", ev[("xp", tau)],
                      rel.get(("xd_mine", par)))
                    tok = c_s.inc(scalar.activation(out=xd_mine[:, par, :],
                                                    in_=ps_sm[0:96, 0, :],
                                                    func=AF.Copy))
                    ev[("xdev", tau)] = tok
                    rel["ps_xp"] = tok

                @block.gpsimd
                def _(gpsimd, tau=tau, par=par, t0=t0, t1=t1):
                    W(gpsimd, "gpsimd", ev[("xdev", tau)])
                    tok = bdma(gpsimd, g_dma,
                               lambda k, par=par, t0=t0, t1=t1:
                               (xdbl_shr[k, :, t0:t1], xd_mine[:, par, :]))
                    gpsimd.wait_ge(g_dma.h, g_dma.n)
                    rel[("xd_mine", par)] = tok
                    bdma(gpsimd, g_dma,
                         lambda k, tau=tau:
                         (flg_xd[k, tau, :],
                          nonce_in[0:1, 2 * tau:2 * tau + 2]))
                    poll(gpsimd, lambda k, tau=tau: flg_xd[k, tau, :],
                         lambda k, tau=tau: 2 * tau)
                    W(gpsimd, "gpsimd", rel.get("xd_peer"))
                    ev[("xdr", tau)] = bdma(
                        gpsimd, g_dma,
                        lambda k, t0=t0, t1=t1:
                        (xd_peer[:, :], xdbl_shr[k, :, t0:t1]),
                        use_peer=True)

                @block.vector
                def _(vector, tau=tau, par=par):
                    W(vector, "vector", ev[("xdr", tau)], rel.get("xd_sum"))
                    tok = c_v.inc(vector.tensor_add(
                        xd_sum[:], xd_mine[:, par, :], xd_peer[:, :]))
                    ev[("xdsum", tau)] = tok
                    rel["xd_peer"] = tok

                @block.scalar
                def _(scalar, tau=tau):
                    W(scalar, "scalar", ev[("xdsum", tau)], rel.get("dtraw"))
                    scalar.activation(out=dtraw_bf[:], in_=xd_sum[0:64, :],
                                      func=AF.Copy)
                    tok = c_s.inc(scalar.activation(out=bc_bf[:],
                                                    in_=xd_sum[64:96, :],
                                                    func=AF.Copy))
                    ev[("bcc", tau)] = tok
                    rel["xd_sum"] = tok

                for which, dst in (("b", brep_bf), ("c", crep_bf)):
                    roff = 0 if which == "b" else 16
                    for sp_ in range(DS // 2):
                        @block.tensor
                        def _(tensor, tau=tau, sp_=sp_, which=which, roff=roff):
                            W(tensor, "tensor", ev[("bcc", tau)],
                              rel.get("ps_bc"),
                              rel.get("brepcrep") if sp_ == 0 and which == "b"
                              else None)
                            for jj in (0, 1):
                                row = roff + 2 * sp_ + jj
                                inst = nc.tensor.matmul(
                                    ps_bc[:, jj, :],
                                    sel_bf[:, row * 128:(row + 1) * 128],
                                    bc_bf[:, :], start=True, stop=True)
                            ev[("bcm", tau, sp_, which)] = c_t.inc(inst)

                        @block.scalar
                        def _(scalar, tau=tau, sp_=sp_, which=which, dst=dst):
                            W(scalar, "scalar", ev[("bcm", tau, sp_, which)])
                            tok = c_s.inc(scalar.activation(
                                out=dst[:, 2 * sp_:2 * sp_ + 2, :],
                                in_=ps_bc[:, :, :], func=AF.Copy))
                            for jj in (0, 1):
                                ev[("bcev", tau, 2 * sp_ + jj, which)] = tok
                            rel["ps_bc"] = tok

                def dt_chain(tau, par, c):
                    cp = c % 2

                    @block.tensor
                    def _(tensor, tau=tau, c=c, cp=cp):
                        W(tensor, "tensor", ev[("bcc", tau)],
                          rel.get(("ps_dt", cp)))
                        ev[("dtm", tau, c)] = c_t.inc(nc.tensor.matmul(
                            ps_sm[:, cp, :],
                            wdt_bf[:, c * 128:(c + 1) * 128],
                            dtraw_bf[:, :], start=True, stop=True))

                    @block.scalar
                    def _(scalar, tau=tau, c=c, cp=cp):
                        W(scalar, "scalar", ev[("dtm", tau, c)],
                          rel.get("t_sp"), rel.get(("dt", cp)))
                        scalar.activation(out=t_a[:], in_=ps_sm[:, cp, :],
                                          func=AF.Abs, bias=bdt_sb[:, c:c + 1])
                        scalar.activation(out=t_a[:], in_=t_a[:], func=AF.Exp,
                                          scale=-1.0)
                        scalar.activation(out=t_ln[:, :], in_=t_a[:],
                                          func=AF.Ln, bias=1.0)
                        inst = scalar.activation(out=t_re[:, :],
                                                 in_=ps_sm[:, cp, :],
                                                 func=AF.Relu,
                                                 bias=bdt_sb[:, c:c + 1])
                        tok = c_s.inc(inst)
                        ev[("sp", tau, c)] = tok
                        rel[("ps_dt", cp)] = tok

                    @block.vector
                    def _(vector, tau=tau, c=c, cp=cp):
                        W(vector, "vector", ev[("sp", tau, c)])
                        tok = c_v.inc(vector.tensor_add(
                            dt_f[:, cp, :], t_ln[:, :], t_re[:, :]))
                        ev[("dt", tau, c)] = tok
                        rel["t_sp"] = tok

                    @block.vector
                    def _(vector, tau=tau, par=par, c=c, cp=cp):
                        W(vector, "vector", ev[("xc", tau, c)])
                        ev[("dtx", tau, c)] = c_v.inc(vector.tensor_mul(
                            dtx_bf[:, cp, :], dt_f[:, cp, :],
                            xc_bf[:, par, c, :]))

                dt_chain(tau, par, 0)
                dt_chain(tau, par, 1)
                for c in range(NCH):
                    cp = c % 2
                    for g in range(NG):
                        p2 = (tau * NCH * NG + c * NG + g) % 2

                        @block.scalar
                        def _(scalar, tau=tau, c=c, g=g, p2=p2, cp=cp):
                            W(scalar, "scalar", ev[("dt", tau, c)],
                              rel.get(("da", p2)))
                            for j in range(GS):
                                s = g * GS + j
                                inst = scalar.activation(
                                    out=da_f[:, p2, j * T1 + 2:j * T1 + 2 + TT],
                                    in_=dt_f[:, cp, :], func=AF.Exp,
                                    scale=a_sb[:, c, s:s + 1])
                            ev[("da", tau, c, g)] = c_s.inc(inst)

                        if g == 0 and c + 2 < NCH:
                            dt_chain(tau, par, c + 2)

                        @block.vector
                        def _(vector, tau=tau, c=c, g=g, p2=p2, cp=cp):
                            W(vector, "vector", ev[("da", tau, c, g)],
                              ev[("dtx", tau, c)],
                              [ev[("bcev", tau, g * GS + j, w)]
                               for j in range(GS) for w in ("b", "c")],
                              rel.get(("yc", (c * NG + g) % 2)))
                            for j in range(GS):
                                s = g * GS + j
                                vector.tensor_mul(
                                    dbx_bf[:, j * T1 + 2:j * T1 + 2 + TT],
                                    dtx_bf[:, cp, :], brep_bf[:, s, :])
                                vector.tensor_copy(
                                    dbx_bf[:, j * T1:j * T1 + 1],
                                    carr[:, c, s:s + 1])
                            vector.tensor_tensor_scan(
                                out=h_bf[:], data0=da_f[:, p2, :],
                                data1=dbx_bf[:], initial=0.0,
                                op0=ALU.mult, op1=ALU.add)
                            vector.tensor_copy(
                                carr[:, c, g * GS:(g + 1) * GS].unsqueeze(2),
                                h_bf[:].rearrange("p (s t) -> p s t", s=GS)
                                [:, :, T1 - 1:T1])
                            inst = vector.tensor_mul(
                                yc_bf[:, (c * NG + g) % 2, :, :],
                                h_bf[:].rearrange("p (s t) -> p s t", s=GS)
                                [:, :, 2:2 + TT],
                                crep_bf[:, g * GS:(g + 1) * GS, :])
                            tok = c_v.inc(inst)
                            ev[("scan", tau, c, g)] = tok
                            rel[("da", p2)] = tok
                            if c == NCH - 1 and g == NG - 1:
                                rel["brepcrep"] = tok

                        @block.tensor
                        def _(tensor, tau=tau, c=c, g=g):
                            W(tensor, "tensor", ev[("scan", tau, c, g)],
                              rel.get(("ps_y", c % 2)) if g == 0 else None)
                            for j in range(GS):
                                inst = nc.tensor.matmul(
                                    ps_y[:, c % 2, :], ident_bf[:, :],
                                    yc_bf[:, (c * NG + g) % 2, j, :],
                                    start=(g == 0 and j == 0),
                                    stop=(g == NG - 1 and j == GS - 1))
                            tok = c_t.inc(inst)
                            ev[("yacc", tau, c, g)] = tok
                            rel[("yc", (c * NG + g) % 2)] = tok

                    @block.vector
                    def _(vector, tau=tau, par=par, c=c):
                        W(vector, "vector", ev[("yacc", tau, c, NG - 1)],
                          rel.get(("yfin", c % 2)))
                        vector.scalar_tensor_tensor(
                            out=epi_f[:], in0=xc_bf[:, par, c, :],
                            scalar=dp_sb[:, c:c + 1], in1=ps_y[:, c % 2, :],
                            op0=ALU.mult, op1=ALU.add)
                        tok = c_v.inc(vector.tensor_mul(
                            yfin_bf[:, c % 2, :], epi_f[:],
                            zs_bf[:, par, c, :]))
                        ev[("yfin", tau, c)] = tok
                        rel[("ps_y", c % 2)] = tok
                        if c == NCH - 1:
                            rel[("xc", par)] = tok
                            rel[("zs", par)] = tok

                    @block.gpsimd
                    def _(gpsimd, tau=tau, c=c, t0=t0, t1=t1):
                        W(gpsimd, "gpsimd", ev[("yfin", tau, c)])
                        tok = bdma(gpsimd, g_dma,
                                   lambda k, c=c, t0=t0, t1=t1:
                                   (y_shr[k, c, :, t0:t1],
                                    yfin_bf[:, c % 2, :]))
                        ev[("yw", tau, c)] = tok
                        rel[("yfin", c % 2)] = tok

                if tau == NT - 1:
                    @block.gpsimd
                    def _(gpsimd):
                        gpsimd.wait_ge(g_dma.h, g_dma.n)
                        bdma(gpsimd, g_dma,
                             lambda k: (flg_y[k, :],
                                        nonce_in[0:1, 32:34]))

                if BARRIER_TILES:
                    barrier_all(("mt", tau))

            m_end = {"scalar": c_s.now(), "vector": c_v.now(),
                     "tensor": c_t.now(), "gpsimd": (g_dma, g_dma.n),
                     "sync": s_dma.now()}

        def barrier(ends, tag):
            toks = {}
            for name in ("scalar", "vector", "tensor", "gpsimd", "sync"):
                @getattr(block, name)
                def _(eng, name=name):
                    for oname, tok in ends.items():
                        if oname != name:
                            eng.wait_ge(tok[0].h, tok[1])
                    if name == "scalar":
                        toks["t"] = c_s.inc(eng.nop())
            return toks["t"]

        bar_m = barrier(m_end, "m")

        with ExitStack() as st3:
            A3, P3 = mk(st3)
            wout_bf = A3("wout_bf", [128, 16, DM], BF16)
            wst3 = A3("wst3", [128, 2, DM], F32)
            y_in = A3("y_in", [128, 2, 16, TT], BF16)
            x3_st = A3("x3_st", [128, 2, NDM, TT], F32)
            v_f = A3("v_f", [128, NDM, TT], F32)
            v_bf = A3("v_bf", [128, NDM, TT], BF16)
            vsq_bf = A3("vsq_bf", [128, NDM, TT], BF16)
            var_f = A3("var_f", [128, TT], F32)
            r1_f = A3("r1_f", [128, TT], F32)
            r2_f = A3("r2_f", [128, TT], F32)
            r12_f = A3("r12_f", [128, TT], F32)
            tvmu = A3("tvmu", [128, NDM, TT], F32)
            x2_f = A3("x2_f", [128, NDM, TT], F32)
            x2n_bf = A3("x2n_bf", [128, NDM, TT], BF16)

            ps_o = P3("ps_o", [128, 2, TT])
            ps_st = P3("ps_st", [128, 2, TT])

            for part, src in (("lo", wout_lo), ("hi", wout_hi)):
                for k in range(NCH):
                    kk = (0 if part == "lo" else NCH) + k

                    @block.sync
                    def _(sync, src=src, k=k, kk=kk):
                        W(sync, "sync", bar_m, rel.get(("wst3", kk % 2)))
                        ev[("w3", kk)] = s_dma.inc(sync.dma_start(
                            out=wst3[:, kk % 2, :],
                            in_=src[k * 128:(k + 1) * 128, :]))

                    @block.scalar
                    def _(scalar, kk=kk):
                        W(scalar, "scalar", ev[("w3", kk)], bar_m)
                        rel[("wst3", kk % 2)] = c_s.inc(scalar.activation(
                            out=wout_bf[:, kk, :], in_=wst3[:, kk % 2, :],
                            func=AF.Copy))
            ev["w3d"] = c_s.now()

            @block.gpsimd
            def _(gpsimd):
                for oname, tok in m_end.items():
                    if oname != "gpsimd":
                        gpsimd.wait_ge(tok[0].h, tok[1])
                poll(gpsimd, lambda k: flg_y[k, :], lambda k: 32)
                ev["ypeer"] = c_g.inc(gpsimd.nop())

            for t3 in range(NT3):
                par = t3 % 2

                @block.gpsimd
                def _(gpsimd, t3=t3, par=par):
                    W(gpsimd, "gpsimd", ev["ypeer"], m_end["gpsimd"],
                      rel.get(("y_in", par)), rel.get(("x3", par)))
                    for hh in (0, 1):
                        with if_slot(gpsimd, hh):
                            a0 = hh * TOK + t3 * TT
                            gpsimd.dma_start(
                                out=y_in[:, par, :, :],
                                in_=y_shr[:, :, :, a0:a0 + TT].rearrange(
                                    "s c p t -> p (s c) t")
                            ).then_inc(g_dma.h, 16)
                            gpsimd.dma_start(
                                out=x3_st[:, par, :, :],
                                in_=xT[:, a0:a0 + TT].rearrange(
                                    "(c p) t -> p c t", p=128)
                            ).then_inc(g_dma.h, 16)
                        gpsimd.end_ifs()
                    g_dma.n += 32
                    ev[("y3", t3)] = g_dma.now()

                for i in range(NDM):
                    @block.tensor
                    def _(tensor, t3=t3, par=par, i=i):
                        W(tensor, "tensor", ev[("y3", t3)], ev["w3d"],
                          rel.get(("ps_o", i % 2)))
                        for k in range(16):
                            inst = nc.tensor.matmul(
                                ps_o[:, i % 2, :],
                                wout_bf[:, k, i * 128:(i + 1) * 128],
                                y_in[:, par, k, :],
                                start=(k == 0), stop=(k == 15))
                        ev[("om", t3, i)] = c_t.inc(inst)

                    @block.vector
                    def _(vector, t3=t3, par=par, i=i):
                        W(vector, "vector", ev[("om", t3, i)], ev[("y3", t3)],
                          ev.get(("x2n", t3 - 1)),
                          ev.get(("m3w", t3 - 1)))
                        tok = c_v.inc(vector.tensor_add(
                            v_f[:, i, :], x3_st[:, par, i, :],
                            ps_o[:, i % 2, :]))
                        ev[("v", t3, i)] = tok
                        rel[("ps_o", i % 2)] = tok
                        if i == NDM - 1:
                            rel[("x3", par)] = tok
                            rel[("y_in", par)] = tok

                @block.scalar
                def _(scalar, t3=t3):
                    W(scalar, "scalar", [ev[("v", t3, i)] for i in range(NDM)],
                      ev.get(("stat", t3 - 1)))
                    scalar.activation(out=v_bf[:], in_=v_f[:], func=AF.Copy)
                    ev[("vbf", t3)] = c_s.inc(scalar.activation(
                        out=vsq_bf[:], in_=v_f[:], func=AF.Square))

                @block.tensor
                def _(tensor, t3=t3):
                    W(tensor, "tensor", ev[("vbf", t3)], rel.get("ps_st"))
                    for k in range(NDM):
                        nc.tensor.matmul(ps_st[:, 0, :], onesh_bf[:, :],
                                         v_bf[:, k, :], start=(k == 0),
                                         stop=(k == NDM - 1))
                    for k in range(NDM):
                        inst = nc.tensor.matmul(ps_st[:, 1, :], onesh_bf[:, :],
                                                vsq_bf[:, k, :],
                                                start=(k == 0),
                                                stop=(k == NDM - 1))
                    ev[("stat", t3)] = c_t.inc(inst)

                @block.vector
                def _(vector, t3=t3):
                    W(vector, "vector", ev[("stat", t3)],
                      ev.get(("r1", t3 - 1)))
                    vector.tensor_copy(var_f[:], ps_st[:, 0, :])
                    vector.tensor_mul(var_f[:], var_f[:], ps_st[:, 0, :])
                    ev[("var", t3)] = c_v.inc(vector.tensor_sub(
                        var_f[:], ps_st[:, 1, :], var_f[:]))

                @block.scalar
                def _(scalar, t3=t3):
                    W(scalar, "scalar", ev[("var", t3)])
                    scalar.activation(out=r1_f[:], in_=var_f[:], func=AF.Ln,
                                      bias=EPS)
                    ev[("r1", t3)] = c_s.inc(scalar.activation(
                        out=r1_f[:], in_=r1_f[:], func=AF.Exp, scale=-0.5))

                @block.vector
                def _(vector, t3=t3):
                    W(vector, "vector", ev[("r1", t3)],
                      ev.get(("r2", t3 - 1)))
                    vector.tensor_mul(r2_f[:], r1_f[:], r1_f[:])
                    ev[("var2", t3)] = c_v.inc(vector.tensor_mul(
                        r2_f[:], r2_f[:], var_f[:]))

                @block.scalar
                def _(scalar, t3=t3):
                    W(scalar, "scalar", ev[("var2", t3)])
                    scalar.activation(out=r2_f[:], in_=r2_f[:], func=AF.Ln,
                                      bias=EPS)
                    ev[("r2", t3)] = c_s.inc(scalar.activation(
                        out=r2_f[:], in_=r2_f[:], func=AF.Exp, scale=-0.5))

                @block.vector
                def _(vector, t3=t3):
                    W(vector, "vector", ev[("r2", t3)],
                      ev.get(("x2n", t3 - 1)), ev.get(("m3w", t3 - 1)))
                    vector.tensor_mul(r12_f[:], r1_f[:], r2_f[:])
                    for i in range(NDM):
                        vector.scalar_tensor_tensor(
                            out=tvmu[:, i, :], in0=v_f[:, i, :], scalar=1.0,
                            in1=ps_st[:, 0, :], op0=ALU.mult, op1=ALU.subtract)
                        vector.tensor_mul(x2_f[:, i, :], tvmu[:, i, :], r1_f[:])
                        inst = vector.tensor_mul(tvmu[:, i, :], tvmu[:, i, :],
                                                 r12_f[:])
                    tok = c_v.inc(inst)
                    ev[("x2", t3)] = tok
                    rel["v_f"] = tok
                    rel["ps_st"] = tok

                @block.scalar
                def _(scalar, t3=t3):
                    W(scalar, "scalar", ev[("x2", t3)])
                    for i in range(NDM):
                        inst = scalar.activation(out=x2n_bf[:, i, :],
                                                 in_=tvmu[:, i, :],
                                                 func=AF.Identity,
                                                 scale=gam_sb[:, i:i + 1],
                                                 bias=bet_sb[:, i:i + 1])
                    ev[("x2n", t3)] = c_s.inc(inst)

                @block.sync
                def _(sync, t3=t3):
                    W(sync, "sync", ev[("x2n", t3)])
                    s_dma.inc(sync.dma_start(
                        out=x2_d[:, :, t3 * TT:(t3 + 1) * TT].transpose(
                            [1, 0, 2]),
                        in_=x2_f[:]))
                    ev[("m3w", t3)] = s_dma.inc(sync.dma_start(
                        out=x2n_d[:, :, t3 * TT:(t3 + 1) * TT].transpose(
                            [1, 0, 2]),
                        in_=x2n_bf[:]))

                if BARRIER_TILES:
                    barrier_all(("m3t", t3))

            m3_end = {"scalar": c_s.now(), "vector": c_v.now(),
                      "tensor": c_t.now(), "gpsimd": (g_dma, g_dma.n),
                      "sync": s_dma.now()}

        bar_3 = barrier(m3_end, "m3")

        with ExitStack() as stf:
            Af, Pf = mk(stf)
            w1_bf = Af("w1_bf", [128, NDM, FF], BF16)
            w2_bf = Af("w2_bf", [128, NFF, DM], BF16)

            with ExitStack() as stq:
                Aq, _ = mk(stq)
                qst = Aq("qst", [128, 2048], I32)
                qf = Aq("qf", [128, 2048], F32)
                scb = Aq("scb", [128, FF], F32)
                sel_f = Aq("sel_f", [32, 32 * 128], F32)
                sc_sb = Aq("sc_sb", [32, FF], F32)
                ps_sc = stq.enter_context(
                    nc.psum_tensor("ps_sc", [128, TT], F32))

                @block.sync
                def _(sync):
                    W(sync, "sync", bar_3)
                    s_dma.inc(sync.dma_start(out=sel_f[:], in_=selmat[:, :]))
                    ev["scl"] = s_dma.inc(sync.dma_start(
                        out=sc_sb[0:8, 0:FF], in_=s1[:, :]))

                for wi, (qsrc, nk, fdim, dst, nrows) in enumerate(
                        [(q1, NDM, FF, w1_bf, 8), (q2, NFF, DM, w2_bf, 32)]):
                    if wi == 1:
                        @block.sync
                        def _(sync):
                            W(sync, "sync", ev["w1done"])
                            ev["scl"] = s_dma.inc(sync.dma_start(
                                out=sc_sb[0:32, 0:DM], in_=s2[:, :]))
                    for g in range(nk):
                        for jj in range(fdim // TT):
                            @block.tensor
                            def _(tensor, g=g, jj=jj, nrows=nrows):
                                W(tensor, "tensor", ev["scl"], bar_3,
                                  rel.get("ps_sc"))
                                ev["scm"] = c_t.inc(nc.tensor.matmul(
                                    ps_sc[:, :],
                                    sel_f[0:nrows, g * 128:(g + 1) * 128],
                                    sc_sb[0:nrows, jj * TT:(jj + 1) * TT],
                                    start=True, stop=True))

                            @block.scalar
                            def _(scalar, jj=jj):
                                W(scalar, "scalar", ev["scm"], rel.get("scb"))
                                rel["ps_sc"] = c_s.inc(scalar.activation(
                                    out=scb[:, jj * TT:(jj + 1) * TT],
                                    in_=ps_sc[:, :], func=AF.Copy))
                        for fh in range((fdim + 2047) // 2048):
                            w0 = fh * 2048
                            w1w = min(fdim - w0, 2048)

                            @block.sync
                            def _(sync, qsrc=qsrc, g=g, w0=w0, w1w=w1w):
                                W(sync, "sync", bar_3, rel.get("qst"))
                                ev["ql"] = s_dma.inc(sync.dma_start(
                                    out=qst[:, 0:w1w],
                                    in_=qsrc[g * 128:(g + 1) * 128,
                                             w0:w0 + w1w]))

                            @block.scalar
                            def _(scalar, w1w=w1w):
                                W(scalar, "scalar", ev["ql"], rel.get("qf"))
                                ev["qd"] = c_s.inc(scalar.activation(
                                    out=qf[:, 0:w1w], in_=qst[:, 0:w1w],
                                    func=AF.Copy, bias=-128.0))

                            @block.vector
                            def _(vector, g=g, w0=w0, w1w=w1w, dst=dst):
                                W(vector, "vector", ev["qd"], rel.get("ps_sc"))
                                tok = c_v.inc(vector.tensor_mul(
                                    dst[:, g, w0:w0 + w1w], qf[:, 0:w1w],
                                    scb[:, w0:w0 + w1w]))
                                rel["qst"] = tok
                                rel["qf"] = tok
                                if w0 + w1w == fdim:
                                    rel["scb"] = tok
                    if wi == 0:
                        ev["w1done"] = c_v.now()
                ev["fw"] = c_v.now()

            act_bf = Af("act_bf", [128, 16, TT], BF16)
            z2a = Af("z2a", [128, NDM, TT], F32)
            x2r = Af("x2r", [128, NDM, TT], F32)
            x2nr = Af("x2nr", [128, NDM, TT], BF16)
            f_ta = Af("f_ta", [128, TT], F32)
            f_tb = Af("f_tb", [128, TT], F32)
            f_tc = Af("f_tc", [128, TT], F32)
            fvar = Af("fvar", [128, TT], F32)
            fr = Af("fr", [128, TT], F32)

            ps_f1 = Pf("ps_f1", [128, 4, TT])
            ps_f2 = Pf("ps_f2", [128, 2, TT])
            ps_fs = Pf("ps_fs", [128, 2, TT])

            for t3 in range(NT3):
                @block.sync
                def _(sync, t3=t3):
                    W(sync, "sync", ev[("m3w", t3)], rel.get("x2nr"))
                    s_dma.inc(sync.dma_start(
                        out=x2nr[:, :, :],
                        in_=x2n_d[:, :, t3 * TT:(t3 + 1) * TT].rearrange(
                            "c p t -> p c t")))
                    ev[("fx", t3)] = s_dma.inc(sync.dma_start(
                        out=x2r[:, :, :],
                        in_=x2_d[:, :, t3 * TT:(t3 + 1) * TT].rearrange(
                            "c p t -> p c t")))

                for halfF in range(2):
                    for fi in range(16):
                        f = halfF * 16 + fi

                        @block.tensor
                        def _(tensor, t3=t3, f=f, fi=fi):
                            W(tensor, "tensor", ev[("fx", t3)], ev["fw"],
                              rel.get(("ps_f1", fi % 4)))
                            for k in range(NDM):
                                inst = nc.tensor.matmul(
                                    ps_f1[:, fi % 4, :],
                                    w1_bf[:, k, f * 128:(f + 1) * 128],
                                    x2nr[:, k, :],
                                    start=(k == 0), stop=(k == NDM - 1))
                            ev[("f1", t3, f)] = c_t.inc(inst)

                        @block.scalar
                        def _(scalar, t3=t3, f=f, fi=fi):
                            W(scalar, "scalar", ev[("f1", t3, f)],
                              rel.get("f_t"))
                            inst = scalar.activation(out=f_tb[:],
                                                     in_=ps_f1[:, fi % 4, :],
                                                     func=AF.Sigmoid,
                                                     bias=b1_sb[:, f:f + 1])
                            ev[("fs", t3, f)] = c_s.inc(inst)

                        @block.vector
                        def _(vector, t3=t3, f=f, fi=fi):
                            W(vector, "vector", ev[("fs", t3, f)],
                              rel.get(("act", fi)))
                            vector.tensor_scalar_add(
                                f_tc[:], ps_f1[:, fi % 4, :],
                                b1_sb[:, f:f + 1])
                            tok = c_v.inc(vector.tensor_mul(
                                act_bf[:, fi, :], f_tc[:], f_tb[:]))
                            ev[("act", t3, f)] = tok
                            rel["f_t"] = tok
                            rel[("ps_f1", fi % 4)] = tok

                    for i in range(NDM):
                        @block.tensor
                        def _(tensor, t3=t3, halfF=halfF, i=i):
                            W(tensor, "tensor",
                              [ev[("act", t3, halfF * 16 + fk)]
                               for fk in range(16)],
                              rel.get(("ps_f2", i % 2)))
                            for fk in range(16):
                                inst = nc.tensor.matmul(
                                    ps_f2[:, i % 2, :],
                                    w2_bf[:, halfF * 16 + fk,
                                          i * 128:(i + 1) * 128],
                                    act_bf[:, fk, :],
                                    start=(fk == 0), stop=(fk == 15))
                            tok = c_t.inc(inst)
                            ev[("f2", t3, halfF, i)] = tok
                            if i == NDM - 1:
                                for fi2 in range(16):
                                    rel[("act", fi2)] = tok

                        if halfF == 0:
                            @block.scalar
                            def _(scalar, t3=t3, i=i):
                                W(scalar, "scalar", ev[("f2", t3, 0, i)],
                                  rel.get(("z2a", i)))
                                tok = c_s.inc(scalar.activation(
                                    out=z2a[:, i, :], in_=ps_f2[:, i % 2, :],
                                    func=AF.Identity, scale=0.5,
                                    bias=b2h_sb[:, i:i + 1]))
                                ev[("z2a", t3, i)] = tok
                                rel[("ps_f2", i % 2)] = tok
                        else:
                            @block.vector
                            def _(vector, t3=t3, i=i):
                                W(vector, "vector", ev[("f2", t3, 1, i)],
                                  ev[("z2a", t3, i)])
                                vector.scalar_tensor_tensor(
                                    out=z2a[:, i, :], in0=ps_f2[:, i % 2, :],
                                    scalar=0.5, in1=z2a[:, i, :],
                                    op0=ALU.mult, op1=ALU.add)
                                tok = c_v.inc(vector.scalar_tensor_tensor(
                                    out=z2a[:, i, :], in0=x2r[:, i, :],
                                    scalar=2.0, in1=z2a[:, i, :],
                                    op0=ALU.mult, op1=ALU.add))
                                ev[("u", t3, i)] = tok
                                rel[("ps_f2", i % 2)] = tok

                @block.scalar
                def _(scalar, t3=t3):
                    W(scalar, "scalar", [ev[("u", t3, i)] for i in range(NDM)],
                      ev.get(("fst", t3 - 1)))
                    scalar.activation(out=act_bf[:, 0:NDM, :],
                                      in_=z2a[:], func=AF.Copy)
                    ev[("ub", t3)] = c_s.inc(scalar.activation(
                        out=act_bf[:, NDM:2 * NDM, :],
                        in_=z2a[:], func=AF.Square))

                @block.tensor
                def _(tensor, t3=t3):
                    W(tensor, "tensor", ev[("ub", t3)], rel.get("ps_fs"))
                    for k in range(NDM):
                        nc.tensor.matmul(ps_fs[:, 0, :], onesh_bf[:, :],
                                         act_bf[:, k, :], start=(k == 0),
                                         stop=(k == NDM - 1))
                    for k in range(NDM):
                        inst = nc.tensor.matmul(ps_fs[:, 1, :], onesh_bf[:, :],
                                                act_bf[:, NDM + k, :],
                                                start=(k == 0),
                                                stop=(k == NDM - 1))
                    ev[("fst", t3)] = c_t.inc(inst)

                @block.vector
                def _(vector, t3=t3):
                    W(vector, "vector", ev[("fst", t3)],
                      ev.get(("fr", t3 - 1)))
                    vector.tensor_copy(fvar[:], ps_fs[:, 0, :])
                    vector.tensor_mul(fvar[:], fvar[:], ps_fs[:, 0, :])
                    ev[("fvar", t3)] = c_v.inc(vector.tensor_sub(
                        fvar[:], ps_fs[:, 1, :], fvar[:]))

                @block.scalar
                def _(scalar, t3=t3):
                    W(scalar, "scalar", ev[("fvar", t3)])
                    scalar.activation(out=fr[:], in_=fvar[:], func=AF.Ln,
                                      bias=EPS)
                    ev[("fr", t3)] = c_s.inc(scalar.activation(
                        out=fr[:], in_=fr[:], func=AF.Exp, scale=-0.5))

                @block.vector
                def _(vector, t3=t3):
                    W(vector, "vector", ev[("fr", t3)],
                      ev.get(("outw", t3 - 1)))
                    for i in range(NDM):
                        vector.scalar_tensor_tensor(
                            out=x2r[:, i, :], in0=z2a[:, i, :], scalar=1.0,
                            in1=ps_fs[:, 0, :], op0=ALU.mult, op1=ALU.subtract)
                        inst = vector.tensor_mul(x2r[:, i, :], x2r[:, i, :],
                                                 fr[:])
                    tok = c_v.inc(inst)
                    ev[("fout", t3)] = tok
                    rel["ps_fs"] = tok
                    for i in range(NDM):
                        rel[("z2a", i)] = tok

                @block.sync
                def _(sync, t3=t3):
                    W(sync, "sync", ev[("fout", t3)])
                    tok = s_dma.inc(sync.dma_start(
                        out=out_d[:, :, t3 * TT:(t3 + 1) * TT].transpose(
                            [1, 0, 2]),
                        in_=x2r[:]))
                    ev[("outw", t3)] = tok
                    rel["x2nr"] = tok

                if BARRIER_TILES:
                    barrier_all(("ft", t3))

            @block.sync
            def _(sync):
                sync.wait_ge(s_dma.h, s_dma.n)

            @block.gpsimd
            def _(gpsimd):
                gpsimd.wait_ge(s_dma.h, s_dma.n)

    return nc


_CACHE = {}


def kernel(**inputs):
    if "nc" not in _CACHE:
        _CACHE["nc"] = build_nc()
    nc = _CACHE["nc"]

    gi = lambda k: np.ascontiguousarray(inputs[k])
    x = gi("L0_attn_x")
    W_in = gi("L0_attn_W_in")
    conv_w = gi("L0_attn_conv_w")
    conv_b = gi("L0_attn_conv_b")
    W_xproj = gi("L0_attn_W_xproj")
    W_dt = gi("L0_attn_W_dt")
    b_dt = gi("L0_attn_b_dt")
    A_log = gi("L0_attn_A_log")
    Dp = gi("L0_attn_Dp")
    W_out = gi("L0_attn_W_out")
    ln_g = gi("L0_ffn_ln_gamma")
    ln_b = gi("L0_ffn_ln_beta")
    fc1_q = gi("L0_ffn_fc1_q")
    fc1_s = gi("L0_ffn_fc1_scales")
    fc1_b = gi("L0_ffn_fc1_bias")
    fc2_q = gi("L0_ffn_fc2_q")
    fc2_s = gi("L0_ffn_fc2_scales")
    fc2_b = gi("L0_ffn_fc2_bias")

    selm = np.zeros((32, 32 * 128), dtype=np.float32)
    for s in range(32):
        selm[s, s * 128:(s + 1) * 128] = 1.0
    identm = np.eye(128, dtype=np.float32)
    onesh = np.full((128, 128), 1.0 / DM, dtype=np.float32)
    da01 = np.tile(np.array([0.0, 1.0], np.float32), (128, 1))

    rng_n = np.random.default_rng()
    nonces = [rng_n.integers(1, 2**32 - 1, size=(1, 64), dtype=np.uint32)
              for _ in range(4)]
    in_maps = []
    for c in range(8):
        b, h = c // 2, c % 2
        ch0 = h * CH
        im = {
            "xT": np.ascontiguousarray(x[b].T),
            "winx": np.ascontiguousarray(W_in[:, ch0:ch0 + CH]),
            "winz": np.ascontiguousarray(W_in[:, 2048 + ch0:2048 + ch0 + CH]),
            "wxp": np.ascontiguousarray(W_xproj[ch0:ch0 + CH, :]),
            "wdt": np.ascontiguousarray(W_dt[:, ch0:ch0 + CH]),
            "convb": np.ascontiguousarray(
                conv_b[ch0:ch0 + CH].reshape(NCH, 128).T),
            "bdt": np.ascontiguousarray(b_dt[ch0:ch0 + CH].reshape(NCH, 128).T),
            "alog": np.ascontiguousarray(
                A_log[ch0:ch0 + CH].reshape(NCH, 128, DS).transpose(1, 0, 2)),
            "dp": np.ascontiguousarray(Dp[ch0:ch0 + CH].reshape(NCH, 128).T),
            "wout_lo": np.ascontiguousarray(W_out[0:CH, :]),
            "wout_hi": np.ascontiguousarray(W_out[CH:2 * CH, :]),
            "gam": np.ascontiguousarray(ln_g.reshape(NDM, 128).T),
            "bet": np.ascontiguousarray(ln_b.reshape(NDM, 128).T),
            "q1": fc1_q,
            "s1": fc1_s,
            "b1": np.ascontiguousarray(fc1_b.reshape(NFF, 128).T),
            "q2": fc2_q,
            "s2": fc2_s,
            "b2": np.ascontiguousarray(fc2_b.reshape(NDM, 128).T),
            "selmat": selm,
            "identm": identm,
            "onesh": onesh,
            "da01": da01,
            "nonce_in": nonces[b],
        }
        cd = np.zeros((32, 128, 128), dtype=np.float32)
        for cc in range(NCH):
            for k in range(DCONV):
                np.fill_diagonal(cd[cc * 4 + k],
                                 conv_w[ch0 + cc * 128:ch0 + (cc + 1) * 128, k])
        im["convdiag"] = cd
        in_maps.append(im)

    res = run_bass_kernel_spmd(nc, in_maps, core_ids=list(range(8)))

    full = np.empty((B, L, DM), dtype=np.float32)
    for c in range(8):
        b, h = c // 2, c % 2
        o = res.results[c]["out"]
        full[b, h * TOK:(h + 1) * TOK, :] = \
            o.transpose(2, 0, 1).reshape(TOK, DM)
    return full


# version 15 (final)
# speedup vs baseline: 1.2435x; 1.0212x over previous
import numpy as np
from contextlib import ExitStack

import concourse.bass as bass
import concourse.mybir as mybir
from concourse.bass_utils import run_bass_kernel_spmd

F32 = mybir.dt.float32
BF16 = mybir.dt.bfloat16
U32 = mybir.dt.uint32
I32 = mybir.dt.int32
AF = mybir.ActivationFunctionType
ALU = mybir.AluOpType
ENG = mybir.EngineType

B, L, DM = 4, 8192, 1024
DS, DCONV, DTR = 16, 4, 64
CH, NCH, NDM = 1024, 8, 8
TT = 512
NT = L // TT
TOK = L // 2
NT3 = TOK // TT
T1 = TT + 2
GS, NG = 4, 4
FF, NFF = 4096, 32
EPS = 1e-5


class Sem:
    def __init__(self, stack, nc, name, engine=None):
        self.h = stack.enter_context(nc.semaphore(name))
        self.n = 0
        self.engine = engine
        self.dk = 1 if engine else 16

    def inc(self, instr, k=None):
        k = self.dk if k is None else k
        instr.then_inc(self.h, k)
        self.n += k
        return (self, self.n)

    def now(self):
        return (self, self.n)


def _flat(evs):
    for e in evs:
        if e is None:
            continue
        if isinstance(e, list):
            yield from _flat(e)
        else:
            yield e


def build_nc():
    nc = bass.Bass("TRN2", target_bir_lowering=False)

    def din(name, shape, dt=F32):
        return nc.dram_tensor(name, shape, dt, kind="ExternalInput").ap()

    xT = din("xT", [DM, L])
    winx = din("winx", [DM, CH])
    winz = din("winz", [DM, CH])
    wxp = din("wxp", [CH, 96])
    wdt_in = din("wdt", [DTR, CH])
    convdiag = din("convdiag", [32, 128, 128])
    convb = din("convb", [128, NCH])
    bdt = din("bdt", [128, NCH])
    alog = din("alog", [128, NCH, DS])
    dp_in = din("dp", [128, NCH])
    wout_lo = din("wout_lo", [CH, DM])
    wout_hi = din("wout_hi", [CH, DM])
    gam = din("gam", [128, NDM])
    bet = din("bet", [128, NDM])
    q1 = din("q1", [DM, FF], I32)
    s1 = din("s1", [8, FF])
    b1 = din("b1", [128, NFF])
    q2 = din("q2", [FF, DM], I32)
    s2 = din("s2", [32, DM])
    b2 = din("b2", [128, NDM])
    selmat = din("selmat", [32, 32 * 128])
    identm = din("identm", [128, 128])
    onesh = din("onesh", [128, 128])
    da01_in = din("da01", [128, 2])
    nonce_in = din("nonce_in", [1, 64], U32)

    out_d = nc.dram_tensor("out", [NDM, 128, TOK], F32, kind="ExternalOutput").ap()
    x2_d = nc.dram_tensor("x2_d", [NDM, 128, TOK], F32, kind="Internal").ap()
    x2n_d = nc.dram_tensor("x2n_d", [NDM, 128, TOK], BF16, kind="Internal").ap()

    xdbl_shr = nc.dram_tensor("xdbl_shr", [2, 96, L], F32, kind="Internal",
                              addr_space="Shared").ap()
    y_shr = nc.dram_tensor("y_shr", [2, NCH, 128, L], BF16, kind="Internal",
                           addr_space="Shared").ap()
    flg_xd = nc.dram_tensor("flg_xd", [2, NT, 2], U32, kind="Internal",
                            addr_space="Shared").ap()
    flg_y = nc.dram_tensor("flg_y", [2, 2], U32, kind="Internal",
                           addr_space="Shared").ap()

    with ExitStack() as st0:
        def mk(stk):
            def A(name, shape, dt=F32):
                return stk.enter_context(nc.sbuf_tensor(name, shape, dt))

            def P(name, shape):
                return stk.enter_context(nc.psum_tensor(name, shape, F32))
            return A, P

        A0, _ = mk(st0)
        ident_bf = A0("ident_bf", [128, 128], BF16)
        sel_bf = A0("sel_bf", [32, 32 * 128], BF16)
        onesh_bf = A0("onesh_bf", [128, 128], BF16)
        nonce = A0("nonce", [1, 64], F32)
        flagw = A0("flagw", [1, 2], U32)
        gam_sb = A0("gam_sb", [128, NDM], F32)
        bet_sb = A0("bet_sb", [128, NDM], F32)
        b2h_sb = A0("b2h_sb", [128, NDM], F32)
        b1_sb = A0("b1_sb", [128, NFF], F32)
        nb1_sb = A0("nb1_sb", [128, NFF], F32)
        const_sb = A0("const_sb", [128, 2], F32)

        nc.const_aps.register(F32, EPS, const_sb[:, 0:1])
        nc.const_aps.register(F32, -128.0, const_sb[:, 1:2])

        block = st0.enter_context(nc.Block())

        s_dma = Sem(st0, nc, "s_dma")
        g_dma = Sem(st0, nc, "g_dma")
        s_poll = Sem(st0, nc, "s_poll")
        c_s = Sem(st0, nc, "c_s", "scalar")
        c_v = Sem(st0, nc, "c_v", "vector")
        c_t = Sem(st0, nc, "c_t", "tensor")
        c_g = Sem(st0, nc, "c_g", "gpsimd")

        def W(eng, engname, *evs):
            seen = {}
            for sem, val in _flat(evs):
                if val <= 0 or sem.engine == engname:
                    continue
                if seen.get(id(sem), -1) < val:
                    seen[id(sem)] = val
                    eng.wait_ge(sem.h, val)

        ev = {}
        rel = {}
        regs = {}

        @block.gpsimd
        def _(gpsimd):
            pid = nc.partition_id([ENG.Pool])
            r = gpsimd.alloc_register("slot")
            gpsimd.reg_alu(r, gpsimd.to_reg(pid), 1, ALU.bitwise_and)
            regs["slot"] = r
            pt = gpsimd.alloc_register("ptgt")
            gpsimd.reg_mov(pt, 0)
            regs["ptgt"] = pt

        def if_slot(gpsimd, k):
            return gpsimd.If_eq(regs["slot"], k)

        def bdma(gpsimd, sem, mk_pair, use_peer=False):
            for k in (0, 1):
                with if_slot(gpsimd, (1 - k) if use_peer else k):
                    o, i = mk_pair(k)
                    gpsimd.dma_start(out=o, in_=i).then_inc(sem.h, 16)
                gpsimd.end_ifs()
            sem.n += 16
            return (sem, sem.n)

        _pollid = [0]

        def poll(gpsimd, flag_ap_fn, word_fn):
            pt = regs["ptgt"]
            u = str(_pollid[0])
            _pollid[0] += 1
            with (
                gpsimd.register("nd" + u) as nd,
                gpsimd.register("f0" + u) as f0,
                gpsimd.register("f1" + u) as f1,
                gpsimd.register("e0" + u) as e0,
                gpsimd.register("e1" + u) as e1,
                gpsimd.register("t0" + u) as t0,
                gpsimd.register("t1" + u) as t1r,
            ):
                for k in (0, 1):
                    with if_slot(gpsimd, 1 - k):
                        w = word_fn(k)
                        gpsimd.reg_load(e0, nonce[0:1, w:w + 1].bitcast(U32))
                        gpsimd.reg_load(e1, nonce[0:1, w + 1:w + 2].bitcast(U32))
                    gpsimd.end_ifs()
                gpsimd.reg_mov(nd, 1)
                with gpsimd.While(nd):
                    gpsimd.reg_alu(pt, pt, 16, ALU.add)
                    for k in (0, 1):
                        with if_slot(gpsimd, 1 - k):
                            gpsimd.dma_start(out=flagw[0:1, :],
                                             in_=flag_ap_fn(k)
                                             ).then_inc(s_poll.h, 16)
                        gpsimd.end_ifs()
                    gpsimd.wait_ge(s_poll.h, pt)
                    gpsimd.reg_load(f0, flagw[0:1, 0:1])
                    gpsimd.reg_load(f1, flagw[0:1, 1:2])
                    gpsimd.reg_alu(t0, f0, e0, ALU.not_equal)
                    gpsimd.reg_alu(t1r, f1, e1, ALU.not_equal)
                    gpsimd.reg_alu(nd, t0, t1r, ALU.bitwise_or)

        BARRIER_TILES = False

        def barrier_all(tag):
            ends = {"scalar": c_s.now(), "vector": c_v.now(),
                    "tensor": c_t.now(), "gpsimd": (g_dma, g_dma.n),
                    "sync": s_dma.now()}
            marks = {}
            for name in ("scalar", "vector", "tensor", "gpsimd", "sync"):
                sem = {"scalar": c_s, "vector": c_v, "tensor": c_t,
                       "gpsimd": c_g, "sync": None}[name]

                @getattr(block, name)
                def _(eng, name=name, sem=sem):
                    for oname, tok in ends.items():
                        if oname != name:
                            eng.wait_ge(tok[0].h, tok[1])
                    if sem is not None:
                        marks[name] = sem.inc(eng.nop())
            for name in ("scalar", "vector", "tensor", "gpsimd", "sync"):
                @getattr(block, name)
                def _(eng, name=name):
                    for oname, tok in marks.items():
                        if oname != name:
                            eng.wait_ge(tok[0].h, tok[1])

        with ExitStack() as stm:
            Am, Pm = mk(stm)
            winx_bf = Am("winx_bf", [128, NDM, CH], BF16)
            winz_bf = Am("winz_bf", [128, NDM, CH], BF16)
            wxp_bf = Am("wxp_bf", [128, NCH, 96], BF16)
            wdt_bf = Am("wdt_bf", [DTR, CH], BF16)
            cdiag_bf = Am("cdiag_bf", [128, 32, 128], BF16)
            convb_sb = Am("convb_sb", [128, NCH], F32)
            bdt_sb = Am("bdt_sb", [128, NCH], F32)
            a_sb = Am("a_sb", [128, NCH, DS], F32)
            dp_sb = Am("dp_sb", [128, NCH], F32)
            da01_f = Am("da01_f", [128, 2], F32)

            with ExitStack() as stw:
                Aw, _ = mk(stw)
                wstage = Aw("wstage", [128, 2, 8192], F32)

                @block.sync
                def _(sync):
                    for dst, src in [
                        (convb_sb[:], convb[:, :]),
                        (bdt_sb[:], bdt[:, :]),
                        (a_sb[:], alog[:, :, :]),
                        (dp_sb[:], dp_in[:, :]),
                        (da01_f[:], da01_in[:, :]),
                        (nonce[:].bitcast(U32), nonce_in[:, :]),
                        (gam_sb[:], gam[:, :]),
                        (bet_sb[:], bet[:, :]),
                        (b2h_sb[:], b2[:, :]),
                        (b1_sb[:], b1[:, :]),
                        (wstage[0:32, 1, 0:32 * 128], selmat[:, :]),
                        (wstage[:, 0, 0:128], identm[:, :]),
                        (wstage[:, 0, 128:256], onesh[:, :]),
                        (wstage[0:64, 0, 256:256 + CH], wdt_in[:, :]),
                        (wstage[:, 0, 2048:2048 + 32 * 128].rearrange(
                            "p (a c) -> p a c", a=32),
                         convdiag[:, :, :].transpose([1, 0, 2])),
                        (wstage[:, 0, 6144:6144 + NCH * 96].rearrange(
                            "p (c n) -> p c n", c=NCH),
                         wxp[:, :].rearrange("(c p) n -> p c n", p=128)),
                    ]:
                        sync.dma_start(out=dst, in_=src).then_inc(s_dma.h, 16)
                        s_dma.n += 16
                    ev["ld0"] = s_dma.now()

                @block.scalar
                def _(scalar):
                    W(scalar, "scalar", ev["ld0"])
                    scalar.activation(out=ident_bf[:], in_=wstage[:, 0, 0:128],
                                      func=AF.Copy)
                    scalar.activation(out=onesh_bf[:], in_=wstage[:, 0, 128:256],
                                      func=AF.Copy)
                    scalar.activation(out=sel_bf[:],
                                      in_=wstage[0:32, 1, 0:32 * 128],
                                      func=AF.Copy)
                    scalar.activation(out=wdt_bf[:],
                                      in_=wstage[0:64, 0, 256:256 + CH],
                                      func=AF.Copy)
                    scalar.activation(out=cdiag_bf[:].rearrange("p a c -> p (a c)"),
                                      in_=wstage[:, 0, 2048:2048 + 32 * 128],
                                      func=AF.Copy)
                    scalar.activation(out=wxp_bf[:].rearrange("p c n -> p (c n)"),
                                      in_=wstage[:, 0, 6144:6144 + NCH * 96],
                                      func=AF.Copy)
                    scalar.activation(out=a_sb[:], in_=a_sb[:], func=AF.Exp)
                    scalar.activation(out=nb1_sb[:], in_=b1_sb[:], func=AF.Copy,
                                      scale=-1.0)
                    ev["sc0"] = c_s.inc(scalar.activation(
                        out=b2h_sb[:], in_=b2h_sb[:], func=AF.Copy, scale=0.5))
                    rel[("wst", 0)] = ev["sc0"]
                    rel[("wst", 1)] = ev["sc0"]

                for kind, src, dst in (("x", winx, winx_bf), ("z", winz, winz_bf)):
                    for k in range(NDM):
                        @block.sync
                        def _(sync, kind=kind, src=src, k=k):
                            W(sync, "sync", rel.get(("wst", k % 2)))
                            ev[("wl", kind, k)] = s_dma.inc(sync.dma_start(
                                out=wstage[:, k % 2, 0:CH],
                                in_=src[k * 128:(k + 1) * 128, :]))

                        @block.scalar
                        def _(scalar, kind=kind, k=k, dst=dst):
                            W(scalar, "scalar", ev[("wl", kind, k)])
                            rel[("wst", k % 2)] = c_s.inc(scalar.activation(
                                out=dst[:, k, :], in_=wstage[:, k % 2, 0:CH],
                                func=AF.Copy))
                ev["wm"] = c_s.now()
                rel[("x_st", 0)] = ev["wm"]
                rel[("x_st", 1)] = ev["wm"]

                @block.vector
                def _(vector):
                    W(vector, "vector", ev["sc0"])
                    vector.tensor_scalar_mul(a_sb[:], a_sb[:], -1.0)
                    vector.memset(const_sb[:, 0:1], EPS)
                    vector.memset(const_sb[:, 1:2], -128.0)
                    ev["v0"] = c_v.inc(vector.nop())

            xbf = Am("xbf", [128, NDM, TT], BF16)
            xin_bf = Am("xin_bf", [128, 2, NCH, TT + 3], BF16)
            zs_bf = Am("zs_bf", [128, 2, NCH, TT], BF16)
            xc_bf = Am("xc_bf", [128, 2, NCH, TT], BF16)
            dt_f = Am("dt_f", [128, 2, TT], F32)
            dtx_bf = Am("dtx_bf", [128, 2, TT], BF16)
            xd_mine = Am("xd_mine", [96, 2, TT], F32)
            xd_peer = Am("xd_peer", [96, TT], F32)
            xd_sum = Am("xd_sum", [96, TT], F32)
            dtraw_bf = Am("dtraw_bf", [DTR, TT], BF16)
            bc_bf = Am("bc_bf", [32, TT], BF16)
            brep_bf = Am("brep_bf", [128, DS, TT], BF16)
            crep_bf = Am("crep_bf", [128, DS, TT], BF16)
            da_f = Am("da_f", [128, 3, GS * T1], F32)
            dbx_bf = Am("dbx_bf", [128, GS * T1], BF16)
            h_bf = Am("h_bf", [128, GS * T1], BF16)
            yc_bf = Am("yc_bf", [128, 2, GS, TT], BF16)
            carr = Am("carr", [128, NCH, DS], F32)
            t_a = Am("t_a", [128, TT], F32)
            t_re = t_a
            epi_f = Am("epi_f", [128, TT], F32)
            yfin_bf = Am("yfin_bf", [128, 2, TT], BF16)

            ps_xz = Pm("ps_xz", [128, 2, TT])
            ps_conv = ps_xz
            ps_sm = Pm("ps_sm", [128, 2, TT])
            ps_bc = Pm("ps_bc", [128, 2, TT])
            ps_y = Pm("ps_y", [128, 2, TT])

            @block.vector
            def _(vector):
                W(vector, "vector", ev["v0"], ev["wm"])
                vector.memset(carr[:], 0.0)
                vector.memset(dbx_bf[:], 0.0)
                vector.memset(xin_bf[:, :, :, 0:3], 0.0)
                for p2 in range(3):
                    for s in range(GS):
                        vector.tensor_copy(da_f[:, p2, s * T1:s * T1 + 2],
                                           da01_f[:])
                ev["v1"] = c_v.inc(vector.nop())

            for tau in range(NT):
                par = tau % 2
                t0, t1 = tau * TT, (tau + 1) * TT

                @block.gpsimd
                def _(gpsimd, tau=tau, t0=t0):
                    W(gpsimd, "gpsimd", rel.get("xbf"),
                      [ev["wm"], ev["v1"]] if tau == 0 else None)
                    ev[("xbf", tau)] = g_dma.inc(gpsimd.dma_start(
                        out=xbf[:, :, :],
                        in_=xT[:, t0:t0 + TT].rearrange(
                            "(c p) t -> p c t", p=128)))

                Vc = c_s.now()[1]
                for i in range(16):
                    @block.tensor
                    def _(tensor, tau=tau, i=i, Vc=Vc):
                        W(tensor, "tensor", ev[("xbf", tau)],
                          rel.get(("ps_xz", i % 2)) if i < 2 else None)
                        if i >= 2:
                            tensor.wait_ge(c_s.h, Vc + i - 1)
                        wsrc = winx_bf if i < NCH else winz_bf
                        io = i % NCH
                        for k in range(NDM):
                            inst = nc.tensor.matmul(
                                ps_xz[:, i % 2, :],
                                wsrc[:, k, io * 128:(io + 1) * 128],
                                xbf[:, k, :], start=(k == 0), stop=(k == NDM - 1))
                        tok = c_t.inc(inst)
                        ev[("xz", tau, i)] = tok
                        if i == 15:
                            rel["xbf"] = tok

                    if i == 7:
                        @block.scalar
                        def _(scalar, tau=tau, par=par):
                            for i2 in range(8):
                                W(scalar, "scalar", ev[("xz", tau, i2)],
                                  rel.get(("xin", par)) if i2 == 0 else None)
                                tok = c_s.inc(scalar.activation(
                                    out=xin_bf[:, par, i2, 3:TT + 3],
                                    in_=ps_xz[:, i2 % 2, :], func=AF.Copy))
                                assert tok[1] == Vc + i2 + 1
                                ev[("xzev", tau, i2)] = tok
                                rel[("ps_xz", i2 % 2)] = tok
                    if i == 15:
                        @block.scalar
                        def _(scalar, tau=tau, par=par):
                            for i2 in range(8, 16):
                                W(scalar, "scalar", ev[("xz", tau, i2)],
                                  rel.get(("zs", par)) if i2 == 8 else None)
                                tok = c_s.inc(scalar.activation(
                                    out=zs_bf[:, par, i2 - NCH, :],
                                    in_=ps_xz[:, i2 % 2, :], func=AF.Silu))
                                assert tok[1] == Vc + i2 + 1
                                ev[("xzev", tau, i2)] = tok
                                rel[("ps_xz", i2 % 2)] = tok

                @block.scalar
                def _(scalar, tau=tau, par=par):
                    if tau >= 1:
                        W(scalar, "scalar", ev[("conv", tau - 1, NCH - 1)])
                        for c in range(NCH):
                            scalar.activation(out=xin_bf[:, par, c, 0:3],
                                              in_=xin_bf[:, 1 - par, c, TT:TT + 3],
                                              func=AF.Copy)
                    ev[("ctx", tau)] = c_s.inc(scalar.nop())

                Vd = c_s.now()[1]
                for c in range(NCH):
                    @block.tensor
                    def _(tensor, tau=tau, par=par, c=c, Vd=Vd):
                        W(tensor, "tensor", ev[("xzev", tau, c)],
                          ev[("ctx", tau)],
                          rel.get(("ps_xz", c % 2)) if c < 2 else None)
                        if c >= 2:
                            tensor.wait_ge(c_s.h, Vd + c - 1)
                        for k in range(DCONV):
                            inst = nc.tensor.matmul(
                                ps_conv[:, c % 2, :], cdiag_bf[:, c * 4 + k, :],
                                xin_bf[:, par, c, k:k + TT],
                                start=(k == 0), stop=(k == DCONV - 1))
                        tok = c_t.inc(inst)
                        ev[("conv", tau, c)] = tok
                        if c == NCH - 1:
                            rel[("xin", par)] = tok

                    if c == NCH - 1:
                        @block.scalar
                        def _(scalar, tau=tau, par=par):
                            for c2 in range(NCH):
                                W(scalar, "scalar", ev[("conv", tau, c2)],
                                  rel.get(("xc", par)) if c2 == 0 else None)
                                tok = c_s.inc(scalar.activation(
                                    out=xc_bf[:, par, c2, :],
                                    in_=ps_conv[:, c2 % 2, :], func=AF.Silu,
                                    bias=convb_sb[:, c2:c2 + 1]))
                                assert tok[1] == Vd + c2 + 1
                                ev[("xc", tau, c2)] = tok
                                rel[("ps_xz", c2 % 2)] = tok

                @block.tensor
                def _(tensor, tau=tau, par=par):
                    W(tensor, "tensor", rel.get("ps_xp"),
                      rel.get(("ps_dt", 0)), rel.get(("ps_dt", 1)),
                      [ev[("xc", tau, c)] for c in range(NCH)])
                    for c in range(NCH):
                        inst = nc.tensor.matmul(
                            ps_sm[0:96, 0, :], wxp_bf[:, c, :],
                            xc_bf[:, par, c, :],
                            start=(c == 0), stop=(c == NCH - 1))
                    ev[("xp", tau)] = c_t.inc(inst)

                @block.scalar
                def _(scalar, tau=tau, par=par):
                    W(scalar, "scalar", ev[("xp", tau)],
                      rel.get(("xd_mine", par)))
                    tok = c_s.inc(scalar.activation(out=xd_mine[:, par, :],
                                                    in_=ps_sm[0:96, 0, :],
                                                    func=AF.Copy))
                    ev[("xdev", tau)] = tok
                    rel["ps_xp"] = tok

                @block.gpsimd
                def _(gpsimd, tau=tau, par=par, t0=t0, t1=t1):
                    W(gpsimd, "gpsimd", ev[("xdev", tau)])
                    tok = bdma(gpsimd, g_dma,
                               lambda k, par=par, t0=t0, t1=t1:
                               (xdbl_shr[k, :, t0:t1], xd_mine[:, par, :]))
                    gpsimd.wait_ge(g_dma.h, g_dma.n)
                    rel[("xd_mine", par)] = tok
                    bdma(gpsimd, g_dma,
                         lambda k, tau=tau:
                         (flg_xd[k, tau, :],
                          nonce_in[0:1, 2 * tau:2 * tau + 2]))
                    poll(gpsimd, lambda k, tau=tau: flg_xd[k, tau, :],
                         lambda k, tau=tau: 2 * tau)
                    W(gpsimd, "gpsimd", rel.get("xd_peer"))
                    ev[("xdr", tau)] = bdma(
                        gpsimd, g_dma,
                        lambda k, t0=t0, t1=t1:
                        (xd_peer[:, :], xdbl_shr[k, :, t0:t1]),
                        use_peer=True)

                @block.vector
                def _(vector, tau=tau, par=par):
                    W(vector, "vector", ev[("xdr", tau)], rel.get("xd_sum"))
                    tok = c_v.inc(vector.tensor_add(
                        xd_sum[:], xd_mine[:, par, :], xd_peer[:, :]))
                    ev[("xdsum", tau)] = tok
                    rel["xd_peer"] = tok

                @block.scalar
                def _(scalar, tau=tau):
                    W(scalar, "scalar", ev[("xdsum", tau)], rel.get("dtraw"))
                    scalar.activation(out=dtraw_bf[:], in_=xd_sum[0:64, :],
                                      func=AF.Copy)
                    tok = c_s.inc(scalar.activation(out=bc_bf[:],
                                                    in_=xd_sum[64:96, :],
                                                    func=AF.Copy))
                    ev[("bcc", tau)] = tok
                    rel["xd_sum"] = tok

                V0 = c_s.now()[1]
                for k in range(16):
                    which = "b" if k < 8 else "c"
                    sp_ = k % 8
                    roff = 0 if which == "b" else 16

                    @block.tensor
                    def _(tensor, tau=tau, k=k, sp_=sp_, roff=roff, V0=V0):
                        W(tensor, "tensor", ev[("bcc", tau)],
                          rel.get("brepcrep") if k == 0 else None)
                        if k >= 1:
                            tensor.wait_ge(c_s.h, V0 + k)
                        for jj in (0, 1):
                            row = roff + 2 * sp_ + jj
                            inst = nc.tensor.matmul(
                                ps_bc[:, jj, :],
                                sel_bf[:, row * 128:(row + 1) * 128],
                                bc_bf[:, :], start=True, stop=True)
                        ev[("bcm", tau, k)] = c_t.inc(inst)

                @block.scalar
                def _(scalar, tau=tau, V0=V0):
                    for k in range(16):
                        dst = brep_bf if k < 8 else crep_bf
                        sp_ = k % 8
                        W(scalar, "scalar", ev[("bcm", tau, k)])
                        tok = c_s.inc(scalar.activation(
                            out=dst[:, 2 * sp_:2 * sp_ + 2, :],
                            in_=ps_bc[:, :, :], func=AF.Copy))
                        assert tok[1] == V0 + k + 1
                        if k >= 8 and k % 2 == 1:
                            ev[("bcg", tau, (k - 9) // 2)] = tok

                def dt_chain(tau, par, c):
                    cp = c % 2

                    @block.tensor
                    def _(tensor, tau=tau, c=c, cp=cp):
                        W(tensor, "tensor", ev[("bcc", tau)],
                          rel.get(("ps_dt", cp)))
                        ev[("dtm", tau, c)] = c_t.inc(nc.tensor.matmul(
                            ps_sm[:, cp, :],
                            wdt_bf[:, c * 128:(c + 1) * 128],
                            dtraw_bf[:, :], start=True, stop=True))

                    @block.scalar
                    def _(scalar, tau=tau, c=c, cp=cp):
                        dtxp = (ev.get(("dtx", tau, c - 2)) if c >= 2 else
                                (ev.get(("dtx", tau - 1, c + 6))
                                 if tau >= 1 else None))
                        W(scalar, "scalar", ev[("dtm", tau, c)],
                          rel.get("t_sp"), dtxp)
                        scalar.activation(out=t_a[:], in_=ps_sm[:, cp, :],
                                          func=AF.Abs, bias=bdt_sb[:, c:c + 1])
                        scalar.activation(out=t_a[:], in_=t_a[:], func=AF.Exp,
                                          scale=-1.0)
                        scalar.activation(out=dt_f[:, cp, :], in_=t_a[:],
                                          func=AF.Ln, bias=1.0)
                        inst = scalar.activation(out=t_re[:, :],
                                                 in_=ps_sm[:, cp, :],
                                                 func=AF.Relu,
                                                 bias=bdt_sb[:, c:c + 1])
                        tok = c_s.inc(inst)
                        ev[("sp", tau, c)] = tok
                        rel[("ps_dt", cp)] = tok

                    @block.vector
                    def _(vector, tau=tau, c=c, cp=cp):
                        W(vector, "vector", ev[("sp", tau, c)])
                        tok = c_v.inc(vector.tensor_add(
                            dt_f[:, cp, :], dt_f[:, cp, :], t_re[:, :]))
                        ev[("dt", tau, c)] = tok
                        rel["t_sp"] = tok

                    @block.vector
                    def _(vector, tau=tau, par=par, c=c, cp=cp):
                        W(vector, "vector", ev[("xc", tau, c)])
                        ev[("dtx", tau, c)] = c_v.inc(vector.tensor_mul(
                            dtx_bf[:, cp, :], dt_f[:, cp, :],
                            xc_bf[:, par, c, :]))

                dt_chain(tau, par, 0)
                dt_chain(tau, par, 1)
                for c in range(NCH):
                    cp = c % 2

                    def da_block(gs, ge, tau=tau, c=c, cp=cp):
                        @block.scalar
                        def _(scalar, gs=gs, ge=ge, tau=tau, c=c, cp=cp):
                            for g in range(gs, ge):
                                p2 = (tau * NCH * NG + c * NG + g) % 3
                                W(scalar, "scalar",
                                  ev[("dt", tau, c)] if g == 0 else None,
                                  rel.get(("da", p2)))
                                for j in range(GS):
                                    s = g * GS + j
                                    inst = scalar.activation(
                                        out=da_f[:, p2,
                                                 j * T1 + 2:j * T1 + 2 + TT],
                                        in_=dt_f[:, cp, :], func=AF.Exp,
                                        scale=a_sb[:, c, s:s + 1])
                                ev[("da", tau, c, g)] = c_s.inc(inst)

                    da_block(0, NG - 1)
                    for g in range(NG):
                        p2 = (tau * NCH * NG + c * NG + g) % 3

                        if g == 1:
                            da_block(NG - 1, NG)
                        if g == NG - 1 and c + 2 < NCH:
                            dt_chain(tau, par, c + 2)

                        @block.vector
                        def _(vector, tau=tau, c=c, g=g, p2=p2, cp=cp):
                            W(vector, "vector", ev[("da", tau, c, g)],
                              ev[("dtx", tau, c)], ev[("bcg", tau, g)],
                              rel.get(("yc", (c * NG + g) % 2)))
                            for j in range(GS):
                                s = g * GS + j
                                vector.tensor_mul(
                                    dbx_bf[:, j * T1 + 2:j * T1 + 2 + TT],
                                    dtx_bf[:, cp, :], brep_bf[:, s, :])
                                vector.tensor_copy(
                                    dbx_bf[:, j * T1:j * T1 + 1],
                                    carr[:, c, s:s + 1])
                            vector.tensor_tensor_scan(
                                out=h_bf[:], data0=da_f[:, p2, :],
                                data1=dbx_bf[:], initial=0.0,
                                op0=ALU.mult, op1=ALU.add)
                            vector.tensor_copy(
                                carr[:, c, g * GS:(g + 1) * GS].unsqueeze(2),
                                h_bf[:].rearrange("p (s t) -> p s t", s=GS)
                                [:, :, T1 - 1:T1])
                            inst = vector.tensor_mul(
                                yc_bf[:, (c * NG + g) % 2, :, :],
                                h_bf[:].rearrange("p (s t) -> p s t", s=GS)
                                [:, :, 2:2 + TT],
                                crep_bf[:, g * GS:(g + 1) * GS, :])
                            tok = c_v.inc(inst)
                            ev[("scan", tau, c, g)] = tok
                            rel[("da", p2)] = tok
                            if c == NCH - 1 and g == NG - 1:
                                rel["brepcrep"] = tok

                        @block.tensor
                        def _(tensor, tau=tau, c=c, g=g):
                            W(tensor, "tensor", ev[("scan", tau, c, g)],
                              rel.get(("ps_y", c % 2)) if g == 0 else None)
                            for j in range(GS):
                                inst = nc.tensor.matmul(
                                    ps_y[:, c % 2, :], ident_bf[:, :],
                                    yc_bf[:, (c * NG + g) % 2, j, :],
                                    start=(g == 0 and j == 0),
                                    stop=(g == NG - 1 and j == GS - 1))
                            tok = c_t.inc(inst)
                            ev[("yacc", tau, c, g)] = tok
                            rel[("yc", (c * NG + g) % 2)] = tok

                    @block.vector
                    def _(vector, tau=tau, par=par, c=c):
                        W(vector, "vector", ev[("yacc", tau, c, NG - 1)],
                          rel.get(("yfin", c % 2)))
                        vector.scalar_tensor_tensor(
                            out=epi_f[:], in0=xc_bf[:, par, c, :],
                            scalar=dp_sb[:, c:c + 1], in1=ps_y[:, c % 2, :],
                            op0=ALU.mult, op1=ALU.add)
                        tok = c_v.inc(vector.tensor_mul(
                            yfin_bf[:, c % 2, :], epi_f[:],
                            zs_bf[:, par, c, :]))
                        ev[("yfin", tau, c)] = tok
                        rel[("ps_y", c % 2)] = tok
                        if c == NCH - 1:
                            rel[("xc", par)] = tok
                            rel[("zs", par)] = tok

                    @block.gpsimd
                    def _(gpsimd, tau=tau, c=c, t0=t0, t1=t1):
                        W(gpsimd, "gpsimd", ev[("yfin", tau, c)])
                        tok = bdma(gpsimd, g_dma,
                                   lambda k, c=c, t0=t0, t1=t1:
                                   (y_shr[k, c, :, t0:t1],
                                    yfin_bf[:, c % 2, :]))
                        ev[("yw", tau, c)] = tok
                        rel[("yfin", c % 2)] = tok

                if tau == NT - 1:
                    @block.gpsimd
                    def _(gpsimd):
                        gpsimd.wait_ge(g_dma.h, g_dma.n)
                        bdma(gpsimd, g_dma,
                             lambda k: (flg_y[k, :],
                                        nonce_in[0:1, 32:34]))

                if BARRIER_TILES:
                    barrier_all(("mt", tau))

            m_end = {"scalar": c_s.now(), "vector": c_v.now(),
                     "tensor": c_t.now(), "gpsimd": (g_dma, g_dma.n),
                     "sync": s_dma.now()}

        def barrier(ends, tag):
            toks = {}
            for name in ("scalar", "vector", "tensor", "gpsimd", "sync"):
                @getattr(block, name)
                def _(eng, name=name):
                    for oname, tok in ends.items():
                        if oname != name:
                            eng.wait_ge(tok[0].h, tok[1])
                    if name == "scalar":
                        toks["t"] = c_s.inc(eng.nop())
            return toks["t"]

        bar_m = barrier(m_end, "m")

        with ExitStack() as st3:
            A3, P3 = mk(st3)
            wout_bf = A3("wout_bf", [128, 16, DM], BF16)
            wst3 = A3("wst3", [128, 2, DM], F32)
            y_in = A3("y_in", [128, 2, 16, TT], BF16)
            x3_st = A3("x3_st", [128, 2, NDM, TT], F32)
            v_f = A3("v_f", [128, NDM, TT], F32)
            v_bf = A3("v_bf", [128, NDM, TT], BF16)
            vsq_bf = A3("vsq_bf", [128, NDM, TT], BF16)
            var_f = A3("var_f", [128, TT], F32)
            r1_f = A3("r1_f", [128, TT], F32)
            r2_f = A3("r2_f", [128, TT], F32)
            r12_f = A3("r12_f", [128, TT], F32)
            tvmu = A3("tvmu", [128, NDM, TT], F32)
            x2_f = A3("x2_f", [128, NDM, TT], F32)
            x2n_bf = A3("x2n_bf", [128, NDM, TT], BF16)

            ps_o = P3("ps_o", [128, 2, TT])
            ps_st = P3("ps_st", [128, 2, TT])

            for part, src in (("lo", wout_lo), ("hi", wout_hi)):
                for k in range(NCH):
                    kk = (0 if part == "lo" else NCH) + k

                    @block.sync
                    def _(sync, src=src, k=k, kk=kk):
                        W(sync, "sync", bar_m, rel.get(("wst3", kk % 2)))
                        ev[("w3", kk)] = s_dma.inc(sync.dma_start(
                            out=wst3[:, kk % 2, :],
                            in_=src[k * 128:(k + 1) * 128, :]))

                    @block.scalar
                    def _(scalar, kk=kk):
                        W(scalar, "scalar", ev[("w3", kk)], bar_m)
                        rel[("wst3", kk % 2)] = c_s.inc(scalar.activation(
                            out=wout_bf[:, kk, :], in_=wst3[:, kk % 2, :],
                            func=AF.Copy))
            ev["w3d"] = c_s.now()

            @block.gpsimd
            def _(gpsimd):
                for oname, tok in m_end.items():
                    if oname != "gpsimd":
                        gpsimd.wait_ge(tok[0].h, tok[1])
                poll(gpsimd, lambda k: flg_y[k, :], lambda k: 32)
                ev["ypeer"] = c_g.inc(gpsimd.nop())

            for t3 in range(NT3):
                par = t3 % 2

                @block.gpsimd
                def _(gpsimd, t3=t3, par=par):
                    W(gpsimd, "gpsimd", ev["ypeer"], m_end["gpsimd"],
                      rel.get(("y_in", par)), rel.get(("x3", par)))
                    for hh in (0, 1):
                        with if_slot(gpsimd, hh):
                            a0 = hh * TOK + t3 * TT
                            gpsimd.dma_start(
                                out=y_in[:, par, :, :],
                                in_=y_shr[:, :, :, a0:a0 + TT].rearrange(
                                    "s c p t -> p (s c) t")
                            ).then_inc(g_dma.h, 16)
                            gpsimd.dma_start(
                                out=x3_st[:, par, :, :],
                                in_=xT[:, a0:a0 + TT].rearrange(
                                    "(c p) t -> p c t", p=128)
                            ).then_inc(g_dma.h, 16)
                        gpsimd.end_ifs()
                    g_dma.n += 32
                    ev[("y3", t3)] = g_dma.now()

                for i in range(NDM):
                    @block.tensor
                    def _(tensor, t3=t3, par=par, i=i):
                        W(tensor, "tensor", ev[("y3", t3)], ev["w3d"],
                          rel.get(("ps_o", i % 2)))
                        for k in range(16):
                            inst = nc.tensor.matmul(
                                ps_o[:, i % 2, :],
                                wout_bf[:, k, i * 128:(i + 1) * 128],
                                y_in[:, par, k, :],
                                start=(k == 0), stop=(k == 15))
                        ev[("om", t3, i)] = c_t.inc(inst)

                    @block.vector
                    def _(vector, t3=t3, par=par, i=i):
                        W(vector, "vector", ev[("om", t3, i)], ev[("y3", t3)],
                          ev.get(("x2n", t3 - 1)),
                          ev.get(("m3w", t3 - 1)))
                        tok = c_v.inc(vector.tensor_add(
                            v_f[:, i, :], x3_st[:, par, i, :],
                            ps_o[:, i % 2, :]))
                        ev[("v", t3, i)] = tok
                        rel[("ps_o", i % 2)] = tok
                        if i == NDM - 1:
                            rel[("x3", par)] = tok
                            rel[("y_in", par)] = tok

                @block.scalar
                def _(scalar, t3=t3):
                    W(scalar, "scalar", [ev[("v", t3, i)] for i in range(NDM)],
                      ev.get(("stat", t3 - 1)))
                    scalar.activation(out=v_bf[:], in_=v_f[:], func=AF.Copy)
                    ev[("vbf", t3)] = c_s.inc(scalar.activation(
                        out=vsq_bf[:], in_=v_f[:], func=AF.Square))

                @block.tensor
                def _(tensor, t3=t3):
                    W(tensor, "tensor", ev[("vbf", t3)], rel.get("ps_st"))
                    for k in range(NDM):
                        nc.tensor.matmul(ps_st[:, 0, :], onesh_bf[:, :],
                                         v_bf[:, k, :], start=(k == 0),
                                         stop=(k == NDM - 1))
                    for k in range(NDM):
                        inst = nc.tensor.matmul(ps_st[:, 1, :], onesh_bf[:, :],
                                                vsq_bf[:, k, :],
                                                start=(k == 0),
                                                stop=(k == NDM - 1))
                    ev[("stat", t3)] = c_t.inc(inst)

                @block.vector
                def _(vector, t3=t3):
                    W(vector, "vector", ev[("stat", t3)],
                      ev.get(("r1", t3 - 1)))
                    vector.tensor_copy(var_f[:], ps_st[:, 0, :])
                    vector.tensor_mul(var_f[:], var_f[:], ps_st[:, 0, :])
                    ev[("var", t3)] = c_v.inc(vector.tensor_sub(
                        var_f[:], ps_st[:, 1, :], var_f[:]))

                @block.scalar
                def _(scalar, t3=t3):
                    W(scalar, "scalar", ev[("var", t3)])
                    scalar.activation(out=r1_f[:], in_=var_f[:], func=AF.Ln,
                                      bias=EPS)
                    ev[("r1", t3)] = c_s.inc(scalar.activation(
                        out=r1_f[:], in_=r1_f[:], func=AF.Exp, scale=-0.5))

                @block.vector
                def _(vector, t3=t3):
                    W(vector, "vector", ev[("r1", t3)],
                      ev.get(("r2", t3 - 1)))
                    vector.tensor_mul(r2_f[:], r1_f[:], r1_f[:])
                    ev[("var2", t3)] = c_v.inc(vector.tensor_mul(
                        r2_f[:], r2_f[:], var_f[:]))

                @block.scalar
                def _(scalar, t3=t3):
                    W(scalar, "scalar", ev[("var2", t3)])
                    scalar.activation(out=r2_f[:], in_=r2_f[:], func=AF.Ln,
                                      bias=EPS)
                    ev[("r2", t3)] = c_s.inc(scalar.activation(
                        out=r2_f[:], in_=r2_f[:], func=AF.Exp, scale=-0.5))

                @block.vector
                def _(vector, t3=t3):
                    W(vector, "vector", ev[("r2", t3)],
                      ev.get(("x2n", t3 - 1)), ev.get(("m3w", t3 - 1)))
                    vector.tensor_mul(r12_f[:], r1_f[:], r2_f[:])
                    for i in range(NDM):
                        vector.scalar_tensor_tensor(
                            out=tvmu[:, i, :], in0=v_f[:, i, :], scalar=1.0,
                            in1=ps_st[:, 0, :], op0=ALU.mult, op1=ALU.subtract)
                        vector.tensor_mul(x2_f[:, i, :], tvmu[:, i, :], r1_f[:])
                        inst = vector.tensor_mul(tvmu[:, i, :], tvmu[:, i, :],
                                                 r12_f[:])
                    tok = c_v.inc(inst)
                    ev[("x2", t3)] = tok
                    rel["v_f"] = tok
                    rel["ps_st"] = tok

                @block.scalar
                def _(scalar, t3=t3):
                    W(scalar, "scalar", ev[("x2", t3)])
                    for i in range(NDM):
                        inst = scalar.activation(out=x2n_bf[:, i, :],
                                                 in_=tvmu[:, i, :],
                                                 func=AF.Identity,
                                                 scale=gam_sb[:, i:i + 1],
                                                 bias=bet_sb[:, i:i + 1])
                    ev[("x2n", t3)] = c_s.inc(inst)

                @block.sync
                def _(sync, t3=t3):
                    W(sync, "sync", ev[("x2n", t3)])
                    s_dma.inc(sync.dma_start(
                        out=x2_d[:, :, t3 * TT:(t3 + 1) * TT].transpose(
                            [1, 0, 2]),
                        in_=x2_f[:]))
                    ev[("m3w", t3)] = s_dma.inc(sync.dma_start(
                        out=x2n_d[:, :, t3 * TT:(t3 + 1) * TT].transpose(
                            [1, 0, 2]),
                        in_=x2n_bf[:]))

                if BARRIER_TILES:
                    barrier_all(("m3t", t3))

            m3_end = {"scalar": c_s.now(), "vector": c_v.now(),
                      "tensor": c_t.now(), "gpsimd": (g_dma, g_dma.n),
                      "sync": s_dma.now()}

        bar_3 = barrier(m3_end, "m3")

        with ExitStack() as stf:
            Af, Pf = mk(stf)
            w1_bf = Af("w1_bf", [128, NDM, FF], BF16)
            w2_bf = Af("w2_bf", [128, NFF, DM], BF16)

            with ExitStack() as stq:
                Aq, _ = mk(stq)
                qst = Aq("qst", [128, 2048], I32)
                qf = Aq("qf", [128, 2048], F32)
                scb = Aq("scb", [128, FF], F32)
                sel_f = Aq("sel_f", [32, 32 * 128], F32)
                sc_sb = Aq("sc_sb", [32, FF], F32)
                ps_sc = stq.enter_context(
                    nc.psum_tensor("ps_sc", [128, TT], F32))

                @block.sync
                def _(sync):
                    W(sync, "sync", bar_3)
                    s_dma.inc(sync.dma_start(out=sel_f[:], in_=selmat[:, :]))
                    ev["scl"] = s_dma.inc(sync.dma_start(
                        out=sc_sb[0:8, 0:FF], in_=s1[:, :]))

                for wi, (qsrc, nk, fdim, dst, nrows) in enumerate(
                        [(q1, NDM, FF, w1_bf, 8), (q2, NFF, DM, w2_bf, 32)]):
                    if wi == 1:
                        @block.sync
                        def _(sync):
                            W(sync, "sync", ev["w1done"])
                            ev["scl"] = s_dma.inc(sync.dma_start(
                                out=sc_sb[0:32, 0:DM], in_=s2[:, :]))
                    for g in range(nk):
                        for jj in range(fdim // TT):
                            @block.tensor
                            def _(tensor, g=g, jj=jj, nrows=nrows):
                                W(tensor, "tensor", ev["scl"], bar_3,
                                  rel.get("ps_sc"))
                                ev["scm"] = c_t.inc(nc.tensor.matmul(
                                    ps_sc[:, :],
                                    sel_f[0:nrows, g * 128:(g + 1) * 128],
                                    sc_sb[0:nrows, jj * TT:(jj + 1) * TT],
                                    start=True, stop=True))

                            @block.scalar
                            def _(scalar, jj=jj):
                                W(scalar, "scalar", ev["scm"], rel.get("scb"))
                                rel["ps_sc"] = c_s.inc(scalar.activation(
                                    out=scb[:, jj * TT:(jj + 1) * TT],
                                    in_=ps_sc[:, :], func=AF.Copy))
                        for fh in range((fdim + 2047) // 2048):
                            w0 = fh * 2048
                            w1w = min(fdim - w0, 2048)

                            @block.sync
                            def _(sync, qsrc=qsrc, g=g, w0=w0, w1w=w1w):
                                W(sync, "sync", bar_3, rel.get("qst"))
                                ev["ql"] = s_dma.inc(sync.dma_start(
                                    out=qst[:, 0:w1w],
                                    in_=qsrc[g * 128:(g + 1) * 128,
                                             w0:w0 + w1w]))

                            @block.scalar
                            def _(scalar, w1w=w1w):
                                W(scalar, "scalar", ev["ql"], rel.get("qf"))
                                ev["qd"] = c_s.inc(scalar.activation(
                                    out=qf[:, 0:w1w], in_=qst[:, 0:w1w],
                                    func=AF.Copy, bias=-128.0))

                            @block.vector
                            def _(vector, g=g, w0=w0, w1w=w1w, dst=dst):
                                W(vector, "vector", ev["qd"], rel.get("ps_sc"))
                                tok = c_v.inc(vector.tensor_mul(
                                    dst[:, g, w0:w0 + w1w], qf[:, 0:w1w],
                                    scb[:, w0:w0 + w1w]))
                                rel["qst"] = tok
                                rel["qf"] = tok
                                if w0 + w1w == fdim:
                                    rel["scb"] = tok
                    if wi == 0:
                        ev["w1done"] = c_v.now()
                ev["fw"] = c_v.now()

            act_bf = Af("act_bf", [128, 16, TT], BF16)
            z2a = Af("z2a", [128, NDM, TT], F32)
            x2r = Af("x2r", [128, NDM, TT], F32)
            x2nr = Af("x2nr", [128, NDM, TT], BF16)
            f_ta = Af("f_ta", [128, TT], F32)
            f_tb = Af("f_tb", [128, TT], F32)
            f_tc = Af("f_tc", [128, TT], F32)
            fvar = Af("fvar", [128, TT], F32)
            fr = Af("fr", [128, TT], F32)

            ps_f1 = Pf("ps_f1", [128, 4, TT])
            ps_f2 = Pf("ps_f2", [128, 2, TT])
            ps_fs = Pf("ps_fs", [128, 2, TT])

            for t3 in range(NT3):
                @block.sync
                def _(sync, t3=t3):
                    W(sync, "sync", ev[("m3w", t3)], rel.get("x2nr"))
                    s_dma.inc(sync.dma_start(
                        out=x2nr[:, :, :],
                        in_=x2n_d[:, :, t3 * TT:(t3 + 1) * TT].rearrange(
                            "c p t -> p c t")))
                    ev[("fx", t3)] = s_dma.inc(sync.dma_start(
                        out=x2r[:, :, :],
                        in_=x2_d[:, :, t3 * TT:(t3 + 1) * TT].rearrange(
                            "c p t -> p c t")))

                for halfF in range(2):
                    for fi in range(16):
                        f = halfF * 16 + fi

                        @block.tensor
                        def _(tensor, t3=t3, f=f, fi=fi):
                            W(tensor, "tensor", ev[("fx", t3)], ev["fw"],
                              rel.get(("ps_f1", fi % 4)))
                            for k in range(NDM):
                                inst = nc.tensor.matmul(
                                    ps_f1[:, fi % 4, :],
                                    w1_bf[:, k, f * 128:(f + 1) * 128],
                                    x2nr[:, k, :],
                                    start=(k == 0), stop=(k == NDM - 1))
                            ev[("f1", t3, f)] = c_t.inc(inst)

                        @block.scalar
                        def _(scalar, t3=t3, f=f, fi=fi):
                            W(scalar, "scalar", ev[("f1", t3, f)],
                              rel.get("f_t"))
                            inst = scalar.activation(out=f_tb[:],
                                                     in_=ps_f1[:, fi % 4, :],
                                                     func=AF.Sigmoid,
                                                     bias=b1_sb[:, f:f + 1])
                            ev[("fs", t3, f)] = c_s.inc(inst)

                        @block.vector
                        def _(vector, t3=t3, f=f, fi=fi):
                            W(vector, "vector", ev[("fs", t3, f)],
                              rel.get(("act", fi)))
                            vector.tensor_scalar_add(
                                f_tc[:], ps_f1[:, fi % 4, :],
                                b1_sb[:, f:f + 1])
                            tok = c_v.inc(vector.tensor_mul(
                                act_bf[:, fi, :], f_tc[:], f_tb[:]))
                            ev[("act", t3, f)] = tok
                            rel["f_t"] = tok
                            rel[("ps_f1", fi % 4)] = tok

                    for i in range(NDM):
                        @block.tensor
                        def _(tensor, t3=t3, halfF=halfF, i=i):
                            W(tensor, "tensor",
                              [ev[("act", t3, halfF * 16 + fk)]
                               for fk in range(16)],
                              rel.get(("ps_f2", i % 2)))
                            for fk in range(16):
                                inst = nc.tensor.matmul(
                                    ps_f2[:, i % 2, :],
                                    w2_bf[:, halfF * 16 + fk,
                                          i * 128:(i + 1) * 128],
                                    act_bf[:, fk, :],
                                    start=(fk == 0), stop=(fk == 15))
                            tok = c_t.inc(inst)
                            ev[("f2", t3, halfF, i)] = tok
                            if i == NDM - 1:
                                for fi2 in range(16):
                                    rel[("act", fi2)] = tok

                        if halfF == 0:
                            @block.scalar
                            def _(scalar, t3=t3, i=i):
                                W(scalar, "scalar", ev[("f2", t3, 0, i)],
                                  rel.get(("z2a", i)))
                                tok = c_s.inc(scalar.activation(
                                    out=z2a[:, i, :], in_=ps_f2[:, i % 2, :],
                                    func=AF.Identity, scale=0.5,
                                    bias=b2h_sb[:, i:i + 1]))
                                ev[("z2a", t3, i)] = tok
                                rel[("ps_f2", i % 2)] = tok
                        else:
                            @block.vector
                            def _(vector, t3=t3, i=i):
                                W(vector, "vector", ev[("f2", t3, 1, i)],
                                  ev[("z2a", t3, i)])
                                vector.scalar_tensor_tensor(
                                    out=z2a[:, i, :], in0=ps_f2[:, i % 2, :],
                                    scalar=0.5, in1=z2a[:, i, :],
                                    op0=ALU.mult, op1=ALU.add)
                                tok = c_v.inc(vector.scalar_tensor_tensor(
                                    out=z2a[:, i, :], in0=x2r[:, i, :],
                                    scalar=2.0, in1=z2a[:, i, :],
                                    op0=ALU.mult, op1=ALU.add))
                                ev[("u", t3, i)] = tok
                                rel[("ps_f2", i % 2)] = tok

                @block.scalar
                def _(scalar, t3=t3):
                    W(scalar, "scalar", [ev[("u", t3, i)] for i in range(NDM)],
                      ev.get(("fst", t3 - 1)))
                    scalar.activation(out=act_bf[:, 0:NDM, :],
                                      in_=z2a[:], func=AF.Copy)
                    ev[("ub", t3)] = c_s.inc(scalar.activation(
                        out=act_bf[:, NDM:2 * NDM, :],
                        in_=z2a[:], func=AF.Square))

                @block.tensor
                def _(tensor, t3=t3):
                    W(tensor, "tensor", ev[("ub", t3)], rel.get("ps_fs"))
                    for k in range(NDM):
                        nc.tensor.matmul(ps_fs[:, 0, :], onesh_bf[:, :],
                                         act_bf[:, k, :], start=(k == 0),
                                         stop=(k == NDM - 1))
                    for k in range(NDM):
                        inst = nc.tensor.matmul(ps_fs[:, 1, :], onesh_bf[:, :],
                                                act_bf[:, NDM + k, :],
                                                start=(k == 0),
                                                stop=(k == NDM - 1))
                    ev[("fst", t3)] = c_t.inc(inst)

                @block.vector
                def _(vector, t3=t3):
                    W(vector, "vector", ev[("fst", t3)],
                      ev.get(("fr", t3 - 1)))
                    vector.tensor_copy(fvar[:], ps_fs[:, 0, :])
                    vector.tensor_mul(fvar[:], fvar[:], ps_fs[:, 0, :])
                    ev[("fvar", t3)] = c_v.inc(vector.tensor_sub(
                        fvar[:], ps_fs[:, 1, :], fvar[:]))

                @block.scalar
                def _(scalar, t3=t3):
                    W(scalar, "scalar", ev[("fvar", t3)])
                    scalar.activation(out=fr[:], in_=fvar[:], func=AF.Ln,
                                      bias=EPS)
                    ev[("fr", t3)] = c_s.inc(scalar.activation(
                        out=fr[:], in_=fr[:], func=AF.Exp, scale=-0.5))

                @block.vector
                def _(vector, t3=t3):
                    W(vector, "vector", ev[("fr", t3)],
                      ev.get(("outw", t3 - 1)))
                    for i in range(NDM):
                        vector.scalar_tensor_tensor(
                            out=x2r[:, i, :], in0=z2a[:, i, :], scalar=1.0,
                            in1=ps_fs[:, 0, :], op0=ALU.mult, op1=ALU.subtract)
                        inst = vector.tensor_mul(x2r[:, i, :], x2r[:, i, :],
                                                 fr[:])
                    tok = c_v.inc(inst)
                    ev[("fout", t3)] = tok
                    rel["ps_fs"] = tok
                    for i in range(NDM):
                        rel[("z2a", i)] = tok

                @block.sync
                def _(sync, t3=t3):
                    W(sync, "sync", ev[("fout", t3)])
                    tok = s_dma.inc(sync.dma_start(
                        out=out_d[:, :, t3 * TT:(t3 + 1) * TT].transpose(
                            [1, 0, 2]),
                        in_=x2r[:]))
                    ev[("outw", t3)] = tok
                    rel["x2nr"] = tok

                if BARRIER_TILES:
                    barrier_all(("ft", t3))

            @block.sync
            def _(sync):
                sync.wait_ge(s_dma.h, s_dma.n)

            @block.gpsimd
            def _(gpsimd):
                gpsimd.wait_ge(s_dma.h, s_dma.n)

    return nc


_CACHE = {}


def kernel(**inputs):
    if "nc" not in _CACHE:
        _CACHE["nc"] = build_nc()
    nc = _CACHE["nc"]

    gi = lambda k: np.ascontiguousarray(inputs[k])
    x = gi("L0_attn_x")
    W_in = gi("L0_attn_W_in")
    conv_w = gi("L0_attn_conv_w")
    conv_b = gi("L0_attn_conv_b")
    W_xproj = gi("L0_attn_W_xproj")
    W_dt = gi("L0_attn_W_dt")
    b_dt = gi("L0_attn_b_dt")
    A_log = gi("L0_attn_A_log")
    Dp = gi("L0_attn_Dp")
    W_out = gi("L0_attn_W_out")
    ln_g = gi("L0_ffn_ln_gamma")
    ln_b = gi("L0_ffn_ln_beta")
    fc1_q = gi("L0_ffn_fc1_q")
    fc1_s = gi("L0_ffn_fc1_scales")
    fc1_b = gi("L0_ffn_fc1_bias")
    fc2_q = gi("L0_ffn_fc2_q")
    fc2_s = gi("L0_ffn_fc2_scales")
    fc2_b = gi("L0_ffn_fc2_bias")

    selm = np.zeros((32, 32 * 128), dtype=np.float32)
    for s in range(32):
        selm[s, s * 128:(s + 1) * 128] = 1.0
    identm = np.eye(128, dtype=np.float32)
    onesh = np.full((128, 128), 1.0 / DM, dtype=np.float32)
    da01 = np.tile(np.array([0.0, 1.0], np.float32), (128, 1))

    rng_n = np.random.default_rng()
    nonces = [rng_n.integers(1, 2**32 - 1, size=(1, 64), dtype=np.uint32)
              for _ in range(4)]
    in_maps = []
    for c in range(8):
        b, h = c // 2, c % 2
        ch0 = h * CH
        im = {
            "xT": np.ascontiguousarray(x[b].T),
            "winx": np.ascontiguousarray(W_in[:, ch0:ch0 + CH]),
            "winz": np.ascontiguousarray(W_in[:, 2048 + ch0:2048 + ch0 + CH]),
            "wxp": np.ascontiguousarray(W_xproj[ch0:ch0 + CH, :]),
            "wdt": np.ascontiguousarray(W_dt[:, ch0:ch0 + CH]),
            "convb": np.ascontiguousarray(
                conv_b[ch0:ch0 + CH].reshape(NCH, 128).T),
            "bdt": np.ascontiguousarray(b_dt[ch0:ch0 + CH].reshape(NCH, 128).T),
            "alog": np.ascontiguousarray(
                A_log[ch0:ch0 + CH].reshape(NCH, 128, DS).transpose(1, 0, 2)),
            "dp": np.ascontiguousarray(Dp[ch0:ch0 + CH].reshape(NCH, 128).T),
            "wout_lo": np.ascontiguousarray(W_out[0:CH, :]),
            "wout_hi": np.ascontiguousarray(W_out[CH:2 * CH, :]),
            "gam": np.ascontiguousarray(ln_g.reshape(NDM, 128).T),
            "bet": np.ascontiguousarray(ln_b.reshape(NDM, 128).T),
            "q1": fc1_q,
            "s1": fc1_s,
            "b1": np.ascontiguousarray(fc1_b.reshape(NFF, 128).T),
            "q2": fc2_q,
            "s2": fc2_s,
            "b2": np.ascontiguousarray(fc2_b.reshape(NDM, 128).T),
            "selmat": selm,
            "identm": identm,
            "onesh": onesh,
            "da01": da01,
            "nonce_in": nonces[b],
        }
        cd = np.zeros((32, 128, 128), dtype=np.float32)
        for cc in range(NCH):
            for k in range(DCONV):
                np.fill_diagonal(cd[cc * 4 + k],
                                 conv_w[ch0 + cc * 128:ch0 + (cc + 1) * 128, k])
        im["convdiag"] = cd
        in_maps.append(im)

    res = run_bass_kernel_spmd(nc, in_maps, core_ids=list(range(8)))

    full = np.empty((B, L, DM), dtype=np.float32)
    for c in range(8):
        b, h = c // 2, c % 2
        o = res.results[c]["out"]
        full[b, h * TOK:(h + 1) * TOK, :] = \
            o.transpose(2, 0, 1).reshape(TOK, DM)
    return full


# version 17
# speedup vs baseline: 1.4457x; 1.1626x over previous
import numpy as np
from contextlib import ExitStack

import concourse.bass as bass
import concourse.mybir as mybir
from concourse.bass_utils import run_bass_kernel_spmd

F32 = mybir.dt.float32
BF16 = mybir.dt.bfloat16
U32 = mybir.dt.uint32
I32 = mybir.dt.int32
AF = mybir.ActivationFunctionType
ALU = mybir.AluOpType
ENG = mybir.EngineType

B, L, DM = 4, 8192, 1024
DS, DCONV, DTR = 16, 4, 64
CH, NCH, NDM = 1024, 8, 8
TT = 512
NT = L // TT
TOK = L // 2
NT3 = TOK // TT
T1 = TT + 2
GS, NG = 4, 4
FF, NFF = 4096, 32
EPS = 1e-5


class Sem:
    def __init__(self, stack, nc, name, engine=None):
        self.h = stack.enter_context(nc.semaphore(name))
        self.n = 0
        self.engine = engine
        self.dk = 1 if engine else 16

    def inc(self, instr, k=None):
        k = self.dk if k is None else k
        instr.then_inc(self.h, k)
        self.n += k
        return (self, self.n)

    def now(self):
        return (self, self.n)


def _flat(evs):
    for e in evs:
        if e is None:
            continue
        if isinstance(e, list):
            yield from _flat(e)
        else:
            yield e


def build_nc():
    nc = bass.Bass("TRN2", target_bir_lowering=False)

    def din(name, shape, dt=F32):
        return nc.dram_tensor(name, shape, dt, kind="ExternalInput").ap()

    xT = din("xT", [DM, L])
    winx = din("winx", [DM, CH])
    winz = din("winz", [DM, CH])
    wxp = din("wxp", [CH, 96])
    wdt_in = din("wdt", [DTR, CH])
    convdiag = din("convdiag", [32, 128, 128])
    convb = din("convb", [128, NCH])
    bdt = din("bdt", [128, NCH])
    alog = din("alog", [128, NCH, DS])
    dp_in = din("dp", [128, NCH])
    wout_lo = din("wout_lo", [CH, DM])
    wout_hi = din("wout_hi", [CH, DM])
    gam = din("gam", [128, NDM])
    bet = din("bet", [128, NDM])
    q1 = din("q1", [DM, FF], I32)
    s1 = din("s1", [8, FF])
    b1 = din("b1", [128, NFF])
    q2 = din("q2", [FF, DM], I32)
    s2 = din("s2", [32, DM])
    b2 = din("b2", [128, NDM])
    selmat = din("selmat", [32, 32 * 128])
    identm = din("identm", [128, 128])
    onesh = din("onesh", [128, 128])
    da01_in = din("da01", [128, 2])
    nonce_in = din("nonce_in", [1, 64], U32)

    out_d = nc.dram_tensor("out", [NDM, 128, TOK], F32, kind="ExternalOutput").ap()
    x2_d = nc.dram_tensor("x2_d", [NDM, 128, TOK], F32, kind="Internal").ap()
    x2n_d = nc.dram_tensor("x2n_d", [NDM, 128, TOK], BF16, kind="Internal").ap()

    xdbl_shr = nc.dram_tensor("xdbl_shr", [2, 96, L], F32, kind="Internal",
                              addr_space="Shared").ap()
    y_shr = nc.dram_tensor("y_shr", [2, NCH, 128, L], BF16, kind="Internal",
                           addr_space="Shared").ap()
    flg_xd = nc.dram_tensor("flg_xd", [2, NT, 2], U32, kind="Internal",
                            addr_space="Shared").ap()
    flg_y = nc.dram_tensor("flg_y", [2, 2], U32, kind="Internal",
                           addr_space="Shared").ap()

    with ExitStack() as st0:
        def mk(stk):
            def A(name, shape, dt=F32):
                return stk.enter_context(nc.sbuf_tensor(name, shape, dt))

            def P(name, shape):
                return stk.enter_context(nc.psum_tensor(name, shape, F32))
            return A, P

        A0, _ = mk(st0)
        ident_bf = A0("ident_bf", [128, 128], BF16)
        sel_bf = A0("sel_bf", [32, 32 * 128], BF16)
        onesh_bf = A0("onesh_bf", [128, 128], BF16)
        nonce = A0("nonce", [1, 64], F32)
        flagw = A0("flagw", [1, 2], U32)
        gam_sb = A0("gam_sb", [128, NDM], F32)
        bet_sb = A0("bet_sb", [128, NDM], F32)
        b2h_sb = A0("b2h_sb", [128, NDM], F32)
        b1_sb = A0("b1_sb", [128, NFF], F32)
        nb1_sb = A0("nb1_sb", [128, NFF], F32)
        const_sb = A0("const_sb", [128, 2], F32)

        nc.const_aps.register(F32, EPS, const_sb[:, 0:1])
        nc.const_aps.register(F32, -128.0, const_sb[:, 1:2])

        block = st0.enter_context(nc.Block())

        s_dma = Sem(st0, nc, "s_dma")
        g_dma = Sem(st0, nc, "g_dma")
        s_poll = Sem(st0, nc, "s_poll")
        c_s = Sem(st0, nc, "c_s", "scalar")
        c_v = Sem(st0, nc, "c_v", "vector")
        c_t = Sem(st0, nc, "c_t", "tensor")
        c_g = Sem(st0, nc, "c_g", "gpsimd")

        def W(eng, engname, *evs):
            seen = {}
            for sem, val in _flat(evs):
                if val <= 0 or sem.engine == engname:
                    continue
                if seen.get(id(sem), -1) < val:
                    seen[id(sem)] = val
                    eng.wait_ge(sem.h, val)

        ev = {}
        rel = {}
        regs = {}

        @block.gpsimd
        def _(gpsimd):
            pid = nc.partition_id([ENG.Pool])
            r = gpsimd.alloc_register("slot")
            gpsimd.reg_alu(r, gpsimd.to_reg(pid), 1, ALU.bitwise_and)
            regs["slot"] = r
            pt = gpsimd.alloc_register("ptgt")
            gpsimd.reg_mov(pt, 0)
            regs["ptgt"] = pt

        def if_slot(gpsimd, k):
            return gpsimd.If_eq(regs["slot"], k)

        def bdma(gpsimd, sem, mk_pair, use_peer=False):
            for k in (0, 1):
                with if_slot(gpsimd, (1 - k) if use_peer else k):
                    o, i = mk_pair(k)
                    gpsimd.dma_start(out=o, in_=i).then_inc(sem.h, 16)
                gpsimd.end_ifs()
            sem.n += 16
            return (sem, sem.n)

        _pollid = [0]

        def poll(gpsimd, flag_ap_fn, word_fn):
            pt = regs["ptgt"]
            u = str(_pollid[0])
            _pollid[0] += 1
            with (
                gpsimd.register("nd" + u) as nd,
                gpsimd.register("f0" + u) as f0,
                gpsimd.register("f1" + u) as f1,
                gpsimd.register("e0" + u) as e0,
                gpsimd.register("e1" + u) as e1,
                gpsimd.register("t0" + u) as t0,
                gpsimd.register("t1" + u) as t1r,
            ):
                for k in (0, 1):
                    with if_slot(gpsimd, 1 - k):
                        w = word_fn(k)
                        gpsimd.reg_load(e0, nonce[0:1, w:w + 1].bitcast(U32))
                        gpsimd.reg_load(e1, nonce[0:1, w + 1:w + 2].bitcast(U32))
                    gpsimd.end_ifs()
                gpsimd.reg_mov(nd, 1)
                with gpsimd.While(nd):
                    gpsimd.reg_alu(pt, pt, 16, ALU.add)
                    for k in (0, 1):
                        with if_slot(gpsimd, 1 - k):
                            gpsimd.dma_start(out=flagw[0:1, :],
                                             in_=flag_ap_fn(k)
                                             ).then_inc(s_poll.h, 16)
                        gpsimd.end_ifs()
                    gpsimd.wait_ge(s_poll.h, pt)
                    gpsimd.reg_load(f0, flagw[0:1, 0:1])
                    gpsimd.reg_load(f1, flagw[0:1, 1:2])
                    gpsimd.reg_alu(t0, f0, e0, ALU.not_equal)
                    gpsimd.reg_alu(t1r, f1, e1, ALU.not_equal)
                    gpsimd.reg_alu(nd, t0, t1r, ALU.bitwise_or)

        BARRIER_TILES = False

        def barrier_all(tag):
            ends = {"scalar": c_s.now(), "vector": c_v.now(),
                    "tensor": c_t.now(), "gpsimd": (g_dma, g_dma.n),
                    "sync": s_dma.now()}
            marks = {}
            for name in ("scalar", "vector", "tensor", "gpsimd", "sync"):
                sem = {"scalar": c_s, "vector": c_v, "tensor": c_t,
                       "gpsimd": c_g, "sync": None}[name]

                @getattr(block, name)
                def _(eng, name=name, sem=sem):
                    for oname, tok in ends.items():
                        if oname != name:
                            eng.wait_ge(tok[0].h, tok[1])
                    if sem is not None:
                        marks[name] = sem.inc(eng.nop())
            for name in ("scalar", "vector", "tensor", "gpsimd", "sync"):
                @getattr(block, name)
                def _(eng, name=name):
                    for oname, tok in marks.items():
                        if oname != name:
                            eng.wait_ge(tok[0].h, tok[1])

        with ExitStack() as stm:
            Am, Pm = mk(stm)
            winx_bf = Am("winx_bf", [128, NDM, CH], BF16)
            winz_bf = Am("winz_bf", [128, NDM, CH], BF16)
            wxp_bf = Am("wxp_bf", [128, NCH, 96], BF16)
            wdt_bf = Am("wdt_bf", [DTR, CH], BF16)
            cdiag_bf = Am("cdiag_bf", [128, 32, 128], BF16)
            convb_sb = Am("convb_sb", [128, NCH], F32)
            bdt_sb = Am("bdt_sb", [128, NCH], F32)
            a_sb = Am("a_sb", [128, NCH, DS], F32)
            dp_sb = Am("dp_sb", [128, NCH], F32)
            da01_f = Am("da01_f", [128, 2], F32)

            with ExitStack() as stw:
                Aw, _ = mk(stw)
                wstage = Aw("wstage", [128, 2, 8192], F32)

                @block.sync
                def _(sync):
                    for dst, src in [
                        (convb_sb[:], convb[:, :]),
                        (bdt_sb[:], bdt[:, :]),
                        (a_sb[:], alog[:, :, :]),
                        (dp_sb[:], dp_in[:, :]),
                        (da01_f[:], da01_in[:, :]),
                        (nonce[:].bitcast(U32), nonce_in[:, :]),
                        (gam_sb[:], gam[:, :]),
                        (bet_sb[:], bet[:, :]),
                        (b2h_sb[:], b2[:, :]),
                        (b1_sb[:], b1[:, :]),
                        (wstage[0:32, 1, 0:32 * 128], selmat[:, :]),
                        (wstage[:, 0, 0:128], identm[:, :]),
                        (wstage[:, 0, 128:256], onesh[:, :]),
                        (wstage[0:64, 0, 256:256 + CH], wdt_in[:, :]),
                        (wstage[:, 0, 2048:2048 + 32 * 128].rearrange(
                            "p (a c) -> p a c", a=32),
                         convdiag[:, :, :].transpose([1, 0, 2])),
                        (wstage[:, 0, 6144:6144 + NCH * 96].rearrange(
                            "p (c n) -> p c n", c=NCH),
                         wxp[:, :].rearrange("(c p) n -> p c n", p=128)),
                    ]:
                        sync.dma_start(out=dst, in_=src).then_inc(s_dma.h, 16)
                        s_dma.n += 16
                    ev["ld0"] = s_dma.now()

                @block.scalar
                def _(scalar):
                    W(scalar, "scalar", ev["ld0"])
                    scalar.activation(out=ident_bf[:], in_=wstage[:, 0, 0:128],
                                      func=AF.Copy)
                    scalar.activation(out=onesh_bf[:], in_=wstage[:, 0, 128:256],
                                      func=AF.Copy)
                    scalar.activation(out=sel_bf[:],
                                      in_=wstage[0:32, 1, 0:32 * 128],
                                      func=AF.Copy)
                    scalar.activation(out=wdt_bf[:],
                                      in_=wstage[0:64, 0, 256:256 + CH],
                                      func=AF.Copy)
                    scalar.activation(out=cdiag_bf[:].rearrange("p a c -> p (a c)"),
                                      in_=wstage[:, 0, 2048:2048 + 32 * 128],
                                      func=AF.Copy)
                    scalar.activation(out=wxp_bf[:].rearrange("p c n -> p (c n)"),
                                      in_=wstage[:, 0, 6144:6144 + NCH * 96],
                                      func=AF.Copy)
                    scalar.activation(out=a_sb[:], in_=a_sb[:], func=AF.Exp)
                    scalar.activation(out=nb1_sb[:], in_=b1_sb[:], func=AF.Copy,
                                      scale=-1.0)
                    ev["sc0"] = c_s.inc(scalar.activation(
                        out=b2h_sb[:], in_=b2h_sb[:], func=AF.Copy, scale=0.5))
                    rel[("wst", 0)] = ev["sc0"]
                    rel[("wst", 1)] = ev["sc0"]

                for kind, src, dst in (("x", winx, winx_bf), ("z", winz, winz_bf)):
                    for k in range(NDM):
                        @block.sync
                        def _(sync, kind=kind, src=src, k=k):
                            W(sync, "sync", rel.get(("wst", k % 2)))
                            ev[("wl", kind, k)] = s_dma.inc(sync.dma_start(
                                out=wstage[:, k % 2, 0:CH],
                                in_=src[k * 128:(k + 1) * 128, :]))

                        @block.scalar
                        def _(scalar, kind=kind, k=k, dst=dst):
                            W(scalar, "scalar", ev[("wl", kind, k)])
                            rel[("wst", k % 2)] = c_s.inc(scalar.activation(
                                out=dst[:, k, :], in_=wstage[:, k % 2, 0:CH],
                                func=AF.Copy))
                ev["wm"] = c_s.now()
                rel[("x_st", 0)] = ev["wm"]
                rel[("x_st", 1)] = ev["wm"]

                @block.vector
                def _(vector):
                    W(vector, "vector", ev["sc0"])
                    vector.tensor_scalar_mul(a_sb[:], a_sb[:], -1.0)
                    vector.memset(const_sb[:, 0:1], EPS)
                    vector.memset(const_sb[:, 1:2], -128.0)
                    ev["v0"] = c_v.inc(vector.nop())

            xbf = Am("xbf", [128, NDM, TT], BF16)
            xin_bf = Am("xin_bf", [128, 2, NCH, TT + 3], BF16)
            zs_bf = Am("zs_bf", [128, 2, NCH, TT], BF16)
            xc_bf = Am("xc_bf", [128, 2, NCH, TT], BF16)
            dt_f = Am("dt_f", [128, 2, TT], F32)
            dtx_bf = Am("dtx_bf", [128, 2, TT], BF16)
            xd_mine = Am("xd_mine", [96, 2, TT], F32)
            xd_peer = Am("xd_peer", [96, TT], F32)
            xd_sum = Am("xd_sum", [96, TT], F32)
            dtraw_bf = Am("dtraw_bf", [DTR, TT], BF16)
            bc_bf = Am("bc_bf", [32, TT], BF16)
            brep_bf = Am("brep_bf", [128, DS, TT], BF16)
            crep_bf = Am("crep_bf", [128, DS, TT], BF16)
            da_f = Am("da_f", [128, 3, GS * T1], F32)
            dbx_bf = Am("dbx_bf", [128, GS * T1], BF16)
            h_bf = Am("h_bf", [128, GS * T1], BF16)
            yc_bf = Am("yc_bf", [128, 2, GS, TT], BF16)
            carr = Am("carr", [128, NCH, DS], F32)
            t_a = Am("t_a", [128, TT], F32)
            t_re = t_a
            epi_f = Am("epi_f", [128, TT], F32)
            yfin_bf = Am("yfin_bf", [128, 2, TT], BF16)

            ps_xz = Pm("ps_xz", [128, 2, TT])
            ps_conv = ps_xz
            ps_sm = Pm("ps_sm", [128, 2, TT])
            ps_bc = Pm("ps_bc", [128, 2, TT])
            ps_y = Pm("ps_y", [128, 2, TT])

            @block.vector
            def _(vector):
                W(vector, "vector", ev["v0"], ev["wm"])
                vector.memset(carr[:], 0.0)
                vector.memset(dbx_bf[:], 0.0)
                vector.memset(xin_bf[:, :, :, 0:3], 0.0)
                for p2 in range(3):
                    for s in range(GS):
                        vector.tensor_copy(da_f[:, p2, s * T1:s * T1 + 2],
                                           da01_f[:])
                ev["v1"] = c_v.inc(vector.nop())

            Vc_of = {}

            def front_xzt(tau, off_silu, lo, hi):
                par = tau % 2
                t0, t1 = tau * TT, (tau + 1) * TT
                if lo == 0:
                    @block.gpsimd
                    def _(gpsimd, tau=tau, t0=t0):
                        W(gpsimd, "gpsimd", rel.get("xbf"),
                          [ev["wm"], ev["v1"]] if tau == 0 else None)
                        ev[("xbf", tau)] = g_dma.inc(gpsimd.dma_start(
                            out=xbf[:, :, :],
                            in_=xT[:, t0:t0 + TT].rearrange(
                                "(c p) t -> p c t", p=128)))

                    Vc_of[tau] = c_s.now()[1]
                Vc = Vc_of[tau]
                for i in range(lo, hi):
                    @block.tensor
                    def _(tensor, tau=tau, i=i, Vc=Vc, off_silu=off_silu):
                        W(tensor, "tensor", ev[("xbf", tau)],
                          rel.get(("ps_xz", i % 2)) if i < 2 else None)
                        if i >= 2:
                            j = i - 2
                            tensor.wait_ge(c_s.h, Vc + j + 1 if j < 8
                                           else Vc + off_silu + j - 7)
                        wsrc = winx_bf if i < NCH else winz_bf
                        io = i % NCH
                        for k in range(NDM):
                            inst = nc.tensor.matmul(
                                ps_xz[:, i % 2, :],
                                wsrc[:, k, io * 128:(io + 1) * 128],
                                xbf[:, k, :], start=(k == 0), stop=(k == NDM - 1))
                        tok = c_t.inc(inst)
                        ev[("xz", tau, i)] = tok
                        if i == 15:
                            rel["xbf"] = tok


            def front_copy(tau):
                par = tau % 2
                t0, t1 = tau * TT, (tau + 1) * TT
                @block.scalar
                def _(scalar, tau=tau, par=par, Vc=None):
                    Vc = Vc_of[tau]
                    if True:
                            for i2 in range(8):
                                W(scalar, "scalar", ev[("xz", tau, i2)],
                                  rel.get(("xin", par)) if i2 == 0 else None)
                                tok = c_s.inc(scalar.activation(
                                    out=xin_bf[:, par, i2, 3:TT + 3],
                                    in_=ps_xz[:, i2 % 2, :], func=AF.Copy))
                                assert tok[1] == Vc + i2 + 1, (tok, Vc)
                                ev[("xzev", tau, i2)] = tok
                                rel[("ps_xz", i2 % 2)] = tok

            def front_silu(tau, off_silu):
                par = tau % 2
                t0, t1 = tau * TT, (tau + 1) * TT
                @block.scalar
                def _(scalar, tau=tau, par=par):
                    Vc = Vc_of[tau]
                    if True:
                            for i2 in range(8, 16):
                                W(scalar, "scalar", ev[("xz", tau, i2)],
                                  rel.get(("zs", par)) if i2 == 8 else None)
                                tok = c_s.inc(scalar.activation(
                                    out=zs_bf[:, par, i2 - NCH, :],
                                    in_=ps_xz[:, i2 % 2, :], func=AF.Silu))
                                assert tok[1] == Vc + off_silu + i2 - 7, (tok, Vc)
                                ev[("xzev", tau, i2)] = tok
                                rel[("ps_xz", i2 % 2)] = tok

                @block.scalar
                def _(scalar, tau=tau, par=par):
                    if tau >= 1:
                        W(scalar, "scalar", ev[("conv", tau - 1, NCH - 1)])
                        for c in range(NCH):
                            scalar.activation(out=xin_bf[:, par, c, 0:3],
                                              in_=xin_bf[:, 1 - par, c, TT:TT + 3],
                                              func=AF.Copy)
                    ev[("ctx", tau)] = c_s.inc(scalar.nop())


            def front_conv(tau):
                par = tau % 2
                t0, t1 = tau * TT, (tau + 1) * TT
                Vd = c_s.now()[1]
                for c in range(NCH):
                    @block.tensor
                    def _(tensor, tau=tau, par=par, c=c, Vd=Vd):
                        W(tensor, "tensor", ev[("xzev", tau, c)],
                          ev[("ctx", tau)],
                          rel.get(("ps_xz", c % 2)) if c < 2 else None)
                        if c >= 2:
                            tensor.wait_ge(c_s.h, Vd + c - 1)
                        for k in range(DCONV):
                            inst = nc.tensor.matmul(
                                ps_conv[:, c % 2, :], cdiag_bf[:, c * 4 + k, :],
                                xin_bf[:, par, c, k:k + TT],
                                start=(k == 0), stop=(k == DCONV - 1))
                        tok = c_t.inc(inst)
                        ev[("conv", tau, c)] = tok
                        if c == NCH - 1:
                            rel[("xin", par)] = tok

                    if c == NCH - 1:
                        @block.scalar
                        def _(scalar, tau=tau, par=par):
                            for c2 in range(NCH):
                                W(scalar, "scalar", ev[("conv", tau, c2)],
                                  rel.get(("xc", par)) if c2 == 0 else None)
                                tok = c_s.inc(scalar.activation(
                                    out=xc_bf[:, par, c2, :],
                                    in_=ps_conv[:, c2 % 2, :], func=AF.Silu,
                                    bias=convb_sb[:, c2:c2 + 1]))
                                assert tok[1] == Vd + c2 + 1
                                ev[("xc", tau, c2)] = tok
                                rel[("ps_xz", c2 % 2)] = tok


            def front_tail(tau):
                par = tau % 2
                t0, t1 = tau * TT, (tau + 1) * TT
                @block.tensor
                def _(tensor, tau=tau, par=par):
                    W(tensor, "tensor", rel.get("ps_xp"),
                      rel.get(("ps_dt", 0)), rel.get(("ps_dt", 1)),
                      [ev[("xc", tau, c)] for c in range(NCH)])
                    for c in range(NCH):
                        inst = nc.tensor.matmul(
                            ps_sm[0:96, 0, :], wxp_bf[:, c, :],
                            xc_bf[:, par, c, :],
                            start=(c == 0), stop=(c == NCH - 1))
                    ev[("xp", tau)] = c_t.inc(inst)

                @block.scalar
                def _(scalar, tau=tau, par=par):
                    W(scalar, "scalar", ev[("xp", tau)],
                      rel.get(("xd_mine", par)))
                    tok = c_s.inc(scalar.activation(out=xd_mine[:, par, :],
                                                    in_=ps_sm[0:96, 0, :],
                                                    func=AF.Copy))
                    ev[("xdev", tau)] = tok
                    rel["ps_xp"] = tok

                @block.gpsimd
                def _(gpsimd, tau=tau, par=par, t0=t0, t1=t1):
                    W(gpsimd, "gpsimd", ev[("xdev", tau)])
                    tok = bdma(gpsimd, g_dma,
                               lambda k, par=par, t0=t0, t1=t1:
                               (xdbl_shr[k, :, t0:t1], xd_mine[:, par, :]))
                    gpsimd.wait_ge(g_dma.h, g_dma.n)
                    rel[("xd_mine", par)] = tok
                    bdma(gpsimd, g_dma,
                         lambda k, tau=tau:
                         (flg_xd[k, tau, :],
                          nonce_in[0:1, 2 * tau:2 * tau + 2]))
                    poll(gpsimd, lambda k, tau=tau: flg_xd[k, tau, :],
                         lambda k, tau=tau: 2 * tau)
                    W(gpsimd, "gpsimd", rel.get("xd_peer"))
                    ev[("xdr", tau)] = bdma(
                        gpsimd, g_dma,
                        lambda k, t0=t0, t1=t1:
                        (xd_peer[:, :], xdbl_shr[k, :, t0:t1]),
                        use_peer=True)


            for tau in range(NT):
                par = tau % 2
                t0, t1 = tau * TT, (tau + 1) * TT

                if tau == 0:
                    front_xzt(0, 8, 0, 16)
                    front_copy(0)
                    front_silu(0, 8)
                    front_conv(0)
                    front_tail(0)

                @block.vector
                def _(vector, tau=tau, par=par):
                    W(vector, "vector", ev[("xdr", tau)], rel.get("xd_sum"))
                    tok = c_v.inc(vector.tensor_add(
                        xd_sum[:], xd_mine[:, par, :], xd_peer[:, :]))
                    ev[("xdsum", tau)] = tok
                    rel["xd_peer"] = tok

                @block.scalar
                def _(scalar, tau=tau):
                    W(scalar, "scalar", ev[("xdsum", tau)], rel.get("dtraw"))
                    scalar.activation(out=dtraw_bf[:], in_=xd_sum[0:64, :],
                                      func=AF.Copy)
                    tok = c_s.inc(scalar.activation(out=bc_bf[:],
                                                    in_=xd_sum[64:96, :],
                                                    func=AF.Copy))
                    ev[("bcc", tau)] = tok
                    rel["xd_sum"] = tok

                V0 = c_s.now()[1]
                for k in range(16):
                    which = "b" if k < 8 else "c"
                    sp_ = k % 8
                    roff = 0 if which == "b" else 16

                    @block.tensor
                    def _(tensor, tau=tau, k=k, sp_=sp_, roff=roff, V0=V0):
                        W(tensor, "tensor", ev[("bcc", tau)],
                          rel.get("brepcrep") if k == 0 else None)
                        if k >= 1:
                            tensor.wait_ge(c_s.h, V0 + k)
                        for jj in (0, 1):
                            row = roff + 2 * sp_ + jj
                            inst = nc.tensor.matmul(
                                ps_bc[:, jj, :],
                                sel_bf[:, row * 128:(row + 1) * 128],
                                bc_bf[:, :], start=True, stop=True)
                        ev[("bcm", tau, k)] = c_t.inc(inst)

                @block.scalar
                def _(scalar, tau=tau, V0=V0):
                    for k in range(16):
                        dst = brep_bf if k < 8 else crep_bf
                        sp_ = k % 8
                        W(scalar, "scalar", ev[("bcm", tau, k)])
                        tok = c_s.inc(scalar.activation(
                            out=dst[:, 2 * sp_:2 * sp_ + 2, :],
                            in_=ps_bc[:, :, :], func=AF.Copy))
                        assert tok[1] == V0 + k + 1
                        if k >= 8 and k % 2 == 1:
                            ev[("bcg", tau, (k - 9) // 2)] = tok

                def dt_chain_st(tau, par, c):
                    cp = c % 2

                    @block.tensor
                    def _(tensor, tau=tau, c=c, cp=cp):
                        W(tensor, "tensor", ev[("bcc", tau)],
                          rel.get(("ps_dt", cp)))
                        ev[("dtm", tau, c)] = c_t.inc(nc.tensor.matmul(
                            ps_sm[:, cp, :],
                            wdt_bf[:, c * 128:(c + 1) * 128],
                            dtraw_bf[:, :], start=True, stop=True))

                    @block.scalar
                    def _(scalar, tau=tau, c=c, cp=cp):
                        dtxp = (ev.get(("dtx", tau, c - 2)) if c >= 2 else
                                (ev.get(("dtx", tau - 1, c + 6))
                                 if tau >= 1 else None))
                        W(scalar, "scalar", ev[("dtm", tau, c)],
                          rel.get("t_sp"), dtxp)
                        scalar.activation(out=t_a[:], in_=ps_sm[:, cp, :],
                                          func=AF.Abs, bias=bdt_sb[:, c:c + 1])
                        scalar.activation(out=t_a[:], in_=t_a[:], func=AF.Exp,
                                          scale=-1.0)
                        scalar.activation(out=dt_f[:, cp, :], in_=t_a[:],
                                          func=AF.Ln, bias=1.0)
                        inst = scalar.activation(out=t_re[:, :],
                                                 in_=ps_sm[:, cp, :],
                                                 func=AF.Relu,
                                                 bias=bdt_sb[:, c:c + 1])
                        tok = c_s.inc(inst)
                        ev[("sp", tau, c)] = tok
                        rel[("ps_dt", cp)] = tok

                def dt_chain_v(tau, par, c):
                    cp = c % 2

                    @block.vector
                    def _(vector, tau=tau, c=c, cp=cp):
                        W(vector, "vector", ev[("sp", tau, c)])
                        tok = c_v.inc(vector.tensor_add(
                            dt_f[:, cp, :], dt_f[:, cp, :], t_re[:, :]))
                        ev[("dt", tau, c)] = tok
                        rel["t_sp"] = tok

                    @block.vector
                    def _(vector, tau=tau, par=par, c=c, cp=cp):
                        W(vector, "vector", ev[("xc", tau, c)])
                        ev[("dtx", tau, c)] = c_v.inc(vector.tensor_mul(
                            dtx_bf[:, cp, :], dt_f[:, cp, :],
                            xc_bf[:, par, c, :]))

                dt_chain_st(tau, par, 0)
                dt_chain_v(tau, par, 0)
                dt_chain_st(tau, par, 1)
                dt_chain_v(tau, par, 1)
                for c in range(NCH):
                    cp = c % 2

                    def da_block(gs, ge, tau=tau, c=c, cp=cp):
                        @block.scalar
                        def _(scalar, gs=gs, ge=ge, tau=tau, c=c, cp=cp):
                            for g in range(gs, ge):
                                p2 = (tau * NCH * NG + c * NG + g) % 3
                                W(scalar, "scalar",
                                  ev[("dt", tau, c)] if g == 0 else None,
                                  rel.get(("da", p2)))
                                for j in range(GS):
                                    s = g * GS + j
                                    inst = scalar.activation(
                                        out=da_f[:, p2,
                                                 j * T1 + 2:j * T1 + 2 + TT],
                                        in_=dt_f[:, cp, :], func=AF.Exp,
                                        scale=a_sb[:, c, s:s + 1])
                                ev[("da", tau, c, g)] = c_s.inc(inst)

                    da_block(0, NG - 1)
                    for g in range(NG):
                        p2 = (tau * NCH * NG + c * NG + g) % 3

                        if g == 1:
                            da_block(NG - 1, NG)
                            if c + 2 < NCH:
                                dt_chain_st(tau, par, c + 2)
                            if tau + 1 < NT:
                                if c == 2:
                                    front_xzt(tau + 1, 13, 0, 10)
                                    front_copy(tau + 1)
                                elif c == 3:
                                    front_xzt(tau + 1, 13, 10, 16)
                                    front_silu(tau + 1, 13)
                                elif c == 4:
                                    front_conv(tau + 1)
                                elif c == 5:
                                    front_tail(tau + 1)
                        if g == 2 and c + 2 < NCH:
                            dt_chain_v(tau, par, c + 2)

                        @block.vector
                        def _(vector, tau=tau, c=c, g=g, p2=p2, cp=cp):
                            W(vector, "vector", ev[("da", tau, c, g)],
                              ev[("dtx", tau, c)], ev[("bcg", tau, g)],
                              rel.get(("yc", (c * NG + g) % 2)))
                            for j in range(GS):
                                s = g * GS + j
                                vector.tensor_mul(
                                    dbx_bf[:, j * T1 + 2:j * T1 + 2 + TT],
                                    dtx_bf[:, cp, :], brep_bf[:, s, :])
                                vector.tensor_copy(
                                    dbx_bf[:, j * T1:j * T1 + 1],
                                    carr[:, c, s:s + 1])
                            vector.tensor_tensor_scan(
                                out=h_bf[:], data0=da_f[:, p2, :],
                                data1=dbx_bf[:], initial=0.0,
                                op0=ALU.mult, op1=ALU.add)
                            vector.tensor_copy(
                                carr[:, c, g * GS:(g + 1) * GS].unsqueeze(2),
                                h_bf[:].rearrange("p (s t) -> p s t", s=GS)
                                [:, :, T1 - 1:T1])
                            inst = vector.tensor_mul(
                                yc_bf[:, (c * NG + g) % 2, :, :],
                                h_bf[:].rearrange("p (s t) -> p s t", s=GS)
                                [:, :, 2:2 + TT],
                                crep_bf[:, g * GS:(g + 1) * GS, :])
                            tok = c_v.inc(inst)
                            ev[("scan", tau, c, g)] = tok
                            rel[("da", p2)] = tok
                            if c == NCH - 1 and g == NG - 1:
                                rel["brepcrep"] = tok

                        @block.tensor
                        def _(tensor, tau=tau, c=c, g=g):
                            W(tensor, "tensor", ev[("scan", tau, c, g)],
                              rel.get(("ps_y", c % 2)) if g == 0 else None)
                            for j in range(GS):
                                inst = nc.tensor.matmul(
                                    ps_y[:, c % 2, :], ident_bf[:, :],
                                    yc_bf[:, (c * NG + g) % 2, j, :],
                                    start=(g == 0 and j == 0),
                                    stop=(g == NG - 1 and j == GS - 1))
                            tok = c_t.inc(inst)
                            ev[("yacc", tau, c, g)] = tok
                            rel[("yc", (c * NG + g) % 2)] = tok

                    @block.vector
                    def _(vector, tau=tau, par=par, c=c):
                        W(vector, "vector", ev[("yacc", tau, c, NG - 1)],
                          rel.get(("yfin", c % 2)))
                        vector.scalar_tensor_tensor(
                            out=epi_f[:], in0=xc_bf[:, par, c, :],
                            scalar=dp_sb[:, c:c + 1], in1=ps_y[:, c % 2, :],
                            op0=ALU.mult, op1=ALU.add)
                        tok = c_v.inc(vector.tensor_mul(
                            yfin_bf[:, c % 2, :], epi_f[:],
                            zs_bf[:, par, c, :]))
                        ev[("yfin", tau, c)] = tok
                        rel[("ps_y", c % 2)] = tok
                        if c == NCH - 1:
                            rel[("xc", par)] = tok
                            rel[("zs", par)] = tok

                    @block.gpsimd
                    def _(gpsimd, tau=tau, c=c, t0=t0, t1=t1):
                        W(gpsimd, "gpsimd", ev[("yfin", tau, c)])
                        tok = bdma(gpsimd, g_dma,
                                   lambda k, c=c, t0=t0, t1=t1:
                                   (y_shr[k, c, :, t0:t1],
                                    yfin_bf[:, c % 2, :]))
                        ev[("yw", tau, c)] = tok
                        rel[("yfin", c % 2)] = tok

                if tau == NT - 1:
                    @block.gpsimd
                    def _(gpsimd):
                        gpsimd.wait_ge(g_dma.h, g_dma.n)
                        bdma(gpsimd, g_dma,
                             lambda k: (flg_y[k, :],
                                        nonce_in[0:1, 32:34]))

                if BARRIER_TILES:
                    barrier_all(("mt", tau))

            m_end = {"scalar": c_s.now(), "vector": c_v.now(),
                     "tensor": c_t.now(), "gpsimd": (g_dma, g_dma.n),
                     "sync": s_dma.now()}

        def barrier(ends, tag):
            toks = {}
            for name in ("scalar", "vector", "tensor", "gpsimd", "sync"):
                @getattr(block, name)
                def _(eng, name=name):
                    for oname, tok in ends.items():
                        if oname != name:
                            eng.wait_ge(tok[0].h, tok[1])
                    if name == "scalar":
                        toks["t"] = c_s.inc(eng.nop())
            return toks["t"]

        bar_m = barrier(m_end, "m")

        with ExitStack() as st3:
            A3, P3 = mk(st3)
            wout_bf = A3("wout_bf", [128, 16, DM], BF16)
            wst3 = A3("wst3", [128, 2, DM], F32)
            y_in = A3("y_in", [128, 2, 16, TT], BF16)
            x3_st = A3("x3_st", [128, 2, NDM, TT], F32)
            v_f = A3("v_f", [128, NDM, TT], F32)
            v_bf = A3("v_bf", [128, NDM, TT], BF16)
            vsq_bf = A3("vsq_bf", [128, NDM, TT], BF16)
            var_f = A3("var_f", [128, TT], F32)
            r1_f = A3("r1_f", [128, TT], F32)
            r2_f = A3("r2_f", [128, TT], F32)
            r12_f = A3("r12_f", [128, TT], F32)
            tvmu = A3("tvmu", [128, NDM, TT], F32)
            x2_f = A3("x2_f", [128, NDM, TT], F32)
            x2n_bf = A3("x2n_bf", [128, NDM, TT], BF16)

            ps_o = P3("ps_o", [128, 2, TT])
            ps_st = P3("ps_st", [128, 2, TT])

            for part, src in (("lo", wout_lo), ("hi", wout_hi)):
                for k in range(NCH):
                    kk = (0 if part == "lo" else NCH) + k

                    @block.sync
                    def _(sync, src=src, k=k, kk=kk):
                        W(sync, "sync", bar_m, rel.get(("wst3", kk % 2)))
                        ev[("w3", kk)] = s_dma.inc(sync.dma_start(
                            out=wst3[:, kk % 2, :],
                            in_=src[k * 128:(k + 1) * 128, :]))

                    @block.scalar
                    def _(scalar, kk=kk):
                        W(scalar, "scalar", ev[("w3", kk)], bar_m)
                        rel[("wst3", kk % 2)] = c_s.inc(scalar.activation(
                            out=wout_bf[:, kk, :], in_=wst3[:, kk % 2, :],
                            func=AF.Copy))
            ev["w3d"] = c_s.now()

            @block.gpsimd
            def _(gpsimd):
                for oname, tok in m_end.items():
                    if oname != "gpsimd":
                        gpsimd.wait_ge(tok[0].h, tok[1])
                poll(gpsimd, lambda k: flg_y[k, :], lambda k: 32)
                ev["ypeer"] = c_g.inc(gpsimd.nop())

            for t3 in range(NT3):
                par = t3 % 2

                @block.gpsimd
                def _(gpsimd, t3=t3, par=par):
                    W(gpsimd, "gpsimd", ev["ypeer"], m_end["gpsimd"],
                      rel.get(("y_in", par)), rel.get(("x3", par)))
                    for hh in (0, 1):
                        with if_slot(gpsimd, hh):
                            a0 = hh * TOK + t3 * TT
                            gpsimd.dma_start(
                                out=y_in[:, par, :, :],
                                in_=y_shr[:, :, :, a0:a0 + TT].rearrange(
                                    "s c p t -> p (s c) t")
                            ).then_inc(g_dma.h, 16)
                            gpsimd.dma_start(
                                out=x3_st[:, par, :, :],
                                in_=xT[:, a0:a0 + TT].rearrange(
                                    "(c p) t -> p c t", p=128)
                            ).then_inc(g_dma.h, 16)
                        gpsimd.end_ifs()
                    g_dma.n += 32
                    ev[("y3", t3)] = g_dma.now()

                for i in range(NDM):
                    @block.tensor
                    def _(tensor, t3=t3, par=par, i=i):
                        W(tensor, "tensor", ev[("y3", t3)], ev["w3d"],
                          rel.get(("ps_o", i % 2)))
                        for k in range(16):
                            inst = nc.tensor.matmul(
                                ps_o[:, i % 2, :],
                                wout_bf[:, k, i * 128:(i + 1) * 128],
                                y_in[:, par, k, :],
                                start=(k == 0), stop=(k == 15))
                        ev[("om", t3, i)] = c_t.inc(inst)

                    @block.vector
                    def _(vector, t3=t3, par=par, i=i):
                        W(vector, "vector", ev[("om", t3, i)], ev[("y3", t3)],
                          ev.get(("x2n", t3 - 1)),
                          ev.get(("m3w", t3 - 1)))
                        tok = c_v.inc(vector.tensor_add(
                            v_f[:, i, :], x3_st[:, par, i, :],
                            ps_o[:, i % 2, :]))
                        ev[("v", t3, i)] = tok
                        rel[("ps_o", i % 2)] = tok
                        if i == NDM - 1:
                            rel[("x3", par)] = tok
                            rel[("y_in", par)] = tok

                @block.scalar
                def _(scalar, t3=t3):
                    W(scalar, "scalar", [ev[("v", t3, i)] for i in range(NDM)],
                      ev.get(("stat", t3 - 1)))
                    scalar.activation(out=v_bf[:], in_=v_f[:], func=AF.Copy)
                    ev[("vbf", t3)] = c_s.inc(scalar.activation(
                        out=vsq_bf[:], in_=v_f[:], func=AF.Square))

                @block.tensor
                def _(tensor, t3=t3):
                    W(tensor, "tensor", ev[("vbf", t3)], rel.get("ps_st"))
                    for k in range(NDM):
                        nc.tensor.matmul(ps_st[:, 0, :], onesh_bf[:, :],
                                         v_bf[:, k, :], start=(k == 0),
                                         stop=(k == NDM - 1))
                    for k in range(NDM):
                        inst = nc.tensor.matmul(ps_st[:, 1, :], onesh_bf[:, :],
                                                vsq_bf[:, k, :],
                                                start=(k == 0),
                                                stop=(k == NDM - 1))
                    ev[("stat", t3)] = c_t.inc(inst)

                @block.vector
                def _(vector, t3=t3):
                    W(vector, "vector", ev[("stat", t3)],
                      ev.get(("r1", t3 - 1)))
                    vector.tensor_copy(var_f[:], ps_st[:, 0, :])
                    vector.tensor_mul(var_f[:], var_f[:], ps_st[:, 0, :])
                    ev[("var", t3)] = c_v.inc(vector.tensor_sub(
                        var_f[:], ps_st[:, 1, :], var_f[:]))

                @block.scalar
                def _(scalar, t3=t3):
                    W(scalar, "scalar", ev[("var", t3)])
                    scalar.activation(out=r1_f[:], in_=var_f[:], func=AF.Ln,
                                      bias=EPS)
                    ev[("r1", t3)] = c_s.inc(scalar.activation(
                        out=r1_f[:], in_=r1_f[:], func=AF.Exp, scale=-0.5))

                @block.vector
                def _(vector, t3=t3):
                    W(vector, "vector", ev[("r1", t3)],
                      ev.get(("r2", t3 - 1)))
                    vector.tensor_mul(r2_f[:], r1_f[:], r1_f[:])
                    ev[("var2", t3)] = c_v.inc(vector.tensor_mul(
                        r2_f[:], r2_f[:], var_f[:]))

                @block.scalar
                def _(scalar, t3=t3):
                    W(scalar, "scalar", ev[("var2", t3)])
                    scalar.activation(out=r2_f[:], in_=r2_f[:], func=AF.Ln,
                                      bias=EPS)
                    ev[("r2", t3)] = c_s.inc(scalar.activation(
                        out=r2_f[:], in_=r2_f[:], func=AF.Exp, scale=-0.5))

                @block.vector
                def _(vector, t3=t3):
                    W(vector, "vector", ev[("r2", t3)],
                      ev.get(("x2n", t3 - 1)), ev.get(("m3w", t3 - 1)))
                    vector.tensor_mul(r12_f[:], r1_f[:], r2_f[:])
                    for i in range(NDM):
                        vector.scalar_tensor_tensor(
                            out=tvmu[:, i, :], in0=v_f[:, i, :], scalar=1.0,
                            in1=ps_st[:, 0, :], op0=ALU.mult, op1=ALU.subtract)
                        vector.tensor_mul(x2_f[:, i, :], tvmu[:, i, :], r1_f[:])
                        inst = vector.tensor_mul(tvmu[:, i, :], tvmu[:, i, :],
                                                 r12_f[:])
                    tok = c_v.inc(inst)
                    ev[("x2", t3)] = tok
                    rel["v_f"] = tok
                    rel["ps_st"] = tok

                @block.scalar
                def _(scalar, t3=t3):
                    W(scalar, "scalar", ev[("x2", t3)])
                    for i in range(NDM):
                        inst = scalar.activation(out=x2n_bf[:, i, :],
                                                 in_=tvmu[:, i, :],
                                                 func=AF.Identity,
                                                 scale=gam_sb[:, i:i + 1],
                                                 bias=bet_sb[:, i:i + 1])
                    ev[("x2n", t3)] = c_s.inc(inst)

                @block.sync
                def _(sync, t3=t3):
                    W(sync, "sync", ev[("x2n", t3)])
                    s_dma.inc(sync.dma_start(
                        out=x2_d[:, :, t3 * TT:(t3 + 1) * TT].transpose(
                            [1, 0, 2]),
                        in_=x2_f[:]))
                    ev[("m3w", t3)] = s_dma.inc(sync.dma_start(
                        out=x2n_d[:, :, t3 * TT:(t3 + 1) * TT].transpose(
                            [1, 0, 2]),
                        in_=x2n_bf[:]))

                if BARRIER_TILES:
                    barrier_all(("m3t", t3))

            m3_end = {"scalar": c_s.now(), "vector": c_v.now(),
                      "tensor": c_t.now(), "gpsimd": (g_dma, g_dma.n),
                      "sync": s_dma.now()}

        bar_3 = barrier(m3_end, "m3")

        with ExitStack() as stf:
            Af, Pf = mk(stf)
            w1_bf = Af("w1_bf", [128, NDM, FF], BF16)
            w2_bf = Af("w2_bf", [128, NFF, DM], BF16)

            with ExitStack() as stq:
                Aq, _ = mk(stq)
                qst = Aq("qst", [128, 2048], I32)
                qf = Aq("qf", [128, 2048], F32)
                scb = Aq("scb", [128, FF], F32)
                sel_f = Aq("sel_f", [32, 32 * 128], F32)
                sc_sb = Aq("sc_sb", [32, FF], F32)
                ps_sc = stq.enter_context(
                    nc.psum_tensor("ps_sc", [128, TT], F32))

                @block.sync
                def _(sync):
                    W(sync, "sync", bar_3)
                    s_dma.inc(sync.dma_start(out=sel_f[:], in_=selmat[:, :]))
                    ev["scl"] = s_dma.inc(sync.dma_start(
                        out=sc_sb[0:8, 0:FF], in_=s1[:, :]))

                for wi, (qsrc, nk, fdim, dst, nrows) in enumerate(
                        [(q1, NDM, FF, w1_bf, 8), (q2, NFF, DM, w2_bf, 32)]):
                    if wi == 1:
                        @block.sync
                        def _(sync):
                            W(sync, "sync", ev["w1done"])
                            ev["scl"] = s_dma.inc(sync.dma_start(
                                out=sc_sb[0:32, 0:DM], in_=s2[:, :]))
                    for g in range(nk):
                        for jj in range(fdim // TT):
                            @block.tensor
                            def _(tensor, g=g, jj=jj, nrows=nrows):
                                W(tensor, "tensor", ev["scl"], bar_3,
                                  rel.get("ps_sc"))
                                ev["scm"] = c_t.inc(nc.tensor.matmul(
                                    ps_sc[:, :],
                                    sel_f[0:nrows, g * 128:(g + 1) * 128],
                                    sc_sb[0:nrows, jj * TT:(jj + 1) * TT],
                                    start=True, stop=True))

                            @block.scalar
                            def _(scalar, jj=jj):
                                W(scalar, "scalar", ev["scm"], rel.get("scb"))
                                rel["ps_sc"] = c_s.inc(scalar.activation(
                                    out=scb[:, jj * TT:(jj + 1) * TT],
                                    in_=ps_sc[:, :], func=AF.Copy))
                        for fh in range((fdim + 2047) // 2048):
                            w0 = fh * 2048
                            w1w = min(fdim - w0, 2048)

                            @block.sync
                            def _(sync, qsrc=qsrc, g=g, w0=w0, w1w=w1w):
                                W(sync, "sync", bar_3, rel.get("qst"))
                                ev["ql"] = s_dma.inc(sync.dma_start(
                                    out=qst[:, 0:w1w],
                                    in_=qsrc[g * 128:(g + 1) * 128,
                                             w0:w0 + w1w]))

                            @block.scalar
                            def _(scalar, w1w=w1w):
                                W(scalar, "scalar", ev["ql"], rel.get("qf"))
                                ev["qd"] = c_s.inc(scalar.activation(
                                    out=qf[:, 0:w1w], in_=qst[:, 0:w1w],
                                    func=AF.Copy, bias=-128.0))

                            @block.vector
                            def _(vector, g=g, w0=w0, w1w=w1w, dst=dst):
                                W(vector, "vector", ev["qd"], rel.get("ps_sc"))
                                tok = c_v.inc(vector.tensor_mul(
                                    dst[:, g, w0:w0 + w1w], qf[:, 0:w1w],
                                    scb[:, w0:w0 + w1w]))
                                rel["qst"] = tok
                                rel["qf"] = tok
                                if w0 + w1w == fdim:
                                    rel["scb"] = tok
                    if wi == 0:
                        ev["w1done"] = c_v.now()
                ev["fw"] = c_v.now()

            act_bf = Af("act_bf", [128, 16, TT], BF16)
            z2a = Af("z2a", [128, NDM, TT], F32)
            x2r = Af("x2r", [128, NDM, TT], F32)
            x2nr = Af("x2nr", [128, NDM, TT], BF16)
            f_ta = Af("f_ta", [128, TT], F32)
            f_tb = Af("f_tb", [128, TT], F32)
            f_tc = Af("f_tc", [128, TT], F32)
            fvar = Af("fvar", [128, TT], F32)
            fr = Af("fr", [128, TT], F32)

            ps_f1 = Pf("ps_f1", [128, 4, TT])
            ps_f2 = Pf("ps_f2", [128, 2, TT])
            ps_fs = Pf("ps_fs", [128, 2, TT])

            for t3 in range(NT3):
                @block.sync
                def _(sync, t3=t3):
                    W(sync, "sync", ev[("m3w", t3)], rel.get("x2nr"))
                    s_dma.inc(sync.dma_start(
                        out=x2nr[:, :, :],
                        in_=x2n_d[:, :, t3 * TT:(t3 + 1) * TT].rearrange(
                            "c p t -> p c t")))
                    ev[("fx", t3)] = s_dma.inc(sync.dma_start(
                        out=x2r[:, :, :],
                        in_=x2_d[:, :, t3 * TT:(t3 + 1) * TT].rearrange(
                            "c p t -> p c t")))

                for halfF in range(2):
                    for fi in range(16):
                        f = halfF * 16 + fi

                        @block.tensor
                        def _(tensor, t3=t3, f=f, fi=fi):
                            W(tensor, "tensor", ev[("fx", t3)], ev["fw"],
                              rel.get(("ps_f1", fi % 4)))
                            for k in range(NDM):
                                inst = nc.tensor.matmul(
                                    ps_f1[:, fi % 4, :],
                                    w1_bf[:, k, f * 128:(f + 1) * 128],
                                    x2nr[:, k, :],
                                    start=(k == 0), stop=(k == NDM - 1))
                            ev[("f1", t3, f)] = c_t.inc(inst)

                        @block.scalar
                        def _(scalar, t3=t3, f=f, fi=fi):
                            W(scalar, "scalar", ev[("f1", t3, f)],
                              rel.get("f_t"))
                            inst = scalar.activation(out=f_tb[:],
                                                     in_=ps_f1[:, fi % 4, :],
                                                     func=AF.Sigmoid,
                                                     bias=b1_sb[:, f:f + 1])
                            ev[("fs", t3, f)] = c_s.inc(inst)

                        @block.vector
                        def _(vector, t3=t3, f=f, fi=fi):
                            W(vector, "vector", ev[("fs", t3, f)],
                              rel.get(("act", fi)))
                            vector.tensor_scalar_add(
                                f_tc[:], ps_f1[:, fi % 4, :],
                                b1_sb[:, f:f + 1])
                            tok = c_v.inc(vector.tensor_mul(
                                act_bf[:, fi, :], f_tc[:], f_tb[:]))
                            ev[("act", t3, f)] = tok
                            rel["f_t"] = tok
                            rel[("ps_f1", fi % 4)] = tok

                    for i in range(NDM):
                        @block.tensor
                        def _(tensor, t3=t3, halfF=halfF, i=i):
                            W(tensor, "tensor",
                              [ev[("act", t3, halfF * 16 + fk)]
                               for fk in range(16)],
                              rel.get(("ps_f2", i % 2)))
                            for fk in range(16):
                                inst = nc.tensor.matmul(
                                    ps_f2[:, i % 2, :],
                                    w2_bf[:, halfF * 16 + fk,
                                          i * 128:(i + 1) * 128],
                                    act_bf[:, fk, :],
                                    start=(fk == 0), stop=(fk == 15))
                            tok = c_t.inc(inst)
                            ev[("f2", t3, halfF, i)] = tok
                            if i == NDM - 1:
                                for fi2 in range(16):
                                    rel[("act", fi2)] = tok

                        if halfF == 0:
                            @block.scalar
                            def _(scalar, t3=t3, i=i):
                                W(scalar, "scalar", ev[("f2", t3, 0, i)],
                                  rel.get(("z2a", i)))
                                tok = c_s.inc(scalar.activation(
                                    out=z2a[:, i, :], in_=ps_f2[:, i % 2, :],
                                    func=AF.Identity, scale=0.5,
                                    bias=b2h_sb[:, i:i + 1]))
                                ev[("z2a", t3, i)] = tok
                                rel[("ps_f2", i % 2)] = tok
                        else:
                            @block.vector
                            def _(vector, t3=t3, i=i):
                                W(vector, "vector", ev[("f2", t3, 1, i)],
                                  ev[("z2a", t3, i)])
                                vector.scalar_tensor_tensor(
                                    out=z2a[:, i, :], in0=ps_f2[:, i % 2, :],
                                    scalar=0.5, in1=z2a[:, i, :],
                                    op0=ALU.mult, op1=ALU.add)
                                tok = c_v.inc(vector.scalar_tensor_tensor(
                                    out=z2a[:, i, :], in0=x2r[:, i, :],
                                    scalar=2.0, in1=z2a[:, i, :],
                                    op0=ALU.mult, op1=ALU.add))
                                ev[("u", t3, i)] = tok
                                rel[("ps_f2", i % 2)] = tok

                @block.scalar
                def _(scalar, t3=t3):
                    W(scalar, "scalar", [ev[("u", t3, i)] for i in range(NDM)],
                      ev.get(("fst", t3 - 1)))
                    scalar.activation(out=act_bf[:, 0:NDM, :],
                                      in_=z2a[:], func=AF.Copy)
                    ev[("ub", t3)] = c_s.inc(scalar.activation(
                        out=act_bf[:, NDM:2 * NDM, :],
                        in_=z2a[:], func=AF.Square))

                @block.tensor
                def _(tensor, t3=t3):
                    W(tensor, "tensor", ev[("ub", t3)], rel.get("ps_fs"))
                    for k in range(NDM):
                        nc.tensor.matmul(ps_fs[:, 0, :], onesh_bf[:, :],
                                         act_bf[:, k, :], start=(k == 0),
                                         stop=(k == NDM - 1))
                    for k in range(NDM):
                        inst = nc.tensor.matmul(ps_fs[:, 1, :], onesh_bf[:, :],
                                                act_bf[:, NDM + k, :],
                                                start=(k == 0),
                                                stop=(k == NDM - 1))
                    ev[("fst", t3)] = c_t.inc(inst)

                @block.vector
                def _(vector, t3=t3):
                    W(vector, "vector", ev[("fst", t3)],
                      ev.get(("fr", t3 - 1)))
                    vector.tensor_copy(fvar[:], ps_fs[:, 0, :])
                    vector.tensor_mul(fvar[:], fvar[:], ps_fs[:, 0, :])
                    ev[("fvar", t3)] = c_v.inc(vector.tensor_sub(
                        fvar[:], ps_fs[:, 1, :], fvar[:]))

                @block.scalar
                def _(scalar, t3=t3):
                    W(scalar, "scalar", ev[("fvar", t3)])
                    scalar.activation(out=fr[:], in_=fvar[:], func=AF.Ln,
                                      bias=EPS)
                    ev[("fr", t3)] = c_s.inc(scalar.activation(
                        out=fr[:], in_=fr[:], func=AF.Exp, scale=-0.5))

                @block.vector
                def _(vector, t3=t3):
                    W(vector, "vector", ev[("fr", t3)],
                      ev.get(("outw", t3 - 1)))
                    for i in range(NDM):
                        vector.scalar_tensor_tensor(
                            out=x2r[:, i, :], in0=z2a[:, i, :], scalar=1.0,
                            in1=ps_fs[:, 0, :], op0=ALU.mult, op1=ALU.subtract)
                        inst = vector.tensor_mul(x2r[:, i, :], x2r[:, i, :],
                                                 fr[:])
                    tok = c_v.inc(inst)
                    ev[("fout", t3)] = tok
                    rel["ps_fs"] = tok
                    for i in range(NDM):
                        rel[("z2a", i)] = tok

                @block.sync
                def _(sync, t3=t3):
                    W(sync, "sync", ev[("fout", t3)])
                    tok = s_dma.inc(sync.dma_start(
                        out=out_d[:, :, t3 * TT:(t3 + 1) * TT].transpose(
                            [1, 0, 2]),
                        in_=x2r[:]))
                    ev[("outw", t3)] = tok
                    rel["x2nr"] = tok

                if BARRIER_TILES:
                    barrier_all(("ft", t3))

            @block.sync
            def _(sync):
                sync.wait_ge(s_dma.h, s_dma.n)

            @block.gpsimd
            def _(gpsimd):
                gpsimd.wait_ge(s_dma.h, s_dma.n)

    return nc


_CACHE = {}


def kernel(**inputs):
    if "nc" not in _CACHE:
        _CACHE["nc"] = build_nc()
    nc = _CACHE["nc"]

    gi = lambda k: np.ascontiguousarray(inputs[k])
    x = gi("L0_attn_x")
    W_in = gi("L0_attn_W_in")
    conv_w = gi("L0_attn_conv_w")
    conv_b = gi("L0_attn_conv_b")
    W_xproj = gi("L0_attn_W_xproj")
    W_dt = gi("L0_attn_W_dt")
    b_dt = gi("L0_attn_b_dt")
    A_log = gi("L0_attn_A_log")
    Dp = gi("L0_attn_Dp")
    W_out = gi("L0_attn_W_out")
    ln_g = gi("L0_ffn_ln_gamma")
    ln_b = gi("L0_ffn_ln_beta")
    fc1_q = gi("L0_ffn_fc1_q")
    fc1_s = gi("L0_ffn_fc1_scales")
    fc1_b = gi("L0_ffn_fc1_bias")
    fc2_q = gi("L0_ffn_fc2_q")
    fc2_s = gi("L0_ffn_fc2_scales")
    fc2_b = gi("L0_ffn_fc2_bias")

    selm = np.zeros((32, 32 * 128), dtype=np.float32)
    for s in range(32):
        selm[s, s * 128:(s + 1) * 128] = 1.0
    identm = np.eye(128, dtype=np.float32)
    onesh = np.full((128, 128), 1.0 / DM, dtype=np.float32)
    da01 = np.tile(np.array([0.0, 1.0], np.float32), (128, 1))

    rng_n = np.random.default_rng()
    nonces = [rng_n.integers(1, 2**32 - 1, size=(1, 64), dtype=np.uint32)
              for _ in range(4)]
    in_maps = []
    for c in range(8):
        b, h = c // 2, c % 2
        ch0 = h * CH
        im = {
            "xT": np.ascontiguousarray(x[b].T),
            "winx": np.ascontiguousarray(W_in[:, ch0:ch0 + CH]),
            "winz": np.ascontiguousarray(W_in[:, 2048 + ch0:2048 + ch0 + CH]),
            "wxp": np.ascontiguousarray(W_xproj[ch0:ch0 + CH, :]),
            "wdt": np.ascontiguousarray(W_dt[:, ch0:ch0 + CH]),
            "convb": np.ascontiguousarray(
                conv_b[ch0:ch0 + CH].reshape(NCH, 128).T),
            "bdt": np.ascontiguousarray(b_dt[ch0:ch0 + CH].reshape(NCH, 128).T),
            "alog": np.ascontiguousarray(
                A_log[ch0:ch0 + CH].reshape(NCH, 128, DS).transpose(1, 0, 2)),
            "dp": np.ascontiguousarray(Dp[ch0:ch0 + CH].reshape(NCH, 128).T),
            "wout_lo": np.ascontiguousarray(W_out[0:CH, :]),
            "wout_hi": np.ascontiguousarray(W_out[CH:2 * CH, :]),
            "gam": np.ascontiguousarray(ln_g.reshape(NDM, 128).T),
            "bet": np.ascontiguousarray(ln_b.reshape(NDM, 128).T),
            "q1": fc1_q,
            "s1": fc1_s,
            "b1": np.ascontiguousarray(fc1_b.reshape(NFF, 128).T),
            "q2": fc2_q,
            "s2": fc2_s,
            "b2": np.ascontiguousarray(fc2_b.reshape(NDM, 128).T),
            "selmat": selm,
            "identm": identm,
            "onesh": onesh,
            "da01": da01,
            "nonce_in": nonces[b],
        }
        cd = np.zeros((32, 128, 128), dtype=np.float32)
        for cc in range(NCH):
            for k in range(DCONV):
                np.fill_diagonal(cd[cc * 4 + k],
                                 conv_w[ch0 + cc * 128:ch0 + (cc + 1) * 128, k])
        im["convdiag"] = cd
        in_maps.append(im)

    res = run_bass_kernel_spmd(nc, in_maps, core_ids=list(range(8)))

    full = np.empty((B, L, DM), dtype=np.float32)
    for c in range(8):
        b, h = c // 2, c % 2
        o = res.results[c]["out"]
        full[b, h * TOK:(h + 1) * TOK, :] = \
            o.transpose(2, 0, 1).reshape(TOK, DM)
    return full
